# version 06
# baseline (speedup 1.0000x reference)
import numpy as np
import ml_dtypes
from contextlib import ExitStack

import concourse.bass as bass
import concourse.mybir as mybir
from concourse.bass_utils import run_bass_kernel_spmd

F32 = mybir.dt.float32
BF16 = mybir.dt.bfloat16
I32 = mybir.dt.int32
U16 = mybir.dt.uint16
AF = mybir.ActivationFunctionType
ALU = mybir.AluOpType

D = 2048
NT = D // 128
H = 16
DH = 128
KVH = 4
I_FFN = 5632
GU = 2 * I_FFN
T_PRE = 4096
T_SUF = 2048
QCW = 512
LN_EPS = 1e-5
RMS_EPS = 1e-6
EXP_SHIFT_B = 4.0

PHASES = ("deq", "p1", "p2", "p3", "p4")
DEBUG_TAPS = ()
DOWN_LITE = 0


def _pi_h():
    hmap = np.empty(D, np.int64)
    for jb in range(2):
        for k in range(8):
            jj = np.arange(128)
            hmap[(jb * 8 + k) * 128 + jj] = 8 * (jb * 128 + jj) + k
    return hmap


def _pi_i():
    imap = np.empty(I_FFN, np.int64)
    off = 0
    tiles = []
    for jb2 in range(6):
        rows = 128 if jb2 < 5 else 64
        for k in range(8):
            jj = np.arange(rows)
            imap[off:off + rows] = 8 * (jb2 * 128 + jj) + k
            tiles.append((jb2, k, off, rows))
            off += rows
    assert off == I_FFN
    return imap, tiles


HMAP = _pi_h()
IMAP, DN_TILES = _pi_i()


class SemC:
    def __init__(self, ctx, nc, name):
        self.h = ctx.enter_context(nc.semaphore(name))
        self.n = 0

    def inc(self, inst, k):
        inst.then_inc(self.h, k)
        self.n += k
        return self.n


class Pipe:

    def __init__(self, b, name, nslots=2):
        self.sems = [b.sem(f"{name}_{i}") for i in range(nslots)]
        self.nslots = nslots

    def inc(self, inst, slot, k=16):
        sc = self.sems[slot % self.nslots]
        inst.then_inc(sc.h, k)
        sc.n += k
        return (slot % self.nslots, sc.n)

    def wait(self, eng, mark):
        slot, val = mark
        eng.wait_ge(self.sems[slot].h, val)

    def gates(self):
        return [(sc, sc.n) for sc in self.sems]


class B:
    def __init__(self, nc, ctx, block):
        self.nc = nc
        self.ctx = ctx
        self.block = block
        self.sems = {}

    def sem(self, name):
        if name not in self.sems:
            self.sems[name] = SemC(self.ctx, self.nc, name)
        return self.sems[name]

    def sec(self, engine_name, fn):
        getattr(self.block, engine_name)(fn)


def _waits(eng, ready):
    if ready is None:
        return
    if isinstance(ready, tuple):
        ready = [ready]
    for (s, v) in ready:
        if v > 0:
            eng.wait_ge(s.h, v)



def emit_dequant(b, gupT, sc_gu, downT, sc_dn, wgu_d, wdn_d):
    nc = b.nc
    pld = Pipe(b, "deq_ld")
    s_dv = b.sem("deq_dv")
    pst = Pipe(b, "deq_st")

    OC = 704
    jobs = []
    for jb in range(2):
        for oc in range(GU // OC):
            cs = slice(oc * OC, (oc + 1) * OC)
            jobs.append((128, OC, gupT[jb * 128:(jb + 1) * 128, cs],
                         sc_gu[jb * 128:(jb + 1) * 128, cs],
                         [(k, wgu_d[(jb * 8 + k) * 128:(jb * 8 + k + 1) * 128, cs])
                          for k in range(8)]))
    DC = 512
    for jb2 in range(6):
        rows = 128 if jb2 < 5 else 64
        for dc in range(D // DC):
            ds_ = slice(dc * DC, (dc + 1) * DC)
            planes = [(k, wdn_d[off:off + trows, ds_])
                      for (tjb2, k, off, trows) in DN_TILES if tjb2 == jb2]
            jobs.append((rows, DC, downT[jb2 * 128:jb2 * 128 + rows, ds_],
                         sc_dn[jb2 * 128:jb2 * 128 + rows, ds_], planes))

    nbuf = 2
    ctx = b.ctx
    mx_c = OC
    sbr = lambda name, shape, dt: ctx.enter_context(nc.sbuf_tensor(name, shape, dt, side="right"))
    slab = [sbr(f"dq_slab{i}", [128, mx_c], I32) for i in range(nbuf)]
    scl = [sbr(f"dq_scl{i}", [128, mx_c], F32) for i in range(nbuf)]
    t1 = sbr("dq_t1", [128, mx_c], I32)
    bM = sbr("dq_bM", [128, mx_c], U16)
    t4 = sbr("dq_t4", [128, mx_c], BF16)
    t5 = sbr("dq_t5", [128, mx_c], BF16)
    s32 = t1
    s16 = sbr("dq_s16", [128, mx_c], U16)
    t6 = t4.bitcast(U16) if False else sbr("dq_t6", [128, mx_c], U16)
    wout = [sbr(f"dq_w{i}", [128, mx_c], BF16) for i in range(2)]

    dv_base = s_dv.n
    dv_plan = [dv_base]
    acc = dv_base
    for (_, _, _, _, planes) in jobs:
        acc += len(planes)
        dv_plan.append(acc)
    n_planes = acc - dv_base

    ld_marks = []
    st_plan = []

    def loader(eng):
        for ji, (rows, cols, slab_ap, sc_ap, planes) in enumerate(jobs):
            if ji >= nbuf:
                eng.wait_ge(s_dv.h, dv_plan[ji - 1])
            pld.inc(eng.dma_start(out=slab[ji % nbuf][0:rows, 0:cols], in_=slab_ap), ji)
            inst = eng.dma_start(out=scl[ji % nbuf][0:rows, 0:cols], in_=sc_ap)
            ld_marks.append(pld.inc(inst, ji))

    def decoder(eng):
        pcount = 0
        for ji, (rows, cols, slab_ap, sc_ap, planes) in enumerate(jobs):
            pld.wait(eng, ld_marks[ji])
            sl = slab[ji % nbuf][0:rows, 0:cols]
            sc = scl[ji % nbuf][0:rows, 0:cols]
            for (k, dst) in planes:
                wo = wout[pcount % 2][0:rows, 0:cols]
                if pcount >= 2:
                    pst.wait(eng, (pcount % 2, 16 * ((pcount - 2) // 2 + 1)))
                _t1, _bM, _t4, _t5 = (x[0:rows, 0:cols] for x in (t1, bM, t4, t5))
                _s32, _s16, _t6 = (x[0:rows, 0:cols] for x in (s32, s16, t6))
                sh = 6 - 4 * k
                if sh >= 0:
                    eng.tensor_scalar(_t1, sl, sh, 0x1C0, ALU.logical_shift_left, ALU.bitwise_and)
                else:
                    eng.tensor_scalar(_t1, sl, -sh, 0x1C0, ALU.logical_shift_right, ALU.bitwise_and)
                eng.tensor_scalar(_bM, _t1, 0x3F00, None, ALU.add)
                Bv = _bM.bitcast(BF16)
                eng.tensor_scalar(_t4, Bv, 2.0, -1.0, ALU.mult, ALU.add)
                eng.tensor_tensor(_t5, _t4, Bv, ALU.min)
                ssh = 12 - 4 * k
                if ssh >= 0:
                    eng.tensor_scalar(_s32, sl, ssh, 0x8000, ALU.logical_shift_left, ALU.bitwise_and)
                else:
                    eng.tensor_scalar(_s32, sl, -ssh, 0x8000, ALU.logical_shift_right, ALU.bitwise_and)
                eng.tensor_copy(_s16, _s32)
                eng.tensor_tensor(_t6, _t5.bitcast(U16), _s16, ALU.bitwise_or)
                inst = eng.tensor_tensor(wo, _t6.bitcast(BF16), sc, ALU.mult)
                s_dv.inc(inst, 1)
                st_plan.append((wo, dst, rows, cols))
                pcount += 1

    def storer(eng):
        for si, (wo, dst, rows, cols) in enumerate(st_plan):
            eng.wait_ge(s_dv.h, dv_base + si + 1)
            pst.inc(eng.dma_start(out=dst, in_=wo[0:rows, 0:cols]), si)

    b.sec("sync", loader)
    b.sec("vector", decoder)
    b.sec("gpsimd", storer)
    return pst.gates()



def emit_ln(b, tag, src, T, xln_d, cst, ready):
    nc = b.nc
    NQ = T // QCW
    px = Pipe(b, "ln_x")
    po = Pipe(b, "ln_o")
    s_c = b.sem("ln_c")
    s_p = b.sem("ln_p")
    s_d = b.sem("ln_d")
    s_l = b.sem("ln_l")

    src_r = src.rearrange("(dt p) t -> p dt t", p=128)
    xln_r = xln_d.rearrange("(dt p) t -> p dt t", p=128)

    with ExitStack() as ctx:
        xc = [ctx.enter_context(nc.sbuf_tensor(f"{tag}_xc{i}", [128, NT * QCW], F32)) for i in range(2)]
        xb = [ctx.enter_context(nc.sbuf_tensor(f"{tag}_xb{i}", [128, NT * QCW], BF16)) for i in range(2)]
        sq = [ctx.enter_context(nc.sbuf_tensor(f"{tag}_sq{i}", [128, NT * QCW], BF16)) for i in range(2)]
        m_t = ctx.enter_context(nc.sbuf_tensor(f"{tag}_m", [1, 2 * QCW], F32))
        v_t = ctx.enter_context(nc.sbuf_tensor(f"{tag}_v", [1, 2 * QCW], F32))
        var_t = ctx.enter_context(nc.sbuf_tensor(f"{tag}_var", [1, 2 * QCW], F32))
        rstd_t = ctx.enter_context(nc.sbuf_tensor(f"{tag}_rs", [1, 2 * QCW], F32))
        mr_t = ctx.enter_context(nc.sbuf_tensor(f"{tag}_mr", [1, 2 * QCW], F32))
        tmp = [ctx.enter_context(nc.sbuf_tensor(f"{tag}_tm{i}", [128, QCW], F32)) for i in range(2)]
        ob = [ctx.enter_context(nc.sbuf_tensor(f"{tag}_ob{i}", [128, QCW], BF16)) for i in range(2)]
        ps_m = ctx.enter_context(nc.psum_tensor(f"{tag}_pm", [128, QCW], F32))
        ps_v = ctx.enter_context(nc.psum_tensor(f"{tag}_pv", [128, QCW], F32))
        ps_b1 = ctx.enter_context(nc.psum_tensor(f"{tag}_pb1", [128, QCW], F32))
        ps_b2 = ctx.enter_context(nc.psum_tensor(f"{tag}_pb2", [128, QCW], F32))

        c_base, p_base, d_base, l_base = s_c.n, s_p.n, s_d.n, s_l.n
        c_done = lambda qc: c_base + qc + 1
        _s1d, _s2d = {}, {}
        _pc = p_base
        _order = [("s1", 0)] + ([("s1", 1)] if NQ > 1 else [])
        for _q in range(NQ):
            _order.append(("s2", _q))
            if _q + 2 < NQ:
                _order.append(("s1", _q + 2))
        for (_k, _q) in _order:
            _pc += 1
            (_s1d if _k == "s1" else _s2d)[_q] = _pc
        s1_done = lambda qc: _s1d[qc]
        s2_done = lambda qc: _s2d[qc]
        v1_done = lambda qc: d_base + (2 + NT) * qc + 1
        v2_done = lambda qc: d_base + (2 + NT) * qc + 2
        v3_done = lambda qc, dt: d_base + (2 + NT) * qc + 2 + dt + 1
        a_done = lambda qc: l_base + qc + 1
        ob_base = [sc.n for sc in po.sems]
        o_mark = lambda gi: (gi % 2, ob_base[gi % 2] + 16 * (gi // 2 + 1))
        x_marks = []

        def sync_io(eng):
            def load(qc):
                inst = eng.dma_start(out=xc[qc % 2][:].rearrange("p (dt c) -> p dt c", dt=NT),
                                     in_=src_r[:, :, qc * QCW:(qc + 1) * QCW])
                x_marks.append(px.inc(inst, qc))
            _waits(eng, ready)
            load(0)
            if NQ > 1:
                load(1)
            for qc in range(NQ):
                for dt in range(NT):
                    gi = qc * NT + dt
                    eng.wait_ge(s_d.h, v3_done(qc, dt))
                    inst = eng.dma_start(out=xln_r[:, dt, qc * QCW:(qc + 1) * QCW],
                                         in_=ob[gi % 2][:])
                    po.inc(inst, gi)
                if qc + 2 < NQ:
                    load(qc + 2)

        def act_sec(eng):
            def casts(qc):
                px.wait(eng, x_marks[qc])
                if qc >= 2:
                    eng.wait_ge(s_p.h, s1_done(qc - 2))
                x = xc[qc % 2]
                for dt in range(NT):
                    sl = slice(dt * QCW, (dt + 1) * QCW)
                    eng.activation(sq[qc % 2][:, sl], x[:, sl], AF.Square)
                inst = None
                for dt in range(NT):
                    sl = slice(dt * QCW, (dt + 1) * QCW)
                    inst = eng.copy(xb[qc % 2][:, sl], x[:, sl])
                s_c.inc(inst, 1)
            casts(0)
            if NQ > 1:
                casts(1)
            for qc in range(NQ):
                eng.wait_ge(s_d.h, v1_done(qc))
                qs = slice((qc % 2) * QCW, (qc % 2 + 1) * QCW)
                eng.activation(rstd_t[:, qs], var_t[:, qs], AF.Ln, bias=cst["eps_ln"])
                inst = eng.activation(rstd_t[:, qs], rstd_t[:, qs], AF.Exp, scale=-0.5)
                s_l.inc(inst, 1)
                if qc + 2 < NQ:
                    casts(qc + 2)

        def pe_sec(eng):
            def s1(qc):
                eng.wait_ge(s_c.h, c_done(qc))
                if qc >= 1:
                    eng.wait_ge(s_d.h, v1_done(qc - 1))
                for dt in range(NT):
                    sl = slice(dt * QCW, (dt + 1) * QCW)
                    nc.tensor.matmul(ps_m[0:1, :], cst["ones_col_bf"], xb[qc % 2][:, sl],
                                     start=(dt == 0), stop=(dt == NT - 1))
                for dt in range(NT):
                    sl = slice(dt * QCW, (dt + 1) * QCW)
                    inst = nc.tensor.matmul(ps_v[0:1, :], cst["ones_col_bf"], sq[qc % 2][:, sl],
                                            start=(dt == 0), stop=(dt == NT - 1))
                s_p.inc(inst, 1)
            s1(0)
            if NQ > 1:
                s1(1)
            for qc in range(NQ):
                eng.wait_ge(s_d.h, v2_done(qc))
                if qc >= 1:
                    eng.wait_ge(s_d.h, v3_done(qc - 1, NT - 1))
                qs = slice((qc % 2) * QCW, (qc % 2 + 1) * QCW)
                nc.tensor.matmul(ps_b1[:], cst["ones_row_f32"], rstd_t[:, qs], start=True, stop=True)
                inst = nc.tensor.matmul(ps_b2[:], cst["ones_row_f32"], mr_t[:, qs], start=True, stop=True)
                s_p.inc(inst, 1)
                if qc + 2 < NQ:
                    s1(qc + 2)

        def dve_sec(eng):
            for qc in range(NQ):
                qs = slice((qc % 2) * QCW, (qc % 2 + 1) * QCW)
                eng.wait_ge(s_p.h, s1_done(qc))
                if qc >= 2:
                    eng.wait_ge(s_p.h, s2_done(qc - 2))
                eng.tensor_scalar(m_t[:, qs], ps_m[0:1, :], 1.0 / D, None, ALU.mult)
                eng.tensor_scalar(v_t[:, qs], ps_v[0:1, :], 1.0 / D, None, ALU.mult)
                eng.tensor_tensor(mr_t[:, qs], m_t[:, qs], m_t[:, qs], ALU.mult)
                inst = eng.tensor_tensor(var_t[:, qs], v_t[:, qs], mr_t[:, qs], ALU.subtract)
                s_d.inc(inst, 1)
                eng.wait_ge(s_l.h, a_done(qc))
                inst = eng.tensor_tensor(mr_t[:, qs], m_t[:, qs], rstd_t[:, qs], ALU.mult)
                s_d.inc(inst, 1)
                eng.wait_ge(s_p.h, s2_done(qc))
                x = xc[qc % 2]
                for dt in range(NT):
                    gi = qc * NT + dt
                    if gi >= 2:
                        po.wait(eng, o_mark(gi - 2))
                    sl = slice(dt * QCW, (dt + 1) * QCW)
                    tm = tmp[dt % 2]
                    eng.tensor_tensor(tm[:], x[:, sl], ps_b1[:], ALU.mult)
                    inst = eng.tensor_tensor(ob[gi % 2][:], tm[:], ps_b2[:], ALU.subtract)
                    s_d.inc(inst, 1)

        b.sec("sync", sync_io)
        b.sec("scalar", act_sec)
        b.sec("tensor", pe_sec)
        b.sec("vector", dve_sec)
    return po.gates()



def emit_fm_proj(b, tag, w_dram, outd, bias_2d, xln_d, T, out_dram, ready,
                 rope=None, cst=None):
    nc = b.nc
    NQ = T // QCW
    NO = outd // 128
    NTIL = NQ * NO
    s_w = b.sem("fp_w")
    px = Pipe(b, "fp_x")
    po = Pipe(b, "fp_o")
    s_mm = b.sem("fp_mm")
    s_sw = b.sem("fp_sw")
    s_e = b.sem("fp_e")
    s_r = b.sem("fp_r")

    xln_r = xln_d.rearrange("(dt p) t -> p dt t", p=128)

    with ExitStack() as ctx:
        w_sb = ctx.enter_context(nc.sbuf_tensor(f"{tag}_wsb", [128, NT * outd], BF16))
        wst = [ctx.enter_context(nc.sbuf_tensor(f"{tag}_wst{i}", [128, NT * 256], F32)) for i in range(2)]
        bias_sb = ctx.enter_context(nc.sbuf_tensor(f"{tag}_bs", [128, max(NO, 1)], F32))
        xcb = [ctx.enter_context(nc.sbuf_tensor(f"{tag}_xc{i}", [128, NT * QCW], BF16)) for i in range(2)]
        ev = [ctx.enter_context(nc.sbuf_tensor(f"{tag}_ev{i}", [128, QCW], BF16)) for i in range(2)]
        ps = [ctx.enter_context(nc.psum_tensor(f"{tag}_ps{i}", [128, QCW], F32)) for i in range(2)]
        if rope is not None:
            rt1 = ctx.enter_context(nc.sbuf_tensor(f"{tag}_rt1", [128, QCW], F32))
            rt2 = ctx.enter_context(nc.sbuf_tensor(f"{tag}_rt2", [128, QCW], F32))
            ro = [ctx.enter_context(nc.sbuf_tensor(f"{tag}_ro{i}", [128, QCW], BF16)) for i in range(2)]
            ps_sw = [ctx.enter_context(nc.psum_tensor(f"{tag}_psw{i}", [128, QCW], F32)) for i in range(2)]

        has_bias = bias_2d is not None
        NWC = outd // 256
        pwst = Pipe(b, "fp_wst")
        wst_marks = []
        w_base_cast = s_w.n
        w_mark = s_w.n + NWC + (16 if has_bias else 0)
        mm_plan = [s_mm.n + i + 1 for i in range(NTIL)]
        sw_plan = [s_sw.n + i + 1 for i in range(NTIL)]
        e_plan = [s_e.n + i + 1 for i in range(NTIL)]
        r_plan = [s_r.n + i + 1 for i in range(NTIL)]
        ob_base = [sc.n for sc in po.sems]
        o_mark = lambda i: (i % 2, ob_base[i % 2] + 16 * (i // 2 + 1))
        last_plan = r_plan if rope is not None else e_plan
        s_last = s_r if rope is not None else s_e
        x_marks = []

        def w_loader(eng):
            _waits(eng, ready)
            if has_bias:
                eng.dma_start(out=bias_sb[:, 0:NO], in_=bias_2d).then_inc(s_w.h, 16)
                s_w.n += 16
            for ch in range(NWC):
                if ch >= 2:
                    eng.wait_ge(s_w.h, w_base_cast + (16 if has_bias else 0) + ch - 1)
                inst = eng.dma_start(out=wst[ch % 2][:].rearrange("p (dt o) -> p dt o", o=256),
                                     in_=w_dram.rearrange("(dt p) o -> p dt o", p=128)[:, :, ch * 256:(ch + 1) * 256])
                wst_marks.append(pwst.inc(inst, ch))

        def w_cast(eng):
            for ch in range(NWC):
                pwst.wait(eng, wst_marks[ch])
                inst = eng.copy(w_sb[:].rearrange("p (dt o) -> p dt o", o=outd)[:, :, ch * 256:(ch + 1) * 256],
                                wst[ch % 2][:].rearrange("p (dt o) -> p dt o", o=256))
                s_w.inc(inst, 1)

        def sync_io(eng):
            def load(qc):
                inst = eng.dma_start(out=xcb[qc % 2][:].rearrange("p (dt c) -> p dt c", dt=NT),
                                     in_=xln_r[:, :, qc * QCW:(qc + 1) * QCW])
                x_marks.append(px.inc(inst, qc))
            _waits(eng, ready)
            load(0)
            if NQ > 1:
                load(1)
            idx = 0
            for qc in range(NQ):
                for ot in range(NO):
                    src = (ro[idx % 2] if rope is not None else ev[idx % 2])[:]
                    eng.wait_ge(s_last.h, last_plan[idx])
                    inst = eng.dma_start(out=out_dram[ot][:, qc * QCW:(qc + 1) * QCW], in_=src)
                    po.inc(inst, idx)
                    idx += 1
                if qc + 2 < NQ:
                    load(qc + 2)

        def mms(eng):
            eng.wait_ge(s_w.h, w_mark)
            idx = 0
            for qc in range(NQ):
                px.wait(eng, x_marks[qc])
                for ot in range(NO):
                    if idx >= 2:
                        eng.wait_ge(s_e.h, e_plan[idx - 2])
                    p = ps[idx % 2]
                    for dt in range(NT):
                        inst = nc.tensor.matmul(
                            p[:], w_sb[:, dt * outd + ot * 128: dt * outd + (ot + 1) * 128],
                            xcb[qc % 2][:, dt * QCW:(dt + 1) * QCW],
                            start=(dt == 0), stop=(dt == NT - 1))
                    s_mm.inc(inst, 1)
                    if rope is not None and idx >= 1:
                        j = idx - 1
                        eng.wait_ge(s_e.h, e_plan[j])
                        if j >= 2:
                            eng.wait_ge(s_r.h, r_plan[j - 2])
                        inst = nc.tensor.matmul(ps_sw[j % 2][:], rope["pswap"], ev[j % 2][:],
                                                start=True, stop=True)
                        s_sw.inc(inst, 1)
                    idx += 1
            if rope is not None:
                j = NTIL - 1
                eng.wait_ge(s_e.h, e_plan[j])
                if j >= 2:
                    eng.wait_ge(s_r.h, r_plan[j - 2])
                inst = nc.tensor.matmul(ps_sw[j % 2][:], rope["pswap"], ev[j % 2][:],
                                        start=True, stop=True)
                s_sw.inc(inst, 1)

        def evac(eng):
            idx = 0
            for qc in range(NQ):
                for ot in range(NO):
                    eng.wait_ge(s_mm.h, mm_plan[idx])
                    if idx >= 2:
                        eng.wait_ge(s_last.h, last_plan[idx - 2])
                        if rope is None:
                            po.wait(eng, o_mark(idx - 2))
                    bias = bias_sb[:, ot:ot + 1] if has_bias else 0.0
                    fn = AF.Identity if has_bias else AF.Copy
                    inst = eng.activation(ev[idx % 2][:], ps[idx % 2][:], fn, bias=bias)
                    s_e.inc(inst, 1)
                    idx += 1

        def rope_dve(eng):
            idx = 0
            for qc in range(NQ):
                qs = slice(qc * QCW, (qc + 1) * QCW)
                for ot in range(NO):
                    eng.wait_ge(s_sw.h, sw_plan[idx])
                    if idx >= 2:
                        po.wait(eng, o_mark(idx - 2))
                    eng.tensor_tensor(rt1[:], ps_sw[idx % 2][:], rope["sin"][:, qs], ALU.mult)
                    eng.tensor_tensor(rt2[:], ev[idx % 2][:], rope["cos"][:, qs], ALU.mult)
                    inst = eng.tensor_tensor(ro[idx % 2][:], rt1[:], rt2[:], ALU.add)
                    s_r.inc(inst, 1)
                    idx += 1

        b.sec("sync", w_loader)
        b.sec("scalar", w_cast)
        b.sec("sync", sync_io)
        b.sec("tensor", mms)
        b.sec("scalar", evac)
        if rope is not None:
            b.sec("vector", rope_dve)
    return po.gates()



def emit_v_proj(b, tag, wv_dram, outd, bv_row, xln_d, T, v_dram, ready, cst):
    nc = b.nc
    NQ = T // QCW
    NOC = outd // QCW
    NTIL = NQ * 4 * NOC
    s_w = b.sem("vp_w")
    px = Pipe(b, "vp_x")
    po = Pipe(b, "vp_o")
    s_mm = b.sem("vp_mm")
    s_e = b.sem("vp_e")
    s_bc = b.sem("vp_bc")

    xln_r = xln_d.rearrange("(dt p) t -> p dt t", p=128)

    with ExitStack() as ctx:
        w_sb = ctx.enter_context(nc.sbuf_tensor(f"{tag}_wsb", [128, NT * outd], BF16))
        wst = [ctx.enter_context(nc.sbuf_tensor(f"{tag}_wst{i}", [128, NT * 256], F32)) for i in range(2)]
        xcb = [ctx.enter_context(nc.sbuf_tensor(f"{tag}_xc{i}", [128, NT * QCW], BF16)) for i in range(2)]
        ev = [ctx.enter_context(nc.sbuf_tensor(f"{tag}_ev{i}", [128, QCW], BF16)) for i in range(2)]
        bvb = ctx.enter_context(nc.sbuf_tensor(f"{tag}_bvb", [128, max(outd, QCW)], F32))
        bvrow_sb = ctx.enter_context(nc.sbuf_tensor(f"{tag}_bvr", [1, max(outd, QCW)], F32))
        ps = [ctx.enter_context(nc.psum_tensor(f"{tag}_ps{i}", [128, QCW], F32)) for i in range(2)]
        ps_b = ctx.enter_context(nc.psum_tensor(f"{tag}_psb", [128, QCW], F32))

        has_bias = bv_row is not None
        NWC = outd // 256
        pwst = Pipe(b, "vp_wst")
        wst_marks = []
        w_base_cast = s_w.n
        w_mark = s_w.n + NWC + (16 if has_bias else 0)
        bc_base = s_bc.n
        mm_base = s_mm.n + (NOC if has_bias else 0)
        mm_plan = [mm_base + i + 1 for i in range(NTIL)]
        e_plan = [s_e.n + i + 1 for i in range(NTIL)]
        ob_base = [sc.n for sc in po.sems]
        o_mark = lambda i: (i % 2, ob_base[i % 2] + 16 * 4 * (i // 2 + 1))
        x_marks = []

        def w_loader(eng):
            _waits(eng, ready)
            if has_bias:
                eng.dma_start(out=bvrow_sb[:, 0:outd], in_=bv_row).then_inc(s_w.h, 16)
                s_w.n += 16
            for ch in range(NWC):
                if ch >= 2:
                    eng.wait_ge(s_w.h, w_base_cast + (16 if has_bias else 0) + ch - 1)
                inst = eng.dma_start(out=wst[ch % 2][:].rearrange("p (dt o) -> p dt o", o=256),
                                     in_=wv_dram.rearrange("(dt p) o -> p dt o", p=128)[:, :, ch * 256:(ch + 1) * 256])
                wst_marks.append(pwst.inc(inst, ch))

        def w_cast(eng):
            for ch in range(NWC):
                pwst.wait(eng, wst_marks[ch])
                inst = eng.copy(w_sb[:].rearrange("p (dt o) -> p dt o", o=outd)[:, :, ch * 256:(ch + 1) * 256],
                                wst[ch % 2][:].rearrange("p (dt o) -> p dt o", o=256))
                s_w.inc(inst, 1)

        def sync_io(eng):
            def load(qc):
                inst = eng.dma_start(out=xcb[qc % 2][:].rearrange("p (dt c) -> p dt c", dt=NT),
                                     in_=xln_r[:, :, qc * QCW:(qc + 1) * QCW])
                x_marks.append(px.inc(inst, qc))
            _waits(eng, ready)
            load(0)
            if NQ > 1:
                load(1)
            idx = 0
            for qc in range(NQ):
                for tt in range(4):
                    for oc in range(NOC):
                        eng.wait_ge(s_e.h, e_plan[idx])
                        for j in range(4):
                            hh = oc * 4 + j
                            inst = eng.dma_start(out=v_dram[hh][qc * QCW + tt * 128: qc * QCW + (tt + 1) * 128, :],
                                                 in_=ev[idx % 2][:, j * 128:(j + 1) * 128])
                            po.inc(inst, idx)
                        idx += 1
                if qc + 2 < NQ:
                    load(qc + 2)

        def mms(eng):
            eng.wait_ge(s_w.h, w_mark)
            if has_bias:
                for oc in range(NOC):
                    if oc >= 1:
                        eng.wait_ge(s_bc.h, bc_base + oc)
                    inst = nc.tensor.matmul(ps_b[:], cst["ones_row_f32"],
                                            bvrow_sb[:, oc * QCW:(oc + 1) * QCW], start=True, stop=True)
                    s_mm.inc(inst, 1)
                eng.wait_ge(s_bc.h, bc_base + NOC)
            idx = 0
            for qc in range(NQ):
                px.wait(eng, x_marks[qc])
                for tt in range(4):
                    for oc in range(NOC):
                        if idx >= 2:
                            eng.wait_ge(s_e.h, e_plan[idx - 2])
                        p = ps[idx % 2]
                        for dt in range(NT):
                            inst = nc.tensor.matmul(
                                p[:], xcb[qc % 2][:, dt * QCW + tt * 128: dt * QCW + (tt + 1) * 128],
                                w_sb[:, dt * outd + oc * QCW: dt * outd + (oc + 1) * QCW],
                                start=(dt == 0), stop=(dt == NT - 1))
                        s_mm.inc(inst, 1)
                        idx += 1

        def bc_evac(eng):
            if not has_bias:
                return
            for oc in range(NOC):
                eng.wait_ge(s_mm.h, mm_base - NOC + oc + 1)
                inst = eng.copy(bvb[:, oc * QCW:(oc + 1) * QCW], ps_b[:])
                s_bc.inc(inst, 1)

        def evac(eng):
            idx = 0
            for qc in range(NQ):
                for tt in range(4):
                    for oc in range(NOC):
                        eng.wait_ge(s_mm.h, mm_plan[idx])
                        if idx >= 2:
                            po.wait(eng, o_mark(idx - 2))
                        if has_bias:
                            inst = eng.scalar_tensor_tensor(ev[idx % 2][:], ps[idx % 2][:], 1.0,
                                                            bvb[:, oc * QCW:(oc + 1) * QCW],
                                                            ALU.mult, ALU.add)
                        else:
                            inst = eng.tensor_copy(ev[idx % 2][:], ps[idx % 2][:])
                        s_e.inc(inst, 1)
                        idx += 1

        b.sec("sync", w_loader)
        b.sec("scalar", w_cast)
        b.sec("scalar", bc_evac)
        b.sec("sync", sync_io)
        b.sec("tensor", mms)
        b.sec("vector", evac)
    return po.gates()



def emit_attention(b, tag, q_dram, k_dram, v_dram, kv_of_head, TQ, TK, scale,
                   shift, pf_dram, nheads, ready, cst):
    nc = b.nc
    NQ = TQ // QCW
    NKT = TK // 128
    pld = Pipe(b, "at_ld")
    po = Pipe(b, "at_o")
    s_s = b.sem("at_s")
    s_p = b.sem("at_p")
    s_e = b.sem("at_e")
    s_d = b.sem("at_d")
    s_bc = b.sem("at_bc")
    s_e2 = b.sem("at_e2")

    with ExitStack() as ctx:
        NSL = 4
        k_sb = [ctx.enter_context(nc.sbuf_tensor(f"{tag}_k{i}", [128, TK], BF16)) for i in range(2)]
        v_sb = [ctx.enter_context(nc.sbuf_tensor(f"{tag}_v{i}", [128, TK], BF16)) for i in range(2)]
        q_sb = [ctx.enter_context(nc.sbuf_tensor(f"{tag}_q{i}", [128, TQ], BF16)) for i in range(2)]
        a_all = ctx.enter_context(nc.sbuf_tensor(f"{tag}_a", [128, NSL * QCW], BF16))
        a_sb = [a_all[:, i * QCW:(i + 1) * QCW] for i in range(NSL)]
        r_sb = ctx.enter_context(nc.sbuf_tensor(f"{tag}_r", [1, QCW], F32))
        rb_sb = ctx.enter_context(nc.sbuf_tensor(f"{tag}_rb", [128, QCW], F32))
        pf_sb = [ctx.enter_context(nc.sbuf_tensor(f"{tag}_pf{i}", [128, QCW], BF16)) for i in range(2)]
        ps_s = ctx.enter_context(nc.psum_tensor(f"{tag}_pss", [128, NSL * QCW], F32))
        ps_av = ctx.enter_context(nc.psum_tensor(f"{tag}_pav", [128, QCW], F32))
        ps_sm = ctx.enter_context(nc.psum_tensor(f"{tag}_psm", [128, QCW], F32))
        ps_bc = ctx.enter_context(nc.psum_tensor(f"{tag}_pbc", [128, QCW], F32))

        s_plan, p_plan, d_plan, bc_plan, e2_plan, o_plan = {}, {}, {}, {}, {}, {}
        cs, cp, cd, cb, ce2 = s_s.n, s_p.n, s_d.n, s_bc.n, s_e2.n
        e_base = s_e.n
        e_done = lambda gi: e_base + gi // 2 + 1
        ob_base = [sc.n for sc in po.sems]
        for h in range(nheads):
            for qc in range(NQ):
                gi = h * NQ + qc
                for kt in range(NKT):
                    cs += 1; s_plan[(h, qc, kt)] = cs
                    cp += 1; p_plan[(h, qc, kt)] = cp
                cd += 1; d_plan[(h, qc, "r")] = cd
                cb += 1; bc_plan[(h, qc)] = cb
                ce2 += 1; e2_plan[(h, qc)] = ce2
                cd += 1; d_plan[(h, qc, "pf")] = cd
                o_plan[(h, qc)] = (gi % 2, ob_base[gi % 2] + 16 * (gi // 2 + 1))
        ld_marks = []

        def g2t(gi):
            return (gi // (NQ * NKT), (gi % (NQ * NKT)) // NKT, gi % NKT)

        def sync_io(eng):
            def load(h):
                kv = kv_of_head(h)
                pld.inc(eng.dma_start(out=k_sb[h % 2][:], in_=k_dram[kv]), h)
                pld.inc(eng.dma_start(out=v_sb[h % 2][:].rearrange("p (kt c) -> p kt c", kt=NKT),
                                      in_=v_dram[kv].rearrange("(kt p) c -> p kt c", p=128)), h)
                inst = eng.dma_start(out=q_sb[h % 2][:], in_=q_dram[h])
                ld_marks.append(pld.inc(inst, h))
            _waits(eng, ready)
            load(0)
            if nheads > 1:
                load(1)
            for h in range(nheads):
                for qc in range(NQ):
                    gi = h * NQ + qc
                    eng.wait_ge(s_d.h, d_plan[(h, qc, "pf")])
                    inst = eng.dma_start(out=pf_dram[h][:, qc * QCW:(qc + 1) * QCW],
                                         in_=pf_sb[gi % 2][:])
                    po.inc(inst, gi)
                if h + 2 < nheads:
                    load(h + 2)

        def pe(eng):
            nun = nheads * NQ

            def s_mm(gi):
                hh, qq, kk = g2t(gi)
                if hh * NQ + qq > 0 and kk == 0:
                    pld.wait(eng, ld_marks[hh])
                if gi >= NSL:
                    eng.wait_ge(s_e.h, e_done(gi - NSL))
                inst = nc.tensor.matmul(ps_s[:, (gi % NSL) * QCW:(gi % NSL + 1) * QCW],
                                        k_sb[hh % 2][:, kk * 128:(kk + 1) * 128],
                                        q_sb[hh % 2][:, qq * QCW:(qq + 1) * QCW],
                                        start=True, stop=True)
                s_s.inc(inst, 1)

            pld.wait(eng, ld_marks[0])
            s_mm(0)
            s_mm(1)
            for u in range(nun):
                h, qc = u // NQ, u % NQ
                if u > 0:
                    prev = ((u - 1) // NQ, (u - 1) % NQ)
                    eng.wait_ge(s_d.h, d_plan[(prev[0], prev[1], "pf")])
                for kt in range(NKT):
                    gi = u * NKT + kt
                    if gi + 2 < nun * NKT:
                        s_mm(gi + 2)
                    eng.wait_ge(s_e.h, e_done(gi))
                    nc.tensor.matmul(ps_av[:], v_sb[h % 2][:, kt * 128:(kt + 1) * 128],
                                     a_sb[gi % NSL], start=(kt == 0), stop=(kt == NKT - 1))
                    inst = nc.tensor.matmul(ps_sm[0:1, :], cst["ones_col_bf"], a_sb[gi % NSL],
                                            start=(kt == 0), stop=(kt == NKT - 1))
                    s_p.inc(inst, 1)
                eng.wait_ge(s_d.h, d_plan[(h, qc, "r")])
                inst = nc.tensor.matmul(ps_bc[:], cst["ones_row_f32"], r_sb[:], start=True, stop=True)
                s_bc.inc(inst, 1)

        def act(eng):
            for h in range(nheads):
                for qc in range(NQ):
                    for kt in range(0, NKT, 2):
                        gi = (h * NQ + qc) * NKT + kt
                        sl = (gi % NSL) * QCW
                        eng.wait_ge(s_s.h, s_plan[(h, qc, kt + 1)])
                        if gi >= NSL:
                            eng.wait_ge(s_p.h, p_plan[g2t(gi + 1 - NSL)])
                        ebias = cst["nshift"] if shift else 0.0
                        inst = eng.activation(a_all[:, sl:sl + 2 * QCW], ps_s[:, sl:sl + 2 * QCW],
                                              AF.Exp, scale=scale, bias=ebias)
                        s_e.inc(inst, 1)
                    eng.wait_ge(s_bc.h, bc_plan[(h, qc)])
                    inst = eng.copy(rb_sb[:], ps_bc[:])
                    s_e2.inc(inst, 1)

        def dve(eng):
            for h in range(nheads):
                for qc in range(NQ):
                    eng.wait_ge(s_p.h, p_plan[(h, qc, NKT - 1)])
                    inst = eng.reciprocal(r_sb[:], ps_sm[0:1, :])
                    s_d.inc(inst, 1)
                    eng.wait_ge(s_e2.h, e2_plan[(h, qc)])
                    gi = h * NQ + qc
                    if gi >= 2:
                        pv = gi - 2
                        po.wait(eng, o_plan[(pv // NQ, pv % NQ)])
                    inst = eng.tensor_tensor(pf_sb[(h * NQ + qc) % 2][:], ps_av[:], rb_sb[:], ALU.mult)
                    s_d.inc(inst, 1)

        b.sec("sync", sync_io)
        b.sec("tensor", pe)
        b.sec("scalar", act)
        b.sec("vector", dve)
    return po.gates()



def emit_out_proj(b, tag, wo_dram, bo_2d, pf_dram, nheads, resid, T, dst,
                  ready, cst):
    nc = b.nc
    NQ = T // QCW
    NTIL = NQ * NT
    s_w = b.sem("op_w")
    px = Pipe(b, "op_x")
    po = Pipe(b, "op_o")
    s_mm = b.sem("op_mm")
    s_e = b.sem("op_e")
    s_d = b.sem("op_d")

    resid_r = resid.rearrange("(dt p) t -> p dt t", p=128)
    dst_r = dst.rearrange("(dt p) t -> p dt t", p=128)

    with ExitStack() as ctx:
        wo_sb = ctx.enter_context(nc.sbuf_tensor(f"{tag}_wo", [128, nheads * D], BF16))
        wst = [ctx.enter_context(nc.sbuf_tensor(f"{tag}_wst{i}", [128, nheads * 64], F32)) for i in range(2)]
        bo_sb = ctx.enter_context(nc.sbuf_tensor(f"{tag}_bo", [128, NT], F32))
        pf_sb = [ctx.enter_context(nc.sbuf_tensor(f"{tag}_pf{i}", [128, nheads * QCW], BF16)) for i in range(2)]
        xr_sb = [ctx.enter_context(nc.sbuf_tensor(f"{tag}_xr{i}", [128, NT * QCW], F32)) for i in range(2)]
        tv = [ctx.enter_context(nc.sbuf_tensor(f"{tag}_tv{i}", [128, QCW], F32)) for i in range(2)]
        ov = [ctx.enter_context(nc.sbuf_tensor(f"{tag}_ov{i}", [128, QCW], F32)) for i in range(2)]
        ps = [ctx.enter_context(nc.psum_tensor(f"{tag}_ps{i}", [128, QCW], F32)) for i in range(2)]

        has_bias = bo_2d is not None
        NWC = D // 64
        pwst = Pipe(b, "op_wst")
        wst_marks = []
        w_base_cast = s_w.n
        w_mark = s_w.n + NWC + (16 if has_bias else 0)
        mm_plan = [s_mm.n + i + 1 for i in range(NTIL)]
        e_plan = [s_e.n + i + 1 for i in range(NTIL)]
        d_plan = [s_d.n + i + 1 for i in range(NTIL)]
        ob_base = [sc.n for sc in po.sems]
        o_mark = lambda i: (i % 2, ob_base[i % 2] + 16 * (i // 2 + 1))
        x_marks = []

        def w_loader(eng):
            _waits(eng, ready)
            if has_bias:
                eng.dma_start(out=bo_sb[:], in_=bo_2d).then_inc(s_w.h, 16)
                s_w.n += 16
            for ch in range(NWC):
                if ch >= 2:
                    eng.wait_ge(s_w.h, w_base_cast + (16 if has_bias else 0) + ch - 1)
                inst = eng.dma_start(out=wst[ch % 2][:].rearrange("p (hh o) -> p hh o", o=64),
                                     in_=wo_dram.rearrange("(hh p) o -> p hh o", p=128)[:, :, ch * 64:(ch + 1) * 64])
                wst_marks.append(pwst.inc(inst, ch))

        def w_cast(eng):
            for ch in range(NWC):
                pwst.wait(eng, wst_marks[ch])
                inst = eng.copy(wo_sb[:].rearrange("p (hh o) -> p hh o", o=D)[:, :, ch * 64:(ch + 1) * 64],
                                wst[ch % 2][:].rearrange("p (hh o) -> p hh o", o=64))
                s_w.inc(inst, 1)

        def sync_io(eng):
            def load(qc):
                px.inc(eng.dma_start(out=pf_sb[qc % 2][:].rearrange("p (hh c) -> p hh c", hh=nheads),
                                     in_=pf_dram.rearrange("hh p t -> p hh t")[:, :, qc * QCW:(qc + 1) * QCW]), qc)
                inst = eng.dma_start(out=xr_sb[qc % 2][:].rearrange("p (dt c) -> p dt c", dt=NT),
                                     in_=resid_r[:, :, qc * QCW:(qc + 1) * QCW])
                x_marks.append(px.inc(inst, qc))
            _waits(eng, ready)
            load(0)
            if NQ > 1:
                load(1)
            idx = 0
            for qc in range(NQ):
                for dt in range(NT):
                    eng.wait_ge(s_d.h, d_plan[idx])
                    inst = eng.dma_start(out=dst_r[:, dt, qc * QCW:(qc + 1) * QCW], in_=ov[idx % 2][:])
                    po.inc(inst, idx)
                    idx += 1
                if qc + 2 < NQ:
                    load(qc + 2)

        def mms(eng):
            eng.wait_ge(s_w.h, w_mark)
            idx = 0
            for qc in range(NQ):
                px.wait(eng, x_marks[qc])
                for dt in range(NT):
                    if idx >= 2:
                        eng.wait_ge(s_e.h, e_plan[idx - 2])
                    p = ps[idx % 2]
                    for hh in range(nheads):
                        inst = nc.tensor.matmul(
                            p[:], wo_sb[:, hh * D + dt * 128: hh * D + (dt + 1) * 128],
                            pf_sb[qc % 2][:, hh * QCW:(hh + 1) * QCW],
                            start=(hh == 0), stop=(hh == nheads - 1))
                    s_mm.inc(inst, 1)
                    idx += 1

        def evac(eng):
            idx = 0
            for qc in range(NQ):
                for dt in range(NT):
                    eng.wait_ge(s_mm.h, mm_plan[idx])
                    if idx >= 2:
                        eng.wait_ge(s_d.h, d_plan[idx - 2])
                    bias = bo_sb[:, dt:dt + 1] if has_bias else 0.0
                    fn = AF.Identity if has_bias else AF.Copy
                    inst = eng.activation(tv[idx % 2][:], ps[idx % 2][:], fn, bias=bias)
                    s_e.inc(inst, 1)
                    idx += 1

        def resid_dve(eng):
            idx = 0
            for qc in range(NQ):
                for dt in range(NT):
                    eng.wait_ge(s_e.h, e_plan[idx])
                    if idx >= 2:
                        po.wait(eng, o_mark(idx - 2))
                    inst = eng.tensor_tensor(ov[idx % 2][:], tv[idx % 2][:],
                                             xr_sb[qc % 2][:, dt * QCW:(dt + 1) * QCW], ALU.add)
                    s_d.inc(inst, 1)
                    idx += 1

        b.sec("sync", w_loader)
        b.sec("scalar", w_cast)
        b.sec("sync", sync_io)
        b.sec("tensor", mms)
        b.sec("scalar", evac)
        b.sec("vector", resid_dve)
    return po.gates()



def emit_gate_up(b, tag, wgu_d, xln_d, T, hid, ready, w_ready):
    nc = b.nc
    NQ = T // QCW
    NP = I_FFN // 128
    GC = 2
    NCH = NP // GC
    NTIL = NP * NQ
    pw = Pipe(b, "gu_w")
    s_x = b.sem("gu_x")
    po = Pipe(b, "gu_o")
    s_mm = b.sem("gu_mm")
    s_si = b.sem("gu_si")
    s_h = b.sem("gu_h")

    xln_r = xln_d.rearrange("(dt p) t -> p dt t", p=128)
    wgu_r = wgu_d.rearrange("(kt p) o -> p kt o", p=128)

    with ExitStack() as ctx:
        xln_sb = ctx.enter_context(nc.sbuf_tensor(f"{tag}_xln", [128, NT * T], BF16))
        wg = [ctx.enter_context(nc.sbuf_tensor(f"{tag}_wg{i}", [128, NT * 128 * GC], BF16)) for i in range(2)]
        wu = [ctx.enter_context(nc.sbuf_tensor(f"{tag}_wu{i}", [128, NT * 128 * GC], BF16)) for i in range(2)]
        sil = [ctx.enter_context(nc.sbuf_tensor(f"{tag}_si{i}", [128, QCW], F32)) for i in range(2)]
        hv = [ctx.enter_context(nc.sbuf_tensor(f"{tag}_hv{i}", [128, QCW], BF16)) for i in range(2)]
        ps_g = [ctx.enter_context(nc.psum_tensor(f"{tag}_pg{i}", [128, QCW], F32)) for i in range(2)]
        ps_u = [ctx.enter_context(nc.psum_tensor(f"{tag}_pu{i}", [128, QCW], F32)) for i in range(2)]

        x_mark = s_x.n + 16
        mm_plan = [s_mm.n + i + 1 for i in range(NTIL)]
        si_plan = [s_si.n + i + 1 for i in range(NTIL)]
        h_plan = [s_h.n + i + 1 for i in range(NTIL)]
        ob_base = [sc.n for sc in po.sems]
        o_mark = lambda i: (i % 2, ob_base[i % 2] + 16 * (i // 2 + 1))
        w_marks = []

        def sync_io(eng):
            _waits(eng, ready)
            eng.dma_start(out=xln_sb[:].rearrange("p (dt t) -> p dt t", t=T),
                          in_=xln_r).then_inc(s_x.h, 16)
            s_x.n += 16
            idx = 0
            for ch in range(NCH):
                for j in range(GC):
                    g = ch * GC + j
                    for qc in range(NQ):
                        eng.wait_ge(s_h.h, h_plan[idx])
                        inst = eng.dma_start(out=hid[g * 128:(g + 1) * 128, qc * QCW:(qc + 1) * QCW],
                                             in_=hv[idx % 2][:])
                        po.inc(inst, idx)
                        idx += 1

        def w_loader(eng):
            _waits(eng, w_ready)
            _waits(eng, ready)
            for ch in range(NCH):
                if ch >= 2:
                    eng.wait_ge(s_mm.h, mm_plan[(ch - 1) * GC * NQ - 1])
                cs = slice(ch * GC * 128, (ch + 1) * GC * 128)
                us = slice(I_FFN + ch * GC * 128, I_FFN + (ch + 1) * GC * 128)
                pw.inc(eng.dma_start(out=wg[ch % 2][:].rearrange("p (kt o) -> p kt o", kt=NT),
                                     in_=wgu_r[:, :, cs]), ch)
                inst = eng.dma_start(out=wu[ch % 2][:].rearrange("p (kt o) -> p kt o", kt=NT),
                                     in_=wgu_r[:, :, us])
                w_marks.append(pw.inc(inst, ch))

        def mms(eng):
            eng.wait_ge(s_x.h, x_mark)
            idx = 0
            for ch in range(NCH):
                pw.wait(eng, w_marks[ch])
                for j in range(GC):
                    for qc in range(NQ):
                        if idx >= 2:
                            eng.wait_ge(s_h.h, h_plan[idx - 2])
                            eng.wait_ge(s_si.h, si_plan[idx - 2])
                        pg, pu = ps_g[idx % 2], ps_u[idx % 2]
                        for kt in range(NT):
                            nc.tensor.matmul(pg[:], wg[ch % 2][:, kt * GC * 128 + j * 128: kt * GC * 128 + (j + 1) * 128],
                                             xln_sb[:, kt * T + qc * QCW: kt * T + (qc + 1) * QCW],
                                             start=(kt == 0), stop=(kt == NT - 1))
                        for kt in range(NT):
                            inst = nc.tensor.matmul(pu[:], wu[ch % 2][:, kt * GC * 128 + j * 128: kt * GC * 128 + (j + 1) * 128],
                                                    xln_sb[:, kt * T + qc * QCW: kt * T + (qc + 1) * QCW],
                                                    start=(kt == 0), stop=(kt == NT - 1))
                        s_mm.inc(inst, 1)
                        idx += 1

        def silu_act(eng):
            for idx in range(NTIL):
                eng.wait_ge(s_mm.h, mm_plan[idx])
                if idx >= 2:
                    eng.wait_ge(s_h.h, h_plan[idx - 2])
                inst = eng.activation(sil[idx % 2][:], ps_g[idx % 2][:], AF.Silu)
                s_si.inc(inst, 1)

        def mult_dve(eng):
            for idx in range(NTIL):
                eng.wait_ge(s_si.h, si_plan[idx])
                if idx >= 2:
                    po.wait(eng, o_mark(idx - 2))
                inst = eng.tensor_tensor(hv[idx % 2][:], sil[idx % 2][:], ps_u[idx % 2][:], ALU.mult)
                s_h.inc(inst, 1)

        b.sec("sync", sync_io)
        b.sec("gpsimd", w_loader)
        b.sec("tensor", mms)
        b.sec("scalar", silu_act)
        b.sec("vector", mult_dve)
    return po.gates()



def emit_down(b, tag, wdn_d, hid, T, resid, dst, hid_ready, w_ready, cst):
    nc = b.nc
    NQ = T // QCW
    NKT = I_FFN // 128
    NTIL = NQ * NT
    pw = Pipe(b, "dn_w")
    s_h = b.sem("dn_h")
    po = Pipe(b, "dn_o")
    s_mm = b.sem("dn_mm")
    s_e = b.sem("dn_e")
    s_q = b.sem("dn_q")
    s_ms = b.sem("dn_ms")
    s_r1 = b.sem("dn_r1")
    s_r2 = b.sem("dn_r2")
    s_bc = b.sem("dn_bc")
    s_f = b.sem("dn_f")

    resid_r = resid.rearrange("(dt p) t -> p dt t", p=128)
    dst_r = dst.rearrange("(dt p) t -> p dt t", p=128)

    def hrow(t):
        return t, 0, 128

    with ExitStack() as ctx:
        WCH = 256
        NW = D // WCH
        h_sb = ctx.enter_context(nc.sbuf_tensor(f"{tag}_hs", [128, 44 * QCW], BF16))
        w_sb = [ctx.enter_context(nc.sbuf_tensor(f"{tag}_ws{i}", [128, 44 * WCH], BF16)) for i in range(2)]
        y_sb = ctx.enter_context(nc.sbuf_tensor(f"{tag}_y", [128, NT * QCW], F32))
        sqv = [ctx.enter_context(nc.sbuf_tensor(f"{tag}_sq{i}", [128, QCW], BF16)) for i in range(2)]
        ms_t = ctx.enter_context(nc.sbuf_tensor(f"{tag}_mst", [1, 2 * QCW], F32))
        rr_t = ctx.enter_context(nc.sbuf_tensor(f"{tag}_rrt", [1, 2 * QCW], F32))
        xr_sb = [ctx.enter_context(nc.sbuf_tensor(f"{tag}_xr{i}", [128, QCW], F32)) for i in range(2)]
        tv = [ctx.enter_context(nc.sbuf_tensor(f"{tag}_tv{i}", [128, QCW], F32)) for i in range(2)]
        ov = [ctx.enter_context(nc.sbuf_tensor(f"{tag}_ov{i}", [128, QCW], F32)) for i in range(2)]
        ps_y = [ctx.enter_context(nc.psum_tensor(f"{tag}_py{i}", [128, QCW], F32)) for i in range(2)]
        ps_ms = ctx.enter_context(nc.psum_tensor(f"{tag}_pms", [128, QCW], F32))
        ps_bc = ctx.enter_context(nc.psum_tensor(f"{tag}_pbc", [128, QCW], F32))

        h_base = s_h.n
        mm_plan = [s_mm.n + i + 1 for i in range(NTIL)]
        e_plan = [s_e.n + i + 1 for i in range(NTIL)]
        q_plan = [s_q.n + i + 1 for i in range(NTIL)]
        ms_plan = [s_ms.n + i + 1 for i in range(NTIL)]
        f_plan = [s_f.n + i + 1 for i in range(NTIL)]
        ob_base = [sc.n for sc in po.sems]
        o_mark = lambda i: (i % 2, ob_base[i % 2] + 16 * (i // 2 + 1))
        r1_base, r2_base = s_r1.n, s_r2.n
        r1_done = lambda qc: r1_base + qc + 1
        r2_done = lambda qc: r2_base + qc + 1
        bc_plan = [s_bc.n + qc + 1 for qc in range(NQ)]
        w_marks = []

        pxr = Pipe(b, "dn_xrl")
        xrb_base = [sc.n for sc in pxr.sems]
        xr_mark = lambda i: (i % 2, xrb_base[i % 2] + 16 * (i // 2 + 1))

        def sync_io(eng):
            def load(qc):
                eng.dma_start(out=h_sb[:].rearrange("p (blk c) -> p blk c", blk=44),
                              in_=hid.rearrange("(blk p) t -> p blk t", p=128)[:, :, qc * QCW:(qc + 1) * QCW]
                              ).then_inc(s_h.h, 16)
                s_h.n += 16

            def load_w(idx):
                if idx >= 2:
                    eng.wait_ge(s_mm.h, mm_plan[(idx - 1) * 2 - 1])
                dtc = idx % NW
                cs = slice(dtc * WCH, (dtc + 1) * WCH)
                inst = eng.dma_start(out=w_sb[idx % 2][:].rearrange("p (blk c) -> p blk c", blk=44),
                                     in_=wdn_d.rearrange("(blk p) o -> p blk o", p=128)[:, :, cs])
                w_marks.append(pw.inc(inst, idx))

            def load_xr(gi):
                qc, dt = gi // NT, gi % NT
                pxr.inc(eng.dma_start(out=xr_sb[gi % 2][:],
                                      in_=resid_r[:, dt, qc * QCW:(qc + 1) * QCW]), gi)
            _waits(eng, hid_ready)
            _waits(eng, w_ready)
            load(0)
            if DOWN_LITE != 2:
                load_xr(0)
                load_xr(1)
            idx = 0
            for qc in range(NQ):
                for dtc in range(NW):
                    load_w(qc * NW + dtc)
                for dt in range(NT):
                    if DOWN_LITE == 2:
                        eng.wait_ge(s_e.h, e_plan[idx])
                        inst = eng.dma_start(out=dst_r[:, dt, qc * QCW:(qc + 1) * QCW],
                                             in_=y_sb[:, dt * QCW:(dt + 1) * QCW])
                        po.inc(inst, idx)
                        idx += 1
                        continue
                    eng.wait_ge(s_f.h, f_plan[idx])
                    inst = eng.dma_start(out=dst_r[:, dt, qc * QCW:(qc + 1) * QCW], in_=ov[idx % 2][:])
                    po.inc(inst, idx)
                    if idx + 2 < NQ * NT:
                        load_xr(idx + 2)
                    idx += 1
                if qc + 1 < NQ:
                    load(qc + 1)

        def pe_sec(eng):
            def ms_mm(qc, dt):
                eng.wait_ge(s_q.h, q_plan[qc * NT + dt])
                if dt == 0 and qc >= 1:
                    eng.wait_ge(s_r1.h, r1_done(qc - 1))
                inst = nc.tensor.matmul(ps_ms[0:1, :], cst["ones_col_bf"], sqv[(qc * NT + dt) % 2][:],
                                        start=(dt == 0), stop=(dt == NT - 1))
                s_ms.inc(inst, 1)
            idx = 0
            for qc in range(NQ):
                eng.wait_ge(s_h.h, h_base + 16 * (qc + 1))
                for dtc in range(NW):
                    pw.wait(eng, w_marks[qc * NW + dtc])
                    for di in range(WCH // 128):
                        dt = dtc * (WCH // 128) + di
                        if idx >= 2:
                            eng.wait_ge(s_e.h, e_plan[idx - 2])
                        p = ps_y[idx % 2]
                        for t in range(NKT):
                            blk, basep, rows = hrow(t)
                            inst = nc.tensor.matmul(
                                p[:],
                                w_sb[(qc * NW + dtc) % 2][basep:basep + rows,
                                                          blk * WCH + di * 128: blk * WCH + (di + 1) * 128],
                                h_sb[basep:basep + rows, blk * QCW:(blk + 1) * QCW],
                                start=(t == 0), stop=(t == NKT - 1))
                        s_mm.inc(inst, 1)
                        if DOWN_LITE == 0 and dt >= 2:
                            ms_mm(qc, dt - 2)
                        idx += 1
                if DOWN_LITE == 0:
                    ms_mm(qc, NT - 2)
                    ms_mm(qc, NT - 1)
                    eng.wait_ge(s_r2.h, r2_done(qc))
                    if qc >= 1:
                        eng.wait_ge(s_f.h, f_plan[qc * NT - 1])
                    inst = nc.tensor.matmul(ps_bc[:], cst["ones_row_f32"],
                                            rr_t[:, (qc % 2) * QCW:(qc % 2 + 1) * QCW], start=True, stop=True)
                    s_bc.inc(inst, 1)

        def act_sec(eng):
            if DOWN_LITE:
                return
            idx = 0
            for qc in range(NQ):
                for dt in range(NT):
                    eng.wait_ge(s_e.h, e_plan[idx])
                    if idx >= 2:
                        eng.wait_ge(s_ms.h, ms_plan[idx - 2])
                    inst = eng.activation(sqv[idx % 2][:], y_sb[:, dt * QCW:(dt + 1) * QCW], AF.Square)
                    s_q.inc(inst, 1)
                    idx += 1
                eng.wait_ge(s_r1.h, r1_done(qc))
                if qc >= 2:
                    eng.wait_ge(s_bc.h, bc_plan[qc - 2])
                qs = slice((qc % 2) * QCW, (qc % 2 + 1) * QCW)
                eng.activation(rr_t[:, qs], ms_t[:, qs], AF.Ln, bias=cst["eps_rms"])
                inst = eng.activation(rr_t[:, qs], rr_t[:, qs], AF.Exp, scale=-0.5)
                s_r2.inc(inst, 1)

        def dve_sec(eng):
            idx = 0
            for qc in range(NQ):
                for dt in range(NT):
                    eng.wait_ge(s_mm.h, mm_plan[idx])
                    if qc >= 1:
                        if DOWN_LITE == 2:
                            po.wait(eng, o_mark((qc - 1) * NT + dt))
                        else:
                            eng.wait_ge(s_f.h, f_plan[(qc - 1) * NT + dt])
                            eng.wait_ge(s_q.h, q_plan[(qc - 1) * NT + dt])
                    inst = eng.tensor_copy(y_sb[:, dt * QCW:(dt + 1) * QCW], ps_y[idx % 2][:])
                    s_e.inc(inst, 1)
                    idx += 1
                if DOWN_LITE == 0:
                    eng.wait_ge(s_ms.h, ms_plan[qc * NT + NT - 1])
                    if qc >= 2:
                        eng.wait_ge(s_r2.h, r2_done(qc - 2))
                    qs = slice((qc % 2) * QCW, (qc % 2 + 1) * QCW)
                    inst = eng.tensor_scalar(ms_t[:, qs], ps_ms[0:1, :], 1.0 / D, None, ALU.mult)
                    s_r1.inc(inst, 1)
                    eng.wait_ge(s_bc.h, bc_plan[qc])
                if DOWN_LITE == 2:
                    continue
                for dt in range(NT):
                    gi = qc * NT + dt
                    sl = slice(dt * QCW, (dt + 1) * QCW)
                    if gi >= 2:
                        po.wait(eng, o_mark(gi - 2))
                    pxr.wait(eng, xr_mark(gi))
                    if DOWN_LITE == 0:
                        eng.tensor_tensor(tv[gi % 2][:], y_sb[:, sl], ps_bc[:], ALU.mult)
                        inst = eng.tensor_tensor(ov[gi % 2][:], tv[gi % 2][:], xr_sb[gi % 2][:], ALU.add)
                    else:
                        inst = eng.tensor_tensor(ov[gi % 2][:], y_sb[:, sl], xr_sb[gi % 2][:], ALU.add)
                    s_f.inc(inst, 1)

        b.sec("sync", sync_io)
        b.sec("tensor", pe_sec)
        b.sec("scalar", act_sec)
        b.sec("vector", dve_sec)
    return po.gates()



def build_program(phases=PHASES, taps=DEBUG_TAPS):
    nc = bass.Bass("TRN2", target_bir_lowering=False)

    def dram_in(name, shape, dt):
        return nc.dram_tensor(name, shape, dt, kind="ExternalInput").ap()

    def dram_int(name, shape, dt, tap=None):
        kind = "ExternalOutput" if (tap is not None and tap in taps) else "Internal"
        return nc.dram_tensor(name, shape, dt, kind=kind).ap()

    x0 = dram_in("x0", [D, T_PRE], F32)
    wq_a = dram_in("wq_a", [D, D], F32)
    wk_a = dram_in("wk_a", [D, D], F32)
    wv_a = dram_in("wv_a", [D, D], F32)
    wo_a = dram_in("wo_a", [D, D], F32)
    bq_a = dram_in("bq_a", [128, NT], F32)
    bk_a = dram_in("bk_a", [128, NT], F32)
    bv_r = dram_in("bv_r", [1, D], F32)
    bo_a = dram_in("bo_a", [128, NT], F32)
    wq_b = dram_in("wq_b", [D, D], F32)
    wk_b = dram_in("wk_b", [D, KVH * DH], F32)
    wv_b = dram_in("wv_b", [D, KVH * DH], F32)
    wo_b = dram_in("wo_b", [D, D], F32)
    gupT = dram_in("gupT", [256, GU], I32)
    sc_gu = dram_in("sc_gu", [256, GU], F32)
    downT = dram_in("downT", [704, D], I32)
    sc_dn = dram_in("sc_dn", [704, D], F32)
    rope_cos = dram_in("rope_cos", [128, T_PRE], F32)
    rope_sin = dram_in("rope_sin", [128, T_PRE], F32)
    pswap_in = dram_in("pswap", [128, 128], F32)

    out_fm = nc.dram_tensor("out_fm", [D, T_SUF], F32, kind="ExternalOutput").ap()

    x1 = dram_int("x1", [D, T_PRE], F32, "x1")
    x2 = dram_int("x2", [D, T_PRE], F32, "x2")
    x3 = dram_int("x3", [D, T_SUF], F32, "x3")
    xln = dram_int("xln", [D, T_PRE], BF16, "xln1")
    wgu_d = dram_int("wgu_d", [D, GU], BF16, "wgu")
    wdn_d = dram_int("wdn_d", [I_FFN, D], BF16, "wdn")
    hid = dram_int("hid", [I_FFN, T_PRE], BF16, "hid")
    ka = dram_int("ka", [H, DH, T_PRE], BF16, "ka")
    va = dram_int("va", [H, T_PRE, DH], BF16, "va")
    qa = dram_int("qa", [H, DH, T_PRE], BF16, "qa")
    pfa = dram_int("pfa", [H, DH, T_PRE], BF16, "pfa")
    kb = dram_int("kb", [KVH, DH, T_PRE], BF16)
    vb = dram_int("vb", [KVH, T_PRE, DH], BF16)
    qb = dram_int("qb", [H, DH, T_SUF], BF16)
    pfb = dram_int("pfb", [H, DH, T_SUF], BF16)

    with ExitStack() as ctx:
        block = ctx.enter_context(nc.Block())
        b = B(nc, ctx, block)

        ones_col_bf = ctx.enter_context(nc.sbuf_tensor("ones_col", [128, 1], BF16))
        ones_row_f32 = ctx.enter_context(nc.sbuf_tensor("ones_row", [1, 128], F32))
        ctpool = ctx.enter_context(nc.sbuf_tensor("ctpool", [128, 3], F32))
        pswap_sb = ctx.enter_context(nc.sbuf_tensor("pswap_sb", [128, 128], BF16))
        s_initc = b.sem("initc")
        s_initd = b.sem("initd")

        def init_consts(eng):
            eng.memset(ones_col_bf[:], 1.0)
            eng.memset(ctpool[:, 0:1], LN_EPS)
            eng.memset(ctpool[:, 1:2], RMS_EPS)
            eng.memset(ctpool[:, 2:3], -EXP_SHIFT_B)
            inst = eng.memset(ones_row_f32[:], 1.0)
            s_initc.inc(inst, 1)

        def init_loads(eng):
            eng.dma_start(out=pswap_sb[:], in_=pswap_in).then_inc(s_initd.h, 16)
            s_initd.n += 16

        b.sec("vector", init_consts)
        b.sec("gpsimd", init_loads)

        def await_init(eng):
            eng.wait_ge(s_initc.h, 1)
            eng.wait_ge(s_initd.h, 16)
        b.sec("tensor", await_init)
        b.sec("scalar", await_init)
        b.sec("vector", await_init)
        b.sec("sync", await_init)

        cst = dict(ones_col_bf=ones_col_bf[:], ones_row_f32=ones_row_f32[:],
                   eps_ln=ctpool[0:1, 0:1], eps_rms=ctpool[0:1, 1:2],
                   nshift=ctpool[:, 2:3])

        deq_done = None
        last = None
        if "p1" in phases or "p1ln" in phases:
            ln1 = emit_ln(b, "l1", x0, T_PRE, xln, cst, None)
            last = ln1
        if "deq" in phases:
            deq_done = emit_dequant(b, gupT, sc_gu, downT, sc_dn, wgu_d, wdn_d)
        if "p1" in phases or "p1kqv" in phases:
            ka_done = emit_fm_proj(b, "ka", wk_a, D, bk_a, xln, T_PRE,
                                   [ka[h] for h in range(H)], ln1, cst=cst)
            qa_done = emit_fm_proj(b, "qa", wq_a, D, bq_a, xln, T_PRE,
                                   [qa[h] for h in range(H)], ka_done, cst=cst)
            va_done = emit_v_proj(b, "va", wv_a, D, bv_r, xln, T_PRE,
                                  [va[h] for h in range(H)], qa_done, cst)
            last = va_done
        if "p1" in phases or "p1att" in phases:
            ata_done = emit_attention(b, "at_a", [qa[h] for h in range(H)],
                                      [ka[h] for h in range(H)], [va[h] for h in range(H)],
                                      lambda h: h, T_PRE, T_PRE, 1.0 / DH, 0.0,
                                      [pfa[h] for h in range(H)], H,
                                      ka_done + qa_done + va_done, cst)
            p1_done = emit_out_proj(b, "wo_a", wo_a, bo_a, pfa, H, x0, T_PRE, x1,
                                    ata_done, cst)
            last = p1_done

        if "p2" in phases or "p2gu" in phases or "p2dn" in phases:
            ln2 = emit_ln(b, "l2", x1, T_PRE, xln, cst, last)
            last = ln2
        if "p2" in phases or "p2gu" in phases or "p2dn" in phases:
            gu2 = emit_gate_up(b, "gu2", wgu_d, xln, T_PRE, hid, ln2, deq_done)
            last = gu2
        if "p2" in phases or "p2dn" in phases:
            p2_done = emit_down(b, "dn2", wdn_d, hid, T_PRE, x1, x2, gu2, deq_done, cst)
            last = p2_done

        if "p3" in phases:
            ln3 = emit_ln(b, "l3", x2, T_PRE, xln, cst, last)
            with ExitStack() as p3ctx:
                cos_sb = p3ctx.enter_context(nc.sbuf_tensor("cos_sb", [128, T_PRE], F32))
                sin_sb = p3ctx.enter_context(nc.sbuf_tensor("sin_sb", [128, T_PRE], F32))
                s_rt = b.sem("ropetab")
                rt_gate_val = ln3

                def rope_loads(eng):
                    _waits(eng, rt_gate_val)
                    eng.dma_start(out=cos_sb[:], in_=rope_cos).then_inc(s_rt.h, 16)
                    s_rt.n += 16
                    eng.dma_start(out=sin_sb[:], in_=rope_sin).then_inc(s_rt.h, 16)
                    s_rt.n += 16
                b.sec("gpsimd", rope_loads)
                rope = dict(pswap=pswap_sb[:], cos=cos_sb[:], sin=sin_sb[:])
                kb_done = emit_fm_proj(b, "kb", wk_b, KVH * DH, None, xln, T_PRE,
                                       [kb[h] for h in range(KVH)], ln3 + [(s_rt, 32)],
                                       rope=rope, cst=cst)
                vb_done = emit_v_proj(b, "vb", wv_b, KVH * DH, None, xln, T_PRE,
                                      [vb[h] for h in range(KVH)], kb_done, cst)
                qb_done = emit_fm_proj(b, "qb", wq_b, D, None, xln, T_SUF,
                                       [qb[h] for h in range(H)], vb_done, rope=rope, cst=cst)
                atb_done = emit_attention(b, "at_b", [qb[h] for h in range(H)],
                                          [kb[h] for h in range(KVH)], [vb[h] for h in range(KVH)],
                                          lambda h: h % 4, T_SUF, T_PRE, 1.0 / float(np.sqrt(DH)),
                                          EXP_SHIFT_B, [pfb[h] for h in range(H)], H,
                                          kb_done + vb_done + qb_done, cst)
            p3_done = emit_out_proj(b, "wo_b", wo_b, None, pfb, H, x2[:, 0:T_SUF], T_SUF,
                                    x3, atb_done, cst)
            last = p3_done

        if "p4" in phases:
            xln_s = xln[:, 0:T_SUF]
            hid_s = hid[:, 0:T_SUF]
            ln4 = emit_ln(b, "l4", x3, T_SUF, xln_s, cst, last)
            gu4 = emit_gate_up(b, "gu4", wgu_d, xln_s, T_SUF, hid_s, ln4, deq_done)
            p4_done = emit_down(b, "dn4", wdn_d, hid_s, T_SUF, x3, out_fm, gu4,
                                deq_done, cst)
            last = p4_done

        fin = last if last is not None else deq_done
        fin_deq = deq_done

        def final_wait(eng):
            _waits(eng, fin)
            _waits(eng, fin_deq)

        b.sec("sync", final_wait)
    return nc



def _bias2d(v):
    return np.ascontiguousarray(np.asarray(v, np.float32).reshape(NT, 128).T)


def _prep_inputs(inp):
    f32 = np.float32
    x = np.asarray(inp["L0_attn_a_x"])
    hm = HMAP
    im_full = np.concatenate([IMAP, I_FFN + IMAP])

    shared = dict(
        wq_a=np.ascontiguousarray(np.asarray(inp["L0_attn_a_Wq"], f32)[hm, :]),
        wk_a=np.ascontiguousarray(np.asarray(inp["L0_attn_a_Wk"], f32)[hm, :]),
        wv_a=np.ascontiguousarray(np.asarray(inp["L0_attn_a_Wv"], f32)[hm, :]),
        wo_a=np.ascontiguousarray(np.asarray(inp["L0_attn_a_Wo"], f32)[:, hm]),
        bq_a=_bias2d(inp["L0_attn_a_bq"]),
        bk_a=_bias2d(inp["L0_attn_a_bk"]),
        bv_r=np.asarray(inp["L0_attn_a_bv"], f32).reshape(1, D),
        bo_a=_bias2d(np.asarray(inp["L0_attn_a_bo"], f32)[hm]),
        wq_b=np.ascontiguousarray(np.asarray(inp["L0_attn_b_W_q"], f32)[hm, :]),
        wk_b=np.ascontiguousarray(np.asarray(inp["L0_attn_b_W_k"], f32)[hm, :]),
        wv_b=np.ascontiguousarray(np.asarray(inp["L0_attn_b_W_v"], f32)[hm, :]),
        wo_b=np.ascontiguousarray(np.asarray(inp["L0_attn_b_W_o"], f32)[:, hm]),
    )
    gup = np.asarray(inp["L0_ffn_gate_up_packed"])
    gup_p = np.concatenate([gup[:I_FFN][IMAP], gup[I_FFN:][IMAP]])
    shared["gupT"] = np.ascontiguousarray(gup_p.T)
    shared["sc_gu"] = np.ascontiguousarray(
        np.repeat(np.asarray(inp["L0_ffn_gate_up_scales"]).astype(f32), 16, axis=0)[:, im_full])
    dnp = np.asarray(inp["L0_ffn_down_packed"])[hm, :]
    shared["downT"] = np.ascontiguousarray(dnp.T)
    shared["sc_dn"] = np.ascontiguousarray(
        np.repeat(np.asarray(inp["L0_ffn_down_scales"]).astype(f32), 16, axis=0)[:, hm])
    pswap = np.zeros((128, 128), f32)
    pswap[np.arange(128), np.arange(128) ^ 1] = 1.0
    shared["pswap"] = pswap

    in_maps = []
    for c in range(8):
        bidx, half = c // 2, c % 2
        xb = np.asarray(x[bidx])[:, hm]
        if half == 1:
            xb = np.roll(xb, -T_SUF, axis=0)
        m = dict(shared)
        m["x0"] = np.ascontiguousarray(xb.T.astype(f32))
        pos = (np.arange(T_PRE) + half * T_SUF) % T_PRE
        cidx = np.arange(64, dtype=np.float64)
        freqs = 1.0 / (10000.0 ** (cidx / 64.0))
        ang = pos[None, :] * freqs[:, None]
        cos_t = np.repeat(np.cos(ang), 2, axis=0).astype(f32)
        sin_t = np.repeat(np.sin(ang), 2, axis=0).astype(f32)
        sin_t[0::2, :] *= -1.0
        m["rope_cos"] = np.ascontiguousarray(cos_t)
        m["rope_sin"] = np.ascontiguousarray(sin_t)
        in_maps.append(m)
    return in_maps


def kernel(**inputs):
    in_maps = _prep_inputs(inputs)
    nc = build_program()
    res = run_bass_kernel_spmd(nc, in_maps, core_ids=list(range(8)))
    out = np.empty((4, T_PRE, D), np.float32)
    for c in range(8):
        bidx, half = c // 2, c % 2
        of = np.asarray(res.results[c]["out_fm"])
        sl = slice(half * T_SUF, (half + 1) * T_SUF)
        out[bidx, sl, :][:, HMAP] = of.T
    return out


# version 07
# speedup vs baseline: 1.1490x; 1.1490x over previous
import numpy as np
import ml_dtypes
from contextlib import ExitStack

import concourse.bass as bass
import concourse.mybir as mybir
from concourse.bass_utils import run_bass_kernel_spmd

F32 = mybir.dt.float32
BF16 = mybir.dt.bfloat16
I32 = mybir.dt.int32
U16 = mybir.dt.uint16
AF = mybir.ActivationFunctionType
ALU = mybir.AluOpType

D = 2048
NT = D // 128
H = 16
DH = 128
KVH = 4
I_FFN = 5632
GU = 2 * I_FFN
T_PRE = 4096
T_SUF = 2048
QCW = 512
LN_EPS = 1e-5
RMS_EPS = 1e-6
EXP_SHIFT_B = 4.0

PHASES = ("deq", "p1", "p2", "p3", "p4")
DEBUG_TAPS = ()
DOWN_LITE = 0


def _pi_h():
    hmap = np.empty(D, np.int64)
    for jb in range(2):
        for k in range(8):
            jj = np.arange(128)
            hmap[(jb * 8 + k) * 128 + jj] = 8 * (jb * 128 + jj) + k
    return hmap


def _pi_i():
    imap = np.empty(I_FFN, np.int64)
    off = 0
    tiles = []
    for jb2 in range(6):
        rows = 128 if jb2 < 5 else 64
        for k in range(8):
            jj = np.arange(rows)
            imap[off:off + rows] = 8 * (jb2 * 128 + jj) + k
            tiles.append((jb2, k, off, rows))
            off += rows
    assert off == I_FFN
    return imap, tiles


HMAP = _pi_h()
IMAP, DN_TILES = _pi_i()


class SemC:
    def __init__(self, ctx, nc, name):
        self.h = ctx.enter_context(nc.semaphore(name))
        self.n = 0

    def inc(self, inst, k):
        inst.then_inc(self.h, k)
        self.n += k
        return self.n


class Pipe:

    def __init__(self, b, name, nslots=2):
        self.sems = [b.sem(f"{name}_{i}") for i in range(nslots)]
        self.nslots = nslots

    def inc(self, inst, slot, k=16):
        sc = self.sems[slot % self.nslots]
        inst.then_inc(sc.h, k)
        sc.n += k
        return (slot % self.nslots, sc.n)

    def wait(self, eng, mark):
        slot, val = mark
        eng.wait_ge(self.sems[slot].h, val)

    def gates(self):
        return [(sc, sc.n) for sc in self.sems]


class B:
    def __init__(self, nc, ctx, block):
        self.nc = nc
        self.ctx = ctx
        self.block = block
        self.sems = {}

    def sem(self, name):
        if name not in self.sems:
            self.sems[name] = SemC(self.ctx, self.nc, name)
        return self.sems[name]

    def sec(self, engine_name, fn):
        getattr(self.block, engine_name)(fn)


def _waits(eng, ready):
    if ready is None:
        return
    if isinstance(ready, tuple):
        ready = [ready]
    for (s, v) in ready:
        if v > 0:
            eng.wait_ge(s.h, v)



def emit_dequant(b, gupT, sc_gu, downT, sc_dn, wgu_d, wdn_d):
    nc = b.nc
    pld = Pipe(b, "deq_ld")
    s_dv = b.sem("deq_dv")
    pst = Pipe(b, "deq_st")

    OC = 704
    jobs = []
    for jb in range(2):
        for oc in range(GU // OC):
            cs = slice(oc * OC, (oc + 1) * OC)
            jobs.append((128, OC, gupT[jb * 128:(jb + 1) * 128, cs],
                         sc_gu[jb * 128:(jb + 1) * 128, cs],
                         [(k, wgu_d[(jb * 8 + k) * 128:(jb * 8 + k + 1) * 128, cs])
                          for k in range(8)]))
    DC = 512
    for jb2 in range(6):
        rows = 128 if jb2 < 5 else 64
        for dc in range(D // DC):
            ds_ = slice(dc * DC, (dc + 1) * DC)
            planes = [(k, wdn_d[off:off + trows, ds_])
                      for (tjb2, k, off, trows) in DN_TILES if tjb2 == jb2]
            jobs.append((rows, DC, downT[jb2 * 128:jb2 * 128 + rows, ds_],
                         sc_dn[jb2 * 128:jb2 * 128 + rows, ds_], planes))

    nbuf = 2
    ctx = b.ctx
    mx_c = OC
    sbr = lambda name, shape, dt: ctx.enter_context(nc.sbuf_tensor(name, shape, dt, side="right"))
    slab = [sbr(f"dq_slab{i}", [128, mx_c], I32) for i in range(nbuf)]
    scl = [sbr(f"dq_scl{i}", [128, mx_c], F32) for i in range(nbuf)]
    t1 = sbr("dq_t1", [128, mx_c], I32)
    bM = sbr("dq_bM", [128, mx_c], U16)
    t4 = sbr("dq_t4", [128, mx_c], BF16)
    t5 = sbr("dq_t5", [128, mx_c], BF16)
    s32 = t1
    s16 = sbr("dq_s16", [128, mx_c], U16)
    t6 = t4.bitcast(U16) if False else sbr("dq_t6", [128, mx_c], U16)
    wout = [sbr(f"dq_w{i}", [128, mx_c], BF16) for i in range(2)]

    dv_base = s_dv.n
    dv_plan = [dv_base]
    acc = dv_base
    for (_, _, _, _, planes) in jobs:
        acc += len(planes)
        dv_plan.append(acc)
    n_planes = acc - dv_base

    ld_marks = []
    st_plan = []
    pc = 0
    for (rows, cols, _, _, planes) in jobs:
        for (k, dst) in planes:
            st_plan.append((wout[pc % 2], dst, rows, cols)); pc += 1

    st_bounds = [0]
    for (_, _, _, _, planes) in jobs:
        st_bounds.append(st_bounds[-1] + len(planes))

    def gp_io(eng):
        def store_job(j):
            for si in range(st_bounds[j], st_bounds[j + 1]):
                wo, dst, rows, cols = st_plan[si]
                eng.wait_ge(s_dv.h, dv_base + si + 1)
                pst.inc(eng.dma_start(out=dst, in_=wo[0:rows, 0:cols]), si)
        for ji, (rows, cols, slab_ap, sc_ap, planes) in enumerate(jobs):
            if ji >= 1:
                store_job(ji - 1)
            pld.inc(eng.dma_start(out=slab[ji % nbuf][0:rows, 0:cols], in_=slab_ap), ji)
            inst = eng.dma_start(out=scl[ji % nbuf][0:rows, 0:cols], in_=sc_ap)
            ld_marks.append(pld.inc(inst, ji))
        store_job(len(jobs) - 1)

    def decoder(eng):
        pcount = 0
        for ji, (rows, cols, slab_ap, sc_ap, planes) in enumerate(jobs):
            pld.wait(eng, ld_marks[ji])
            sl = slab[ji % nbuf][0:rows, 0:cols]
            sc = scl[ji % nbuf][0:rows, 0:cols]
            for (k, dst) in planes:
                wo = wout[pcount % 2][0:rows, 0:cols]
                if pcount >= 2:
                    pst.wait(eng, (pcount % 2, 16 * ((pcount - 2) // 2 + 1)))
                _t1, _bM, _t4, _t5 = (x[0:rows, 0:cols] for x in (t1, bM, t4, t5))
                _s32, _s16, _t6 = (x[0:rows, 0:cols] for x in (s32, s16, t6))
                sh = 6 - 4 * k
                if sh >= 0:
                    eng.tensor_scalar(_t1, sl, sh, 0x1C0, ALU.logical_shift_left, ALU.bitwise_and)
                else:
                    eng.tensor_scalar(_t1, sl, -sh, 0x1C0, ALU.logical_shift_right, ALU.bitwise_and)
                eng.tensor_scalar(_bM, _t1, 0x3F00, None, ALU.add)
                Bv = _bM.bitcast(BF16)
                eng.tensor_scalar(_t4, Bv, 2.0, -1.0, ALU.mult, ALU.add)
                eng.tensor_tensor(_t5, _t4, Bv, ALU.min)
                ssh = 12 - 4 * k
                if ssh >= 0:
                    eng.tensor_scalar(_s32, sl, ssh, 0x8000, ALU.logical_shift_left, ALU.bitwise_and)
                else:
                    eng.tensor_scalar(_s32, sl, -ssh, 0x8000, ALU.logical_shift_right, ALU.bitwise_and)
                eng.tensor_copy(_s16, _s32)
                eng.tensor_tensor(_t6, _t5.bitcast(U16), _s16, ALU.bitwise_or)
                inst = eng.tensor_tensor(wo, _t6.bitcast(BF16), sc, ALU.mult)
                s_dv.inc(inst, 1)
                pcount += 1

    b.sec("gpsimd", gp_io)
    b.sec("vector", decoder)
    return pst.gates()



def emit_ln(b, tag, src, T, xln_d, cst, ready):
    nc = b.nc
    NQ = T // QCW
    px = Pipe(b, "ln_x")
    po = Pipe(b, "ln_o")
    s_c = b.sem("ln_c")
    s_p = b.sem("ln_p")
    s_d = b.sem("ln_d")
    s_l = b.sem("ln_l")

    src_r = src.rearrange("(dt p) t -> p dt t", p=128)
    xln_r = xln_d.rearrange("(dt p) t -> p dt t", p=128)

    with ExitStack() as ctx:
        xc = [ctx.enter_context(nc.sbuf_tensor(f"{tag}_xc{i}", [128, NT * QCW], F32)) for i in range(2)]
        xb = [ctx.enter_context(nc.sbuf_tensor(f"{tag}_xb{i}", [128, NT * QCW], BF16)) for i in range(2)]
        sq = [ctx.enter_context(nc.sbuf_tensor(f"{tag}_sq{i}", [128, NT * QCW], BF16)) for i in range(2)]
        m_t = ctx.enter_context(nc.sbuf_tensor(f"{tag}_m", [1, 2 * QCW], F32))
        v_t = ctx.enter_context(nc.sbuf_tensor(f"{tag}_v", [1, 2 * QCW], F32))
        var_t = ctx.enter_context(nc.sbuf_tensor(f"{tag}_var", [1, 2 * QCW], F32))
        rstd_t = ctx.enter_context(nc.sbuf_tensor(f"{tag}_rs", [1, 2 * QCW], F32))
        mr_t = ctx.enter_context(nc.sbuf_tensor(f"{tag}_mr", [1, 2 * QCW], F32))
        tmp = [ctx.enter_context(nc.sbuf_tensor(f"{tag}_tm{i}", [128, QCW], F32)) for i in range(2)]
        ob = [ctx.enter_context(nc.sbuf_tensor(f"{tag}_ob{i}", [128, QCW], BF16)) for i in range(2)]
        ps_m = ctx.enter_context(nc.psum_tensor(f"{tag}_pm", [128, QCW], F32))
        ps_v = ctx.enter_context(nc.psum_tensor(f"{tag}_pv", [128, QCW], F32))
        ps_b1 = ctx.enter_context(nc.psum_tensor(f"{tag}_pb1", [128, QCW], F32))
        ps_b2 = ctx.enter_context(nc.psum_tensor(f"{tag}_pb2", [128, QCW], F32))

        c_base, p_base, d_base, l_base = s_c.n, s_p.n, s_d.n, s_l.n
        c_done = lambda qc: c_base + qc + 1
        _s1d, _s2d = {}, {}
        _pc = p_base
        _order = [("s1", 0)] + ([("s1", 1)] if NQ > 1 else [])
        for _q in range(NQ):
            _order.append(("s2", _q))
            if _q + 2 < NQ:
                _order.append(("s1", _q + 2))
        for (_k, _q) in _order:
            _pc += 1
            (_s1d if _k == "s1" else _s2d)[_q] = _pc
        s1_done = lambda qc: _s1d[qc]
        s2_done = lambda qc: _s2d[qc]
        v1_done = lambda qc: d_base + (2 + NT) * qc + 1
        v2_done = lambda qc: d_base + (2 + NT) * qc + 2
        v3_done = lambda qc, dt: d_base + (2 + NT) * qc + 2 + dt + 1
        a_done = lambda qc: l_base + qc + 1
        ob_base = [sc.n for sc in po.sems]
        o_mark = lambda gi: (gi % 2, ob_base[gi % 2] + 16 * (gi // 2 + 1))
        x_marks = []

        def sync_io(eng):
            def load(qc):
                inst = eng.dma_start(out=xc[qc % 2][:].rearrange("p (dt c) -> p dt c", dt=NT),
                                     in_=src_r[:, :, qc * QCW:(qc + 1) * QCW])
                x_marks.append(px.inc(inst, qc))
            _waits(eng, ready)
            load(0)
            if NQ > 1:
                load(1)
            for qc in range(NQ):
                for dt in range(NT):
                    gi = qc * NT + dt
                    eng.wait_ge(s_d.h, v3_done(qc, dt))
                    inst = eng.dma_start(out=xln_r[:, dt, qc * QCW:(qc + 1) * QCW],
                                         in_=ob[gi % 2][:])
                    po.inc(inst, gi)
                if qc + 2 < NQ:
                    load(qc + 2)

        def act_sec(eng):
            def casts(qc):
                px.wait(eng, x_marks[qc])
                if qc >= 2:
                    eng.wait_ge(s_p.h, s1_done(qc - 2))
                x = xc[qc % 2]
                for dt in range(NT):
                    sl = slice(dt * QCW, (dt + 1) * QCW)
                    eng.activation(sq[qc % 2][:, sl], x[:, sl], AF.Square)
                inst = None
                for dt in range(NT):
                    sl = slice(dt * QCW, (dt + 1) * QCW)
                    inst = eng.copy(xb[qc % 2][:, sl], x[:, sl])
                s_c.inc(inst, 1)
            casts(0)
            if NQ > 1:
                casts(1)
            for qc in range(NQ):
                eng.wait_ge(s_d.h, v1_done(qc))
                qs = slice((qc % 2) * QCW, (qc % 2 + 1) * QCW)
                eng.activation(rstd_t[:, qs], var_t[:, qs], AF.Ln, bias=cst["eps_ln"])
                inst = eng.activation(rstd_t[:, qs], rstd_t[:, qs], AF.Exp, scale=-0.5)
                s_l.inc(inst, 1)
                if qc + 2 < NQ:
                    casts(qc + 2)

        def pe_sec(eng):
            def s1(qc):
                eng.wait_ge(s_c.h, c_done(qc))
                if qc >= 1:
                    eng.wait_ge(s_d.h, v1_done(qc - 1))
                for dt in range(NT):
                    sl = slice(dt * QCW, (dt + 1) * QCW)
                    nc.tensor.matmul(ps_m[0:1, :], cst["ones_col_bf"], xb[qc % 2][:, sl],
                                     start=(dt == 0), stop=(dt == NT - 1))
                for dt in range(NT):
                    sl = slice(dt * QCW, (dt + 1) * QCW)
                    inst = nc.tensor.matmul(ps_v[0:1, :], cst["ones_col_bf"], sq[qc % 2][:, sl],
                                            start=(dt == 0), stop=(dt == NT - 1))
                s_p.inc(inst, 1)
            s1(0)
            if NQ > 1:
                s1(1)
            for qc in range(NQ):
                eng.wait_ge(s_d.h, v2_done(qc))
                if qc >= 1:
                    eng.wait_ge(s_d.h, v3_done(qc - 1, NT - 1))
                qs = slice((qc % 2) * QCW, (qc % 2 + 1) * QCW)
                nc.tensor.matmul(ps_b1[:], cst["ones_row_f32"], rstd_t[:, qs], start=True, stop=True)
                inst = nc.tensor.matmul(ps_b2[:], cst["ones_row_f32"], mr_t[:, qs], start=True, stop=True)
                s_p.inc(inst, 1)
                if qc + 2 < NQ:
                    s1(qc + 2)

        def dve_sec(eng):
            for qc in range(NQ):
                qs = slice((qc % 2) * QCW, (qc % 2 + 1) * QCW)
                eng.wait_ge(s_p.h, s1_done(qc))
                if qc >= 2:
                    eng.wait_ge(s_p.h, s2_done(qc - 2))
                eng.tensor_scalar(m_t[:, qs], ps_m[0:1, :], 1.0 / D, None, ALU.mult)
                eng.tensor_scalar(v_t[:, qs], ps_v[0:1, :], 1.0 / D, None, ALU.mult)
                eng.tensor_tensor(mr_t[:, qs], m_t[:, qs], m_t[:, qs], ALU.mult)
                inst = eng.tensor_tensor(var_t[:, qs], v_t[:, qs], mr_t[:, qs], ALU.subtract)
                s_d.inc(inst, 1)
                eng.wait_ge(s_l.h, a_done(qc))
                inst = eng.tensor_tensor(mr_t[:, qs], m_t[:, qs], rstd_t[:, qs], ALU.mult)
                s_d.inc(inst, 1)
                eng.wait_ge(s_p.h, s2_done(qc))
                x = xc[qc % 2]
                for dt in range(NT):
                    gi = qc * NT + dt
                    if gi >= 2:
                        po.wait(eng, o_mark(gi - 2))
                    sl = slice(dt * QCW, (dt + 1) * QCW)
                    tm = tmp[dt % 2]
                    eng.tensor_tensor(tm[:], x[:, sl], ps_b1[:], ALU.mult)
                    inst = eng.tensor_tensor(ob[gi % 2][:], tm[:], ps_b2[:], ALU.subtract)
                    s_d.inc(inst, 1)

        b.sec("sync", sync_io)
        b.sec("scalar", act_sec)
        b.sec("tensor", pe_sec)
        b.sec("vector", dve_sec)
    return po.gates()



def emit_fm_proj(b, tag, w_dram, outd, bias_2d, xln_d, T, out_dram, ready,
                 rope=None, cst=None):
    nc = b.nc
    NQ = T // QCW
    NO = outd // 128
    NTIL = NQ * NO
    s_w = b.sem("fp_w")
    px = Pipe(b, "fp_x")
    po = Pipe(b, "fp_o")
    s_mm = b.sem("fp_mm")
    s_sw = b.sem("fp_sw")
    s_e = b.sem("fp_e")
    s_r = b.sem("fp_r")

    xln_r = xln_d.rearrange("(dt p) t -> p dt t", p=128)

    with ExitStack() as ctx:
        w_sb = ctx.enter_context(nc.sbuf_tensor(f"{tag}_wsb", [128, NT * outd], BF16))
        wst = [ctx.enter_context(nc.sbuf_tensor(f"{tag}_wst{i}", [128, NT * 256], F32)) for i in range(2)]
        bias_sb = ctx.enter_context(nc.sbuf_tensor(f"{tag}_bs", [128, max(NO, 1)], F32))
        xcb = [ctx.enter_context(nc.sbuf_tensor(f"{tag}_xc{i}", [128, NT * QCW], BF16)) for i in range(2)]
        ev = [ctx.enter_context(nc.sbuf_tensor(f"{tag}_ev{i}", [128, QCW], BF16)) for i in range(2)]
        ps = [ctx.enter_context(nc.psum_tensor(f"{tag}_ps{i}", [128, QCW], F32)) for i in range(2)]
        if rope is not None:
            rt1 = ctx.enter_context(nc.sbuf_tensor(f"{tag}_rt1", [128, QCW], F32))
            rt2 = ctx.enter_context(nc.sbuf_tensor(f"{tag}_rt2", [128, QCW], F32))
            ro = [ctx.enter_context(nc.sbuf_tensor(f"{tag}_ro{i}", [128, QCW], BF16)) for i in range(2)]
            ps_sw = [ctx.enter_context(nc.psum_tensor(f"{tag}_psw{i}", [128, QCW], F32)) for i in range(2)]

        has_bias = bias_2d is not None
        NWC = outd // 256
        pwst = Pipe(b, "fp_wst")
        wst_marks = []
        w_base_cast = s_w.n
        w_mark = s_w.n + NWC + (16 if has_bias else 0)
        mm_plan = [s_mm.n + i + 1 for i in range(NTIL)]
        sw_plan = [s_sw.n + i + 1 for i in range(NTIL)]
        e_plan = [s_e.n + i + 1 for i in range(NTIL)]
        r_plan = [s_r.n + i + 1 for i in range(NTIL)]
        ob_base = [sc.n for sc in po.sems]
        o_mark = lambda i: (i % 2, ob_base[i % 2] + 16 * (i // 2 + 1))
        last_plan = r_plan if rope is not None else e_plan
        s_last = s_r if rope is not None else s_e
        x_marks = []

        def w_loader(eng):
            _waits(eng, ready)
            if has_bias:
                eng.dma_start(out=bias_sb[:, 0:NO], in_=bias_2d).then_inc(s_w.h, 16)
                s_w.n += 16
            for ch in range(NWC):
                if ch >= 2:
                    eng.wait_ge(s_w.h, w_base_cast + (16 if has_bias else 0) + ch - 1)
                inst = eng.dma_start(out=wst[ch % 2][:].rearrange("p (dt o) -> p dt o", o=256),
                                     in_=w_dram.rearrange("(dt p) o -> p dt o", p=128)[:, :, ch * 256:(ch + 1) * 256])
                wst_marks.append(pwst.inc(inst, ch))

        def w_cast(eng):
            for ch in range(NWC):
                pwst.wait(eng, wst_marks[ch])
                inst = eng.copy(w_sb[:].rearrange("p (dt o) -> p dt o", o=outd)[:, :, ch * 256:(ch + 1) * 256],
                                wst[ch % 2][:].rearrange("p (dt o) -> p dt o", o=256))
                s_w.inc(inst, 1)

        def sync_io(eng):
            def load(qc):
                inst = eng.dma_start(out=xcb[qc % 2][:].rearrange("p (dt c) -> p dt c", dt=NT),
                                     in_=xln_r[:, :, qc * QCW:(qc + 1) * QCW])
                x_marks.append(px.inc(inst, qc))
            _waits(eng, ready)
            load(0)
            if NQ > 1:
                load(1)
            idx = 0
            for qc in range(NQ):
                for ot in range(NO):
                    src = (ro[idx % 2] if rope is not None else ev[idx % 2])[:]
                    eng.wait_ge(s_last.h, last_plan[idx])
                    inst = eng.dma_start(out=out_dram[ot][:, qc * QCW:(qc + 1) * QCW], in_=src)
                    po.inc(inst, idx)
                    idx += 1
                if qc + 2 < NQ:
                    load(qc + 2)

        def mms(eng):
            eng.wait_ge(s_w.h, w_mark)
            idx = 0
            for qc in range(NQ):
                px.wait(eng, x_marks[qc])
                for ot in range(NO):
                    if idx >= 2:
                        eng.wait_ge(s_e.h, e_plan[idx - 2])
                    p = ps[idx % 2]
                    for dt in range(NT):
                        inst = nc.tensor.matmul(
                            p[:], w_sb[:, dt * outd + ot * 128: dt * outd + (ot + 1) * 128],
                            xcb[qc % 2][:, dt * QCW:(dt + 1) * QCW],
                            start=(dt == 0), stop=(dt == NT - 1))
                    s_mm.inc(inst, 1)
                    if rope is not None and idx >= 1:
                        j = idx - 1
                        eng.wait_ge(s_e.h, e_plan[j])
                        if j >= 2:
                            eng.wait_ge(s_r.h, r_plan[j - 2])
                        inst = nc.tensor.matmul(ps_sw[j % 2][:], rope["pswap"], ev[j % 2][:],
                                                start=True, stop=True)
                        s_sw.inc(inst, 1)
                    idx += 1
            if rope is not None:
                j = NTIL - 1
                eng.wait_ge(s_e.h, e_plan[j])
                if j >= 2:
                    eng.wait_ge(s_r.h, r_plan[j - 2])
                inst = nc.tensor.matmul(ps_sw[j % 2][:], rope["pswap"], ev[j % 2][:],
                                        start=True, stop=True)
                s_sw.inc(inst, 1)

        def evac(eng):
            idx = 0
            for qc in range(NQ):
                for ot in range(NO):
                    eng.wait_ge(s_mm.h, mm_plan[idx])
                    if idx >= 2:
                        eng.wait_ge(s_last.h, last_plan[idx - 2])
                        if rope is None:
                            po.wait(eng, o_mark(idx - 2))
                    bias = bias_sb[:, ot:ot + 1] if has_bias else 0.0
                    fn = AF.Identity if has_bias else AF.Copy
                    inst = eng.activation(ev[idx % 2][:], ps[idx % 2][:], fn, bias=bias)
                    s_e.inc(inst, 1)
                    idx += 1

        def rope_dve(eng):
            idx = 0
            for qc in range(NQ):
                qs = slice(qc * QCW, (qc + 1) * QCW)
                for ot in range(NO):
                    eng.wait_ge(s_sw.h, sw_plan[idx])
                    if idx >= 2:
                        po.wait(eng, o_mark(idx - 2))
                    eng.tensor_tensor(rt1[:], ps_sw[idx % 2][:], rope["sin"][:, qs], ALU.mult)
                    eng.tensor_tensor(rt2[:], ev[idx % 2][:], rope["cos"][:, qs], ALU.mult)
                    inst = eng.tensor_tensor(ro[idx % 2][:], rt1[:], rt2[:], ALU.add)
                    s_r.inc(inst, 1)
                    idx += 1

        b.sec("sync", w_loader)
        b.sec("scalar", w_cast)
        b.sec("sync", sync_io)
        b.sec("tensor", mms)
        b.sec("scalar", evac)
        if rope is not None:
            b.sec("vector", rope_dve)
    return po.gates()



def emit_v_proj(b, tag, wv_dram, outd, bv_row, xln_d, T, v_dram, ready, cst):
    nc = b.nc
    NQ = T // QCW
    NOC = outd // QCW
    NTIL = NQ * 4 * NOC
    s_w = b.sem("vp_w")
    px = Pipe(b, "vp_x")
    po = Pipe(b, "vp_o")
    s_mm = b.sem("vp_mm")
    s_e = b.sem("vp_e")
    s_bc = b.sem("vp_bc")

    xln_r = xln_d.rearrange("(dt p) t -> p dt t", p=128)

    with ExitStack() as ctx:
        w_sb = ctx.enter_context(nc.sbuf_tensor(f"{tag}_wsb", [128, NT * outd], BF16))
        wst = [ctx.enter_context(nc.sbuf_tensor(f"{tag}_wst{i}", [128, NT * 256], F32)) for i in range(2)]
        xcb = [ctx.enter_context(nc.sbuf_tensor(f"{tag}_xc{i}", [128, NT * QCW], BF16)) for i in range(2)]
        ev = [ctx.enter_context(nc.sbuf_tensor(f"{tag}_ev{i}", [128, QCW], BF16)) for i in range(2)]
        bvb = ctx.enter_context(nc.sbuf_tensor(f"{tag}_bvb", [128, max(outd, QCW)], F32))
        bvrow_sb = ctx.enter_context(nc.sbuf_tensor(f"{tag}_bvr", [1, max(outd, QCW)], F32))
        ps = [ctx.enter_context(nc.psum_tensor(f"{tag}_ps{i}", [128, QCW], F32)) for i in range(2)]
        ps_b = ctx.enter_context(nc.psum_tensor(f"{tag}_psb", [128, QCW], F32))

        has_bias = bv_row is not None
        NWC = outd // 256
        pwst = Pipe(b, "vp_wst")
        wst_marks = []
        w_base_cast = s_w.n
        w_mark = s_w.n + NWC + (16 if has_bias else 0)
        bc_base = s_bc.n
        mm_base = s_mm.n + (NOC if has_bias else 0)
        mm_plan = [mm_base + i + 1 for i in range(NTIL)]
        e_plan = [s_e.n + i + 1 for i in range(NTIL)]
        ob_base = [sc.n for sc in po.sems]
        o_mark = lambda i: (i % 2, ob_base[i % 2] + 16 * 4 * (i // 2 + 1))
        x_marks = []

        def w_loader(eng):
            _waits(eng, ready)
            if has_bias:
                eng.dma_start(out=bvrow_sb[:, 0:outd], in_=bv_row).then_inc(s_w.h, 16)
                s_w.n += 16
            for ch in range(NWC):
                if ch >= 2:
                    eng.wait_ge(s_w.h, w_base_cast + (16 if has_bias else 0) + ch - 1)
                inst = eng.dma_start(out=wst[ch % 2][:].rearrange("p (dt o) -> p dt o", o=256),
                                     in_=wv_dram.rearrange("(dt p) o -> p dt o", p=128)[:, :, ch * 256:(ch + 1) * 256])
                wst_marks.append(pwst.inc(inst, ch))

        def w_cast(eng):
            for ch in range(NWC):
                pwst.wait(eng, wst_marks[ch])
                inst = eng.copy(w_sb[:].rearrange("p (dt o) -> p dt o", o=outd)[:, :, ch * 256:(ch + 1) * 256],
                                wst[ch % 2][:].rearrange("p (dt o) -> p dt o", o=256))
                s_w.inc(inst, 1)

        def sync_io(eng):
            def load(qc):
                inst = eng.dma_start(out=xcb[qc % 2][:].rearrange("p (dt c) -> p dt c", dt=NT),
                                     in_=xln_r[:, :, qc * QCW:(qc + 1) * QCW])
                x_marks.append(px.inc(inst, qc))
            _waits(eng, ready)
            load(0)
            if NQ > 1:
                load(1)
            idx = 0
            for qc in range(NQ):
                for tt in range(4):
                    for oc in range(NOC):
                        eng.wait_ge(s_e.h, e_plan[idx])
                        for j in range(4):
                            hh = oc * 4 + j
                            inst = eng.dma_start(out=v_dram[hh][qc * QCW + tt * 128: qc * QCW + (tt + 1) * 128, :],
                                                 in_=ev[idx % 2][:, j * 128:(j + 1) * 128])
                            po.inc(inst, idx)
                        idx += 1
                if qc + 2 < NQ:
                    load(qc + 2)

        def mms(eng):
            eng.wait_ge(s_w.h, w_mark)
            if has_bias:
                for oc in range(NOC):
                    if oc >= 1:
                        eng.wait_ge(s_bc.h, bc_base + oc)
                    inst = nc.tensor.matmul(ps_b[:], cst["ones_row_f32"],
                                            bvrow_sb[:, oc * QCW:(oc + 1) * QCW], start=True, stop=True)
                    s_mm.inc(inst, 1)
                eng.wait_ge(s_bc.h, bc_base + NOC)
            idx = 0
            for qc in range(NQ):
                px.wait(eng, x_marks[qc])
                for tt in range(4):
                    for oc in range(NOC):
                        if idx >= 2:
                            eng.wait_ge(s_e.h, e_plan[idx - 2])
                        p = ps[idx % 2]
                        for dt in range(NT):
                            inst = nc.tensor.matmul(
                                p[:], xcb[qc % 2][:, dt * QCW + tt * 128: dt * QCW + (tt + 1) * 128],
                                w_sb[:, dt * outd + oc * QCW: dt * outd + (oc + 1) * QCW],
                                start=(dt == 0), stop=(dt == NT - 1))
                        s_mm.inc(inst, 1)
                        idx += 1

        def bc_evac(eng):
            if not has_bias:
                return
            for oc in range(NOC):
                eng.wait_ge(s_mm.h, mm_base - NOC + oc + 1)
                inst = eng.copy(bvb[:, oc * QCW:(oc + 1) * QCW], ps_b[:])
                s_bc.inc(inst, 1)

        def evac(eng):
            idx = 0
            for qc in range(NQ):
                for tt in range(4):
                    for oc in range(NOC):
                        eng.wait_ge(s_mm.h, mm_plan[idx])
                        if idx >= 2:
                            po.wait(eng, o_mark(idx - 2))
                        if has_bias:
                            inst = eng.scalar_tensor_tensor(ev[idx % 2][:], ps[idx % 2][:], 1.0,
                                                            bvb[:, oc * QCW:(oc + 1) * QCW],
                                                            ALU.mult, ALU.add)
                        else:
                            inst = eng.tensor_copy(ev[idx % 2][:], ps[idx % 2][:])
                        s_e.inc(inst, 1)
                        idx += 1

        b.sec("sync", w_loader)
        b.sec("scalar", w_cast)
        b.sec("scalar", bc_evac)
        b.sec("sync", sync_io)
        b.sec("tensor", mms)
        b.sec("vector", evac)
    return po.gates()



def emit_attention(b, tag, q_dram, k_dram, v_dram, kv_of_head, TQ, TK, scale,
                   shift, pf_dram, nheads, ready, cst):
    nc = b.nc
    NQ = TQ // QCW
    NKT = TK // 128
    pld = Pipe(b, "at_ld")
    po = Pipe(b, "at_o")
    s_s = b.sem("at_s")
    s_p = b.sem("at_p")
    s_e = b.sem("at_e")
    s_d = b.sem("at_d")
    s_bc = b.sem("at_bc")
    s_e2 = b.sem("at_e2")

    with ExitStack() as ctx:
        NSL = 5
        k_sb = [ctx.enter_context(nc.sbuf_tensor(f"{tag}_k{i}", [128, TK], BF16)) for i in range(2)]
        v_sb = [ctx.enter_context(nc.sbuf_tensor(f"{tag}_v{i}", [128, TK], BF16)) for i in range(2)]
        q_sb = [ctx.enter_context(nc.sbuf_tensor(f"{tag}_q{i}", [128, TQ], BF16)) for i in range(2)]
        a_all = ctx.enter_context(nc.sbuf_tensor(f"{tag}_a", [128, NSL * QCW], BF16))
        a_sb = [a_all[:, i * QCW:(i + 1) * QCW] for i in range(NSL)]
        r_sb = ctx.enter_context(nc.sbuf_tensor(f"{tag}_r", [1, QCW], F32))
        rb_sb = ctx.enter_context(nc.sbuf_tensor(f"{tag}_rb", [128, QCW], F32))
        pf_sb = [ctx.enter_context(nc.sbuf_tensor(f"{tag}_pf{i}", [128, QCW], BF16)) for i in range(2)]
        ps_s = ctx.enter_context(nc.psum_tensor(f"{tag}_pss", [128, NSL * QCW], F32))
        ps_av = ctx.enter_context(nc.psum_tensor(f"{tag}_pav", [128, QCW], F32))
        ps_sm = ctx.enter_context(nc.psum_tensor(f"{tag}_psm", [128, QCW], F32))
        ps_bc = ctx.enter_context(nc.psum_tensor(f"{tag}_pbc", [128, QCW], F32))

        s_plan, p_plan, d_plan, bc_plan, e2_plan, o_plan = {}, {}, {}, {}, {}, {}
        cs, cp, cd, cb, ce2 = s_s.n, s_p.n, s_d.n, s_bc.n, s_e2.n
        e_base = s_e.n
        e_done = lambda gi: e_base + gi + 1
        ob_base = [sc.n for sc in po.sems]
        for h in range(nheads):
            for qc in range(NQ):
                gi = h * NQ + qc
                for kt in range(NKT):
                    cs += 1; s_plan[(h, qc, kt)] = cs
                    cp += 1; p_plan[(h, qc, kt)] = cp
                cd += 1; d_plan[(h, qc, "r")] = cd
                cb += 1; bc_plan[(h, qc)] = cb
                ce2 += 1; e2_plan[(h, qc)] = ce2
                cd += 1; d_plan[(h, qc, "pf")] = cd
                o_plan[(h, qc)] = (gi % 2, ob_base[gi % 2] + 16 * (gi // 2 + 1))
        ld_marks = []

        def g2t(gi):
            return (gi // (NQ * NKT), (gi % (NQ * NKT)) // NKT, gi % NKT)

        def sync_io(eng):
            def load(h):
                kv = kv_of_head(h)
                pld.inc(eng.dma_start(out=k_sb[h % 2][:], in_=k_dram[kv]), h)
                pld.inc(eng.dma_start(out=v_sb[h % 2][:].rearrange("p (kt c) -> p kt c", kt=NKT),
                                      in_=v_dram[kv].rearrange("(kt p) c -> p kt c", p=128)), h)
                inst = eng.dma_start(out=q_sb[h % 2][:], in_=q_dram[h])
                ld_marks.append(pld.inc(inst, h))
            _waits(eng, ready)
            load(0)
            if nheads > 1:
                load(1)
            for h in range(nheads):
                for qc in range(NQ):
                    gi = h * NQ + qc
                    eng.wait_ge(s_d.h, d_plan[(h, qc, "pf")])
                    inst = eng.dma_start(out=pf_dram[h][:, qc * QCW:(qc + 1) * QCW],
                                         in_=pf_sb[gi % 2][:])
                    po.inc(inst, gi)
                if h + 2 < nheads:
                    load(h + 2)

        def pe(eng):
            nun = nheads * NQ

            def s_mm(gi):
                hh, qq, kk = g2t(gi)
                if hh * NQ + qq > 0 and kk == 0:
                    pld.wait(eng, ld_marks[hh])
                if gi >= NSL:
                    eng.wait_ge(s_e.h, e_done(gi - NSL))
                inst = nc.tensor.matmul(ps_s[:, (gi % NSL) * QCW:(gi % NSL + 1) * QCW],
                                        k_sb[hh % 2][:, kk * 128:(kk + 1) * 128],
                                        q_sb[hh % 2][:, qq * QCW:(qq + 1) * QCW],
                                        start=True, stop=True)
                s_s.inc(inst, 1)

            pld.wait(eng, ld_marks[0])
            s_mm(0)
            s_mm(1)
            for u in range(nun):
                h, qc = u // NQ, u % NQ
                if u > 0:
                    prev = ((u - 1) // NQ, (u - 1) % NQ)
                    eng.wait_ge(s_d.h, d_plan[(prev[0], prev[1], "pf")])
                for kt in range(NKT):
                    gi = u * NKT + kt
                    if gi + 2 < nun * NKT:
                        s_mm(gi + 2)
                    eng.wait_ge(s_e.h, e_done(gi))
                    nc.tensor.matmul(ps_av[:], v_sb[h % 2][:, kt * 128:(kt + 1) * 128],
                                     a_sb[gi % NSL], start=(kt == 0), stop=(kt == NKT - 1))
                    inst = nc.tensor.matmul(ps_sm[0:1, :], cst["ones_col_bf"], a_sb[gi % NSL],
                                            start=(kt == 0), stop=(kt == NKT - 1))
                    s_p.inc(inst, 1)
                eng.wait_ge(s_d.h, d_plan[(h, qc, "r")])
                inst = nc.tensor.matmul(ps_bc[:], cst["ones_row_f32"], r_sb[:], start=True, stop=True)
                s_bc.inc(inst, 1)

        def act(eng):
            for h in range(nheads):
                for qc in range(NQ):
                    for kt in range(NKT):
                        gi = (h * NQ + qc) * NKT + kt
                        sl = (gi % NSL) * QCW
                        eng.wait_ge(s_s.h, s_plan[(h, qc, kt)])
                        if gi >= NSL:
                            eng.wait_ge(s_p.h, p_plan[g2t(gi - NSL)])
                        ebias = cst["nshift"] if shift else 0.0
                        inst = eng.activation(a_all[:, sl:sl + QCW], ps_s[:, sl:sl + QCW],
                                              AF.Exp, scale=scale, bias=ebias)
                        s_e.inc(inst, 1)
                    eng.wait_ge(s_bc.h, bc_plan[(h, qc)])
                    inst = eng.copy(rb_sb[:], ps_bc[:])
                    s_e2.inc(inst, 1)

        def dve(eng):
            for h in range(nheads):
                for qc in range(NQ):
                    eng.wait_ge(s_p.h, p_plan[(h, qc, NKT - 1)])
                    inst = eng.reciprocal(r_sb[:], ps_sm[0:1, :])
                    s_d.inc(inst, 1)
                    eng.wait_ge(s_e2.h, e2_plan[(h, qc)])
                    gi = h * NQ + qc
                    if gi >= 2:
                        pv = gi - 2
                        po.wait(eng, o_plan[(pv // NQ, pv % NQ)])
                    inst = eng.tensor_tensor(pf_sb[(h * NQ + qc) % 2][:], ps_av[:], rb_sb[:], ALU.mult)
                    s_d.inc(inst, 1)

        b.sec("sync", sync_io)
        b.sec("tensor", pe)
        b.sec("scalar", act)
        b.sec("vector", dve)
    return po.gates()



def emit_out_proj(b, tag, wo_dram, bo_2d, pf_dram, nheads, resid, T, dst,
                  ready, cst):
    nc = b.nc
    NQ = T // QCW
    NTIL = NQ * NT
    s_w = b.sem("op_w")
    px = Pipe(b, "op_x")
    po = Pipe(b, "op_o")
    s_mm = b.sem("op_mm")
    s_e = b.sem("op_e")
    s_d = b.sem("op_d")

    resid_r = resid.rearrange("(dt p) t -> p dt t", p=128)
    dst_r = dst.rearrange("(dt p) t -> p dt t", p=128)

    with ExitStack() as ctx:
        wo_sb = ctx.enter_context(nc.sbuf_tensor(f"{tag}_wo", [128, nheads * D], BF16))
        wst = [ctx.enter_context(nc.sbuf_tensor(f"{tag}_wst{i}", [128, nheads * 64], F32)) for i in range(2)]
        bo_sb = ctx.enter_context(nc.sbuf_tensor(f"{tag}_bo", [128, NT], F32))
        pf_sb = [ctx.enter_context(nc.sbuf_tensor(f"{tag}_pf{i}", [128, nheads * QCW], BF16)) for i in range(2)]
        xr_sb = [ctx.enter_context(nc.sbuf_tensor(f"{tag}_xr{i}", [128, NT * QCW], F32)) for i in range(2)]
        tv = [ctx.enter_context(nc.sbuf_tensor(f"{tag}_tv{i}", [128, QCW], F32)) for i in range(2)]
        ov = [ctx.enter_context(nc.sbuf_tensor(f"{tag}_ov{i}", [128, QCW], F32)) for i in range(2)]
        ps = [ctx.enter_context(nc.psum_tensor(f"{tag}_ps{i}", [128, QCW], F32)) for i in range(2)]

        has_bias = bo_2d is not None
        NWC = D // 64
        pwst = Pipe(b, "op_wst")
        wst_marks = []
        w_base_cast = s_w.n
        w_mark = s_w.n + NWC + (16 if has_bias else 0)
        mm_plan = [s_mm.n + i + 1 for i in range(NTIL)]
        e_plan = [s_e.n + i + 1 for i in range(NTIL)]
        d_plan = [s_d.n + i + 1 for i in range(NTIL)]
        ob_base = [sc.n for sc in po.sems]
        o_mark = lambda i: (i % 2, ob_base[i % 2] + 16 * (i // 2 + 1))
        x_marks = []

        def w_loader(eng):
            _waits(eng, ready)
            if has_bias:
                eng.dma_start(out=bo_sb[:], in_=bo_2d).then_inc(s_w.h, 16)
                s_w.n += 16
            for ch in range(NWC):
                if ch >= 2:
                    eng.wait_ge(s_w.h, w_base_cast + (16 if has_bias else 0) + ch - 1)
                inst = eng.dma_start(out=wst[ch % 2][:].rearrange("p (hh o) -> p hh o", o=64),
                                     in_=wo_dram.rearrange("(hh p) o -> p hh o", p=128)[:, :, ch * 64:(ch + 1) * 64])
                wst_marks.append(pwst.inc(inst, ch))

        def w_cast(eng):
            for ch in range(NWC):
                pwst.wait(eng, wst_marks[ch])
                inst = eng.copy(wo_sb[:].rearrange("p (hh o) -> p hh o", o=D)[:, :, ch * 64:(ch + 1) * 64],
                                wst[ch % 2][:].rearrange("p (hh o) -> p hh o", o=64))
                s_w.inc(inst, 1)

        def sync_io(eng):
            def load(qc):
                px.inc(eng.dma_start(out=pf_sb[qc % 2][:].rearrange("p (hh c) -> p hh c", hh=nheads),
                                     in_=pf_dram.rearrange("hh p t -> p hh t")[:, :, qc * QCW:(qc + 1) * QCW]), qc)
                inst = eng.dma_start(out=xr_sb[qc % 2][:].rearrange("p (dt c) -> p dt c", dt=NT),
                                     in_=resid_r[:, :, qc * QCW:(qc + 1) * QCW])
                x_marks.append(px.inc(inst, qc))
            _waits(eng, ready)
            load(0)
            if NQ > 1:
                load(1)
            idx = 0
            for qc in range(NQ):
                for dt in range(NT):
                    eng.wait_ge(s_d.h, d_plan[idx])
                    inst = eng.dma_start(out=dst_r[:, dt, qc * QCW:(qc + 1) * QCW], in_=ov[idx % 2][:])
                    po.inc(inst, idx)
                    idx += 1
                if qc + 2 < NQ:
                    load(qc + 2)

        def mms(eng):
            eng.wait_ge(s_w.h, w_mark)
            idx = 0
            for qc in range(NQ):
                px.wait(eng, x_marks[qc])
                for dt in range(NT):
                    if idx >= 2:
                        eng.wait_ge(s_e.h, e_plan[idx - 2])
                    p = ps[idx % 2]
                    for hh in range(nheads):
                        inst = nc.tensor.matmul(
                            p[:], wo_sb[:, hh * D + dt * 128: hh * D + (dt + 1) * 128],
                            pf_sb[qc % 2][:, hh * QCW:(hh + 1) * QCW],
                            start=(hh == 0), stop=(hh == nheads - 1))
                    s_mm.inc(inst, 1)
                    idx += 1

        def evac(eng):
            idx = 0
            for qc in range(NQ):
                for dt in range(NT):
                    eng.wait_ge(s_mm.h, mm_plan[idx])
                    if idx >= 2:
                        eng.wait_ge(s_d.h, d_plan[idx - 2])
                    bias = bo_sb[:, dt:dt + 1] if has_bias else 0.0
                    fn = AF.Identity if has_bias else AF.Copy
                    inst = eng.activation(tv[idx % 2][:], ps[idx % 2][:], fn, bias=bias)
                    s_e.inc(inst, 1)
                    idx += 1

        def resid_dve(eng):
            idx = 0
            for qc in range(NQ):
                for dt in range(NT):
                    eng.wait_ge(s_e.h, e_plan[idx])
                    if idx >= 2:
                        po.wait(eng, o_mark(idx - 2))
                    inst = eng.tensor_tensor(ov[idx % 2][:], tv[idx % 2][:],
                                             xr_sb[qc % 2][:, dt * QCW:(dt + 1) * QCW], ALU.add)
                    s_d.inc(inst, 1)
                    idx += 1

        b.sec("sync", w_loader)
        b.sec("scalar", w_cast)
        b.sec("sync", sync_io)
        b.sec("tensor", mms)
        b.sec("scalar", evac)
        b.sec("vector", resid_dve)
    return po.gates()



def emit_gate_up(b, tag, wgu_d, xln_d, T, hid, ready, w_ready):
    nc = b.nc
    NQ = T // QCW
    NP = I_FFN // 128
    GC = 2
    NCH = NP // GC
    NTIL = NP * NQ
    pw = Pipe(b, "gu_w")
    s_x = b.sem("gu_x")
    po = Pipe(b, "gu_o")
    s_mm = b.sem("gu_mm")
    s_si = b.sem("gu_si")
    s_h = b.sem("gu_h")

    xln_r = xln_d.rearrange("(dt p) t -> p dt t", p=128)
    wgu_r = wgu_d.rearrange("(kt p) o -> p kt o", p=128)

    with ExitStack() as ctx:
        xln_sb = ctx.enter_context(nc.sbuf_tensor(f"{tag}_xln", [128, NT * T], BF16))
        wg = [ctx.enter_context(nc.sbuf_tensor(f"{tag}_wg{i}", [128, NT * 128 * GC], BF16)) for i in range(2)]
        wu = [ctx.enter_context(nc.sbuf_tensor(f"{tag}_wu{i}", [128, NT * 128 * GC], BF16)) for i in range(2)]
        sil = [ctx.enter_context(nc.sbuf_tensor(f"{tag}_si{i}", [128, QCW], F32)) for i in range(2)]
        hv = [ctx.enter_context(nc.sbuf_tensor(f"{tag}_hv{i}", [128, QCW], BF16)) for i in range(2)]
        ps_g = [ctx.enter_context(nc.psum_tensor(f"{tag}_pg{i}", [128, QCW], F32)) for i in range(2)]
        ps_u = [ctx.enter_context(nc.psum_tensor(f"{tag}_pu{i}", [128, QCW], F32)) for i in range(2)]

        x_mark = s_x.n + 16
        mm_plan = [s_mm.n + i + 1 for i in range(NTIL)]
        si_plan = [s_si.n + i + 1 for i in range(NTIL)]
        h_plan = [s_h.n + i + 1 for i in range(NTIL)]
        ob_base = [sc.n for sc in po.sems]
        o_mark = lambda i: (i % 2, ob_base[i % 2] + 16 * (i // 2 + 1))
        w_marks = []

        def sync_io(eng):
            _waits(eng, ready)
            eng.dma_start(out=xln_sb[:].rearrange("p (dt t) -> p dt t", t=T),
                          in_=xln_r).then_inc(s_x.h, 16)
            s_x.n += 16
            idx = 0
            for ch in range(NCH):
                for j in range(GC):
                    g = ch * GC + j
                    for qc in range(NQ):
                        eng.wait_ge(s_h.h, h_plan[idx])
                        inst = eng.dma_start(out=hid[g * 128:(g + 1) * 128, qc * QCW:(qc + 1) * QCW],
                                             in_=hv[idx % 2][:])
                        po.inc(inst, idx)
                        idx += 1

        def w_loader(eng):
            _waits(eng, w_ready)
            _waits(eng, ready)
            for ch in range(NCH):
                if ch >= 2:
                    eng.wait_ge(s_mm.h, mm_plan[(ch - 1) * GC * NQ - 1])
                cs = slice(ch * GC * 128, (ch + 1) * GC * 128)
                us = slice(I_FFN + ch * GC * 128, I_FFN + (ch + 1) * GC * 128)
                pw.inc(eng.dma_start(out=wg[ch % 2][:].rearrange("p (kt o) -> p kt o", kt=NT),
                                     in_=wgu_r[:, :, cs]), ch)
                inst = eng.dma_start(out=wu[ch % 2][:].rearrange("p (kt o) -> p kt o", kt=NT),
                                     in_=wgu_r[:, :, us])
                w_marks.append(pw.inc(inst, ch))

        def mms(eng):
            eng.wait_ge(s_x.h, x_mark)
            idx = 0
            for ch in range(NCH):
                pw.wait(eng, w_marks[ch])
                for j in range(GC):
                    for qc in range(NQ):
                        if idx >= 2:
                            eng.wait_ge(s_h.h, h_plan[idx - 2])
                            eng.wait_ge(s_si.h, si_plan[idx - 2])
                        pg, pu = ps_g[idx % 2], ps_u[idx % 2]
                        for kt in range(NT):
                            nc.tensor.matmul(pg[:], wg[ch % 2][:, kt * GC * 128 + j * 128: kt * GC * 128 + (j + 1) * 128],
                                             xln_sb[:, kt * T + qc * QCW: kt * T + (qc + 1) * QCW],
                                             start=(kt == 0), stop=(kt == NT - 1))
                        for kt in range(NT):
                            inst = nc.tensor.matmul(pu[:], wu[ch % 2][:, kt * GC * 128 + j * 128: kt * GC * 128 + (j + 1) * 128],
                                                    xln_sb[:, kt * T + qc * QCW: kt * T + (qc + 1) * QCW],
                                                    start=(kt == 0), stop=(kt == NT - 1))
                        s_mm.inc(inst, 1)
                        idx += 1

        def silu_act(eng):
            for idx in range(NTIL):
                eng.wait_ge(s_mm.h, mm_plan[idx])
                if idx >= 2:
                    eng.wait_ge(s_h.h, h_plan[idx - 2])
                inst = eng.activation(sil[idx % 2][:], ps_g[idx % 2][:], AF.Silu)
                s_si.inc(inst, 1)

        def mult_dve(eng):
            for idx in range(NTIL):
                eng.wait_ge(s_si.h, si_plan[idx])
                if idx >= 2:
                    po.wait(eng, o_mark(idx - 2))
                inst = eng.tensor_tensor(hv[idx % 2][:], sil[idx % 2][:], ps_u[idx % 2][:], ALU.mult)
                s_h.inc(inst, 1)

        b.sec("sync", sync_io)
        b.sec("gpsimd", w_loader)
        b.sec("tensor", mms)
        b.sec("scalar", silu_act)
        b.sec("vector", mult_dve)
    return po.gates()



def emit_down(b, tag, wdn_d, hid, T, resid, dst, hid_ready, w_ready, cst):
    nc = b.nc
    NQ = T // QCW
    NKT = I_FFN // 128
    NTIL = NQ * NT
    pw = Pipe(b, "dn_w")
    s_h = b.sem("dn_h")
    po = Pipe(b, "dn_o")
    s_mm = b.sem("dn_mm")
    s_e = b.sem("dn_e")
    s_q = b.sem("dn_q")
    s_ms = b.sem("dn_ms")
    s_r1 = b.sem("dn_r1")
    s_r2 = b.sem("dn_r2")
    s_bc = b.sem("dn_bc")
    s_f = b.sem("dn_f")

    resid_r = resid.rearrange("(dt p) t -> p dt t", p=128)
    dst_r = dst.rearrange("(dt p) t -> p dt t", p=128)

    def hrow(t):
        return t, 0, 128

    with ExitStack() as ctx:
        WCH = 256
        NW = D // WCH
        h_sb = ctx.enter_context(nc.sbuf_tensor(f"{tag}_hs", [128, 44 * QCW], BF16))
        w_sb = [ctx.enter_context(nc.sbuf_tensor(f"{tag}_ws{i}", [128, 44 * WCH], BF16)) for i in range(2)]
        y_sb = ctx.enter_context(nc.sbuf_tensor(f"{tag}_y", [128, NT * QCW], F32))
        sqv = [ctx.enter_context(nc.sbuf_tensor(f"{tag}_sq{i}", [128, QCW], BF16)) for i in range(2)]
        ms_t = ctx.enter_context(nc.sbuf_tensor(f"{tag}_mst", [1, 2 * QCW], F32))
        rr_t = ctx.enter_context(nc.sbuf_tensor(f"{tag}_rrt", [1, 2 * QCW], F32))
        xr_sb = [ctx.enter_context(nc.sbuf_tensor(f"{tag}_xr{i}", [128, QCW], F32)) for i in range(2)]
        tv = [ctx.enter_context(nc.sbuf_tensor(f"{tag}_tv{i}", [128, QCW], F32)) for i in range(2)]
        ov = [ctx.enter_context(nc.sbuf_tensor(f"{tag}_ov{i}", [128, QCW], F32)) for i in range(2)]
        ps_y = [ctx.enter_context(nc.psum_tensor(f"{tag}_py{i}", [128, QCW], F32)) for i in range(2)]
        ps_ms = ctx.enter_context(nc.psum_tensor(f"{tag}_pms", [128, QCW], F32))
        ps_bc = ctx.enter_context(nc.psum_tensor(f"{tag}_pbc", [128, QCW], F32))

        h_base = s_h.n
        mm_plan = [s_mm.n + i + 1 for i in range(NTIL)]
        e_plan = [s_e.n + i + 1 for i in range(NTIL)]
        q_plan = [s_q.n + i + 1 for i in range(NTIL)]
        ms_plan = [s_ms.n + i + 1 for i in range(NTIL)]
        f_plan = [s_f.n + i + 1 for i in range(NTIL)]
        ob_base = [sc.n for sc in po.sems]
        o_mark = lambda i: (i % 2, ob_base[i % 2] + 16 * (i // 2 + 1))
        r1_base, r2_base = s_r1.n, s_r2.n
        r1_done = lambda qc: r1_base + qc + 1
        r2_done = lambda qc: r2_base + qc + 1
        bc_plan = [s_bc.n + qc + 1 for qc in range(NQ)]
        w_marks = []

        pxr = Pipe(b, "dn_xrl")
        xrb_base = [sc.n for sc in pxr.sems]
        xr_mark = lambda i: (i % 2, xrb_base[i % 2] + 16 * (i // 2 + 1))

        def sync_io(eng):
            def load(qc):
                eng.dma_start(out=h_sb[:].rearrange("p (blk c) -> p blk c", blk=44),
                              in_=hid.rearrange("(blk p) t -> p blk t", p=128)[:, :, qc * QCW:(qc + 1) * QCW]
                              ).then_inc(s_h.h, 16)
                s_h.n += 16

            def load_w(idx):
                if idx >= 2:
                    eng.wait_ge(s_mm.h, mm_plan[(idx - 1) * 2 - 1])
                dtc = idx % NW
                cs = slice(dtc * WCH, (dtc + 1) * WCH)
                inst = eng.dma_start(out=w_sb[idx % 2][:].rearrange("p (blk c) -> p blk c", blk=44),
                                     in_=wdn_d.rearrange("(blk p) o -> p blk o", p=128)[:, :, cs])
                w_marks.append(pw.inc(inst, idx))

            def load_xr(gi):
                qc, dt = gi // NT, gi % NT
                pxr.inc(eng.dma_start(out=xr_sb[gi % 2][:],
                                      in_=resid_r[:, dt, qc * QCW:(qc + 1) * QCW]), gi)
            _waits(eng, hid_ready)
            _waits(eng, w_ready)
            load(0)
            if DOWN_LITE != 2:
                load_xr(0)
                load_xr(1)
            idx = 0
            for qc in range(NQ):
                for dtc in range(NW):
                    load_w(qc * NW + dtc)
                for dt in range(NT):
                    if DOWN_LITE == 2:
                        eng.wait_ge(s_e.h, e_plan[idx])
                        inst = eng.dma_start(out=dst_r[:, dt, qc * QCW:(qc + 1) * QCW],
                                             in_=y_sb[:, dt * QCW:(dt + 1) * QCW])
                        po.inc(inst, idx)
                        idx += 1
                        continue
                    eng.wait_ge(s_f.h, f_plan[idx])
                    inst = eng.dma_start(out=dst_r[:, dt, qc * QCW:(qc + 1) * QCW], in_=ov[idx % 2][:])
                    po.inc(inst, idx)
                    if idx + 2 < NQ * NT:
                        load_xr(idx + 2)
                    idx += 1
                if qc + 1 < NQ:
                    load(qc + 1)

        def pe_sec(eng):
            def ms_mm(qc, dt):
                eng.wait_ge(s_q.h, q_plan[qc * NT + dt])
                if dt == 0 and qc >= 1:
                    eng.wait_ge(s_r1.h, r1_done(qc - 1))
                inst = nc.tensor.matmul(ps_ms[0:1, :], cst["ones_col_bf"], sqv[(qc * NT + dt) % 2][:],
                                        start=(dt == 0), stop=(dt == NT - 1))
                s_ms.inc(inst, 1)
            idx = 0
            for qc in range(NQ):
                eng.wait_ge(s_h.h, h_base + 16 * (qc + 1))
                for dtc in range(NW):
                    pw.wait(eng, w_marks[qc * NW + dtc])
                    for di in range(WCH // 128):
                        dt = dtc * (WCH // 128) + di
                        if idx >= 2:
                            eng.wait_ge(s_e.h, e_plan[idx - 2])
                        p = ps_y[idx % 2]
                        for t in range(NKT):
                            blk, basep, rows = hrow(t)
                            inst = nc.tensor.matmul(
                                p[:],
                                w_sb[(qc * NW + dtc) % 2][basep:basep + rows,
                                                          blk * WCH + di * 128: blk * WCH + (di + 1) * 128],
                                h_sb[basep:basep + rows, blk * QCW:(blk + 1) * QCW],
                                start=(t == 0), stop=(t == NKT - 1))
                        s_mm.inc(inst, 1)
                        if DOWN_LITE == 0 and dt >= 2:
                            ms_mm(qc, dt - 2)
                        idx += 1
                if DOWN_LITE == 0:
                    ms_mm(qc, NT - 2)
                    ms_mm(qc, NT - 1)
                    eng.wait_ge(s_r2.h, r2_done(qc))
                    if qc >= 1:
                        eng.wait_ge(s_f.h, f_plan[qc * NT - 1])
                    inst = nc.tensor.matmul(ps_bc[:], cst["ones_row_f32"],
                                            rr_t[:, (qc % 2) * QCW:(qc % 2 + 1) * QCW], start=True, stop=True)
                    s_bc.inc(inst, 1)

        def act_sec(eng):
            if DOWN_LITE:
                return
            idx = 0
            for qc in range(NQ):
                for dt in range(NT):
                    eng.wait_ge(s_e.h, e_plan[idx])
                    if idx >= 2:
                        eng.wait_ge(s_ms.h, ms_plan[idx - 2])
                    inst = eng.activation(sqv[idx % 2][:], y_sb[:, dt * QCW:(dt + 1) * QCW], AF.Square)
                    s_q.inc(inst, 1)
                    idx += 1
                eng.wait_ge(s_r1.h, r1_done(qc))
                if qc >= 2:
                    eng.wait_ge(s_bc.h, bc_plan[qc - 2])
                qs = slice((qc % 2) * QCW, (qc % 2 + 1) * QCW)
                eng.activation(rr_t[:, qs], ms_t[:, qs], AF.Ln, bias=cst["eps_rms"])
                inst = eng.activation(rr_t[:, qs], rr_t[:, qs], AF.Exp, scale=-0.5)
                s_r2.inc(inst, 1)

        def dve_sec(eng):
            idx = 0
            for qc in range(NQ):
                for dt in range(NT):
                    eng.wait_ge(s_mm.h, mm_plan[idx])
                    if qc >= 1:
                        if DOWN_LITE == 2:
                            po.wait(eng, o_mark((qc - 1) * NT + dt))
                        else:
                            eng.wait_ge(s_f.h, f_plan[(qc - 1) * NT + dt])
                            eng.wait_ge(s_q.h, q_plan[(qc - 1) * NT + dt])
                    inst = eng.tensor_copy(y_sb[:, dt * QCW:(dt + 1) * QCW], ps_y[idx % 2][:])
                    s_e.inc(inst, 1)
                    idx += 1
                if DOWN_LITE == 0:
                    eng.wait_ge(s_ms.h, ms_plan[qc * NT + NT - 1])
                    if qc >= 2:
                        eng.wait_ge(s_r2.h, r2_done(qc - 2))
                    qs = slice((qc % 2) * QCW, (qc % 2 + 1) * QCW)
                    inst = eng.tensor_scalar(ms_t[:, qs], ps_ms[0:1, :], 1.0 / D, None, ALU.mult)
                    s_r1.inc(inst, 1)
                    eng.wait_ge(s_bc.h, bc_plan[qc])
                if DOWN_LITE == 2:
                    continue
                for dt in range(NT):
                    gi = qc * NT + dt
                    sl = slice(dt * QCW, (dt + 1) * QCW)
                    if gi >= 2:
                        po.wait(eng, o_mark(gi - 2))
                    pxr.wait(eng, xr_mark(gi))
                    if DOWN_LITE == 0:
                        eng.tensor_tensor(tv[gi % 2][:], y_sb[:, sl], ps_bc[:], ALU.mult)
                        inst = eng.tensor_tensor(ov[gi % 2][:], tv[gi % 2][:], xr_sb[gi % 2][:], ALU.add)
                    else:
                        inst = eng.tensor_tensor(ov[gi % 2][:], y_sb[:, sl], xr_sb[gi % 2][:], ALU.add)
                    s_f.inc(inst, 1)

        b.sec("sync", sync_io)
        b.sec("tensor", pe_sec)
        b.sec("scalar", act_sec)
        b.sec("vector", dve_sec)
    return po.gates()



def build_program(phases=PHASES, taps=DEBUG_TAPS):
    nc = bass.Bass("TRN2", target_bir_lowering=False)

    def dram_in(name, shape, dt):
        return nc.dram_tensor(name, shape, dt, kind="ExternalInput").ap()

    def dram_int(name, shape, dt, tap=None):
        kind = "ExternalOutput" if (tap is not None and tap in taps) else "Internal"
        return nc.dram_tensor(name, shape, dt, kind=kind).ap()

    x0 = dram_in("x0", [D, T_PRE], F32)
    wq_a = dram_in("wq_a", [D, D], F32)
    wk_a = dram_in("wk_a", [D, D], F32)
    wv_a = dram_in("wv_a", [D, D], F32)
    wo_a = dram_in("wo_a", [D, D], F32)
    bq_a = dram_in("bq_a", [128, NT], F32)
    bk_a = dram_in("bk_a", [128, NT], F32)
    bv_r = dram_in("bv_r", [1, D], F32)
    bo_a = dram_in("bo_a", [128, NT], F32)
    wq_b = dram_in("wq_b", [D, D], F32)
    wk_b = dram_in("wk_b", [D, KVH * DH], F32)
    wv_b = dram_in("wv_b", [D, KVH * DH], F32)
    wo_b = dram_in("wo_b", [D, D], F32)
    gupT = dram_in("gupT", [256, GU], I32)
    sc_gu = dram_in("sc_gu", [256, GU], F32)
    downT = dram_in("downT", [704, D], I32)
    sc_dn = dram_in("sc_dn", [704, D], F32)
    rope_cos = dram_in("rope_cos", [128, T_PRE], F32)
    rope_sin = dram_in("rope_sin", [128, T_PRE], F32)
    pswap_in = dram_in("pswap", [128, 128], F32)

    out_fm = nc.dram_tensor("out_fm", [D, T_SUF], F32, kind="ExternalOutput").ap()

    x1 = dram_int("x1", [D, T_PRE], F32, "x1")
    x2 = dram_int("x2", [D, T_PRE], F32, "x2")
    x3 = dram_int("x3", [D, T_SUF], F32, "x3")
    xln = dram_int("xln", [D, T_PRE], BF16, "xln1")
    wgu_d = dram_int("wgu_d", [D, GU], BF16, "wgu")
    wdn_d = dram_int("wdn_d", [I_FFN, D], BF16, "wdn")
    hid = dram_int("hid", [I_FFN, T_PRE], BF16, "hid")
    ka = dram_int("ka", [H, DH, T_PRE], BF16, "ka")
    va = dram_int("va", [H, T_PRE, DH], BF16, "va")
    qa = dram_int("qa", [H, DH, T_PRE], BF16, "qa")
    pfa = dram_int("pfa", [H, DH, T_PRE], BF16, "pfa")
    kb = dram_int("kb", [KVH, DH, T_PRE], BF16)
    vb = dram_int("vb", [KVH, T_PRE, DH], BF16)
    qb = dram_int("qb", [H, DH, T_SUF], BF16)
    pfb = dram_int("pfb", [H, DH, T_SUF], BF16)

    with ExitStack() as ctx:
        block = ctx.enter_context(nc.Block())
        b = B(nc, ctx, block)

        ones_col_bf = ctx.enter_context(nc.sbuf_tensor("ones_col", [128, 1], BF16))
        ones_row_f32 = ctx.enter_context(nc.sbuf_tensor("ones_row", [1, 128], F32))
        ctpool = ctx.enter_context(nc.sbuf_tensor("ctpool", [128, 3], F32))
        pswap_sb = ctx.enter_context(nc.sbuf_tensor("pswap_sb", [128, 128], BF16))
        s_initc = b.sem("initc")
        s_initd = b.sem("initd")

        def init_consts(eng):
            eng.memset(ones_col_bf[:], 1.0)
            eng.memset(ctpool[:, 0:1], LN_EPS)
            eng.memset(ctpool[:, 1:2], RMS_EPS)
            eng.memset(ctpool[:, 2:3], -EXP_SHIFT_B)
            inst = eng.memset(ones_row_f32[:], 1.0)
            s_initc.inc(inst, 1)

        def init_loads(eng):
            eng.dma_start(out=pswap_sb[:], in_=pswap_in).then_inc(s_initd.h, 16)
            s_initd.n += 16

        b.sec("vector", init_consts)
        b.sec("gpsimd", init_loads)

        def await_init(eng):
            eng.wait_ge(s_initc.h, 1)
            eng.wait_ge(s_initd.h, 16)
        b.sec("tensor", await_init)
        b.sec("scalar", await_init)
        b.sec("vector", await_init)
        b.sec("sync", await_init)

        cst = dict(ones_col_bf=ones_col_bf[:], ones_row_f32=ones_row_f32[:],
                   eps_ln=ctpool[0:1, 0:1], eps_rms=ctpool[0:1, 1:2],
                   nshift=ctpool[:, 2:3])

        deq_done = None
        last = None
        if "p1" in phases or "p1ln" in phases:
            ln1 = emit_ln(b, "l1", x0, T_PRE, xln, cst, None)
            last = ln1
        if "deq" in phases:
            deq_done = emit_dequant(b, gupT, sc_gu, downT, sc_dn, wgu_d, wdn_d)
        if "p1" in phases or "p1kqv" in phases:
            ka_done = emit_fm_proj(b, "ka", wk_a, D, bk_a, xln, T_PRE,
                                   [ka[h] for h in range(H)], ln1, cst=cst)
            qa_done = emit_fm_proj(b, "qa", wq_a, D, bq_a, xln, T_PRE,
                                   [qa[h] for h in range(H)], ka_done, cst=cst)
            va_done = emit_v_proj(b, "va", wv_a, D, bv_r, xln, T_PRE,
                                  [va[h] for h in range(H)], qa_done, cst)
            last = va_done
        if "p1" in phases or "p1att" in phases:
            ata_done = emit_attention(b, "at_a", [qa[h] for h in range(H)],
                                      [ka[h] for h in range(H)], [va[h] for h in range(H)],
                                      lambda h: h, T_PRE, T_PRE, 1.0 / DH, 0.0,
                                      [pfa[h] for h in range(H)], H,
                                      ka_done + qa_done + va_done, cst)
            p1_done = emit_out_proj(b, "wo_a", wo_a, bo_a, pfa, H, x0, T_PRE, x1,
                                    ata_done, cst)
            last = p1_done

        if "p2" in phases or "p2gu" in phases or "p2dn" in phases:
            ln2 = emit_ln(b, "l2", x1, T_PRE, xln, cst, last)
            last = ln2
        if "p2" in phases or "p2gu" in phases or "p2dn" in phases:
            gu2 = emit_gate_up(b, "gu2", wgu_d, xln, T_PRE, hid, ln2, deq_done)
            last = gu2
        if "p2" in phases or "p2dn" in phases:
            p2_done = emit_down(b, "dn2", wdn_d, hid, T_PRE, x1, x2, gu2, deq_done, cst)
            last = p2_done

        if "p3" in phases:
            ln3 = emit_ln(b, "l3", x2, T_PRE, xln, cst, last)
            with ExitStack() as p3ctx:
                cos_sb = p3ctx.enter_context(nc.sbuf_tensor("cos_sb", [128, T_PRE], F32))
                sin_sb = p3ctx.enter_context(nc.sbuf_tensor("sin_sb", [128, T_PRE], F32))
                s_rt = b.sem("ropetab")
                rt_gate_val = ln3

                def rope_loads(eng):
                    _waits(eng, rt_gate_val)
                    eng.dma_start(out=cos_sb[:], in_=rope_cos).then_inc(s_rt.h, 16)
                    s_rt.n += 16
                    eng.dma_start(out=sin_sb[:], in_=rope_sin).then_inc(s_rt.h, 16)
                    s_rt.n += 16
                b.sec("gpsimd", rope_loads)
                rope = dict(pswap=pswap_sb[:], cos=cos_sb[:], sin=sin_sb[:])
                kb_done = emit_fm_proj(b, "kb", wk_b, KVH * DH, None, xln, T_PRE,
                                       [kb[h] for h in range(KVH)], ln3 + [(s_rt, 32)],
                                       rope=rope, cst=cst)
                vb_done = emit_v_proj(b, "vb", wv_b, KVH * DH, None, xln, T_PRE,
                                      [vb[h] for h in range(KVH)], kb_done, cst)
                qb_done = emit_fm_proj(b, "qb", wq_b, D, None, xln, T_SUF,
                                       [qb[h] for h in range(H)], vb_done, rope=rope, cst=cst)
                atb_done = emit_attention(b, "at_b", [qb[h] for h in range(H)],
                                          [kb[h] for h in range(KVH)], [vb[h] for h in range(KVH)],
                                          lambda h: h % 4, T_SUF, T_PRE, 1.0 / float(np.sqrt(DH)),
                                          EXP_SHIFT_B, [pfb[h] for h in range(H)], H,
                                          kb_done + vb_done + qb_done, cst)
            p3_done = emit_out_proj(b, "wo_b", wo_b, None, pfb, H, x2[:, 0:T_SUF], T_SUF,
                                    x3, atb_done, cst)
            last = p3_done

        if "p4" in phases:
            xln_s = xln[:, 0:T_SUF]
            hid_s = hid[:, 0:T_SUF]
            ln4 = emit_ln(b, "l4", x3, T_SUF, xln_s, cst, last)
            gu4 = emit_gate_up(b, "gu4", wgu_d, xln_s, T_SUF, hid_s, ln4, deq_done)
            p4_done = emit_down(b, "dn4", wdn_d, hid_s, T_SUF, x3, out_fm, gu4,
                                deq_done, cst)
            last = p4_done

        fin = last if last is not None else deq_done
        fin_deq = deq_done

        def final_wait(eng):
            _waits(eng, fin)
            _waits(eng, fin_deq)

        b.sec("sync", final_wait)
    return nc



def _bias2d(v):
    return np.ascontiguousarray(np.asarray(v, np.float32).reshape(NT, 128).T)


def _prep_inputs(inp):
    f32 = np.float32
    x = np.asarray(inp["L0_attn_a_x"])
    hm = HMAP
    im_full = np.concatenate([IMAP, I_FFN + IMAP])

    shared = dict(
        wq_a=np.ascontiguousarray(np.asarray(inp["L0_attn_a_Wq"], f32)[hm, :]),
        wk_a=np.ascontiguousarray(np.asarray(inp["L0_attn_a_Wk"], f32)[hm, :]),
        wv_a=np.ascontiguousarray(np.asarray(inp["L0_attn_a_Wv"], f32)[hm, :]),
        wo_a=np.ascontiguousarray(np.asarray(inp["L0_attn_a_Wo"], f32)[:, hm]),
        bq_a=_bias2d(inp["L0_attn_a_bq"]),
        bk_a=_bias2d(inp["L0_attn_a_bk"]),
        bv_r=np.asarray(inp["L0_attn_a_bv"], f32).reshape(1, D),
        bo_a=_bias2d(np.asarray(inp["L0_attn_a_bo"], f32)[hm]),
        wq_b=np.ascontiguousarray(np.asarray(inp["L0_attn_b_W_q"], f32)[hm, :]),
        wk_b=np.ascontiguousarray(np.asarray(inp["L0_attn_b_W_k"], f32)[hm, :]),
        wv_b=np.ascontiguousarray(np.asarray(inp["L0_attn_b_W_v"], f32)[hm, :]),
        wo_b=np.ascontiguousarray(np.asarray(inp["L0_attn_b_W_o"], f32)[:, hm]),
    )
    gup = np.asarray(inp["L0_ffn_gate_up_packed"])
    gup_p = np.concatenate([gup[:I_FFN][IMAP], gup[I_FFN:][IMAP]])
    shared["gupT"] = np.ascontiguousarray(gup_p.T)
    shared["sc_gu"] = np.ascontiguousarray(
        np.repeat(np.asarray(inp["L0_ffn_gate_up_scales"]).astype(f32), 16, axis=0)[:, im_full])
    dnp = np.asarray(inp["L0_ffn_down_packed"])[hm, :]
    shared["downT"] = np.ascontiguousarray(dnp.T)
    shared["sc_dn"] = np.ascontiguousarray(
        np.repeat(np.asarray(inp["L0_ffn_down_scales"]).astype(f32), 16, axis=0)[:, hm])
    pswap = np.zeros((128, 128), f32)
    pswap[np.arange(128), np.arange(128) ^ 1] = 1.0
    shared["pswap"] = pswap

    in_maps = []
    for c in range(8):
        bidx, half = c // 2, c % 2
        xb = np.asarray(x[bidx])[:, hm]
        if half == 1:
            xb = np.roll(xb, -T_SUF, axis=0)
        m = dict(shared)
        m["x0"] = np.ascontiguousarray(xb.T.astype(f32))
        pos = (np.arange(T_PRE) + half * T_SUF) % T_PRE
        cidx = np.arange(64, dtype=np.float64)
        freqs = 1.0 / (10000.0 ** (cidx / 64.0))
        ang = pos[None, :] * freqs[:, None]
        cos_t = np.repeat(np.cos(ang), 2, axis=0).astype(f32)
        sin_t = np.repeat(np.sin(ang), 2, axis=0).astype(f32)
        sin_t[0::2, :] *= -1.0
        m["rope_cos"] = np.ascontiguousarray(cos_t)
        m["rope_sin"] = np.ascontiguousarray(sin_t)
        in_maps.append(m)
    return in_maps


def kernel(**inputs):
    in_maps = _prep_inputs(inputs)
    nc = build_program()
    res = run_bass_kernel_spmd(nc, in_maps, core_ids=list(range(8)))
    out = np.empty((4, T_PRE, D), np.float32)
    for c in range(8):
        bidx, half = c // 2, c % 2
        of = np.asarray(res.results[c]["out_fm"])
        sl = slice(half * T_SUF, (half + 1) * T_SUF)
        out[bidx, sl, :][:, HMAP] = of.T
    return out


# version 09
# speedup vs baseline: 1.2197x; 1.0615x over previous
import numpy as np
import ml_dtypes
from contextlib import ExitStack

import concourse.bass as bass
import concourse.mybir as mybir
from concourse.bass_utils import run_bass_kernel_spmd

F32 = mybir.dt.float32
BF16 = mybir.dt.bfloat16
I32 = mybir.dt.int32
U16 = mybir.dt.uint16
AF = mybir.ActivationFunctionType
ALU = mybir.AluOpType

D = 2048
NT = D // 128
H = 16
DH = 128
KVH = 4
I_FFN = 5632
GU = 2 * I_FFN
T_PRE = 4096
T_SUF = 2048
QCW = 512
LN_EPS = 1e-5
RMS_EPS = 1e-6
EXP_SHIFT_B = 4.0

PHASES = ("deq", "p1", "p2", "p3", "p4")
DEBUG_TAPS = ()
DOWN_LITE = 0


def _pi_h():
    hmap = np.empty(D, np.int64)
    for jb in range(2):
        for k in range(8):
            jj = np.arange(128)
            hmap[(jb * 8 + k) * 128 + jj] = 8 * (jb * 128 + jj) + k
    return hmap


def _pi_i():
    imap = np.empty(I_FFN, np.int64)
    off = 0
    tiles = []
    for jb2 in range(6):
        rows = 128 if jb2 < 5 else 64
        for k in range(8):
            jj = np.arange(rows)
            imap[off:off + rows] = 8 * (jb2 * 128 + jj) + k
            tiles.append((jb2, k, off, rows))
            off += rows
    assert off == I_FFN
    return imap, tiles


HMAP = _pi_h()
IMAP, DN_TILES = _pi_i()


class SemC:
    def __init__(self, ctx, nc, name):
        self.h = ctx.enter_context(nc.semaphore(name))
        self.n = 0

    def inc(self, inst, k):
        inst.then_inc(self.h, k)
        self.n += k
        return self.n


class Pipe:

    def __init__(self, b, name, nslots=2):
        self.sems = [b.sem(f"{name}_{i}") for i in range(nslots)]
        self.nslots = nslots

    def inc(self, inst, slot, k=16):
        sc = self.sems[slot % self.nslots]
        inst.then_inc(sc.h, k)
        sc.n += k
        return (slot % self.nslots, sc.n)

    def wait(self, eng, mark):
        slot, val = mark
        eng.wait_ge(self.sems[slot].h, val)

    def gates(self):
        return [(sc, sc.n) for sc in self.sems]


class B:
    def __init__(self, nc, ctx, block):
        self.nc = nc
        self.ctx = ctx
        self.block = block
        self.sems = {}

    def sem(self, name):
        if name not in self.sems:
            self.sems[name] = SemC(self.ctx, self.nc, name)
        return self.sems[name]

    def sec(self, engine_name, fn):
        getattr(self.block, engine_name)(fn)


def _waits(eng, ready):
    if ready is None:
        return
    if isinstance(ready, tuple):
        ready = [ready]
    for (s, v) in ready:
        if v > 0:
            eng.wait_ge(s.h, v)



def emit_dequant(b, gupT, sc_gu, downT, sc_dn, wgu_d, wdn_d):
    nc = b.nc
    pld = Pipe(b, "deq_ld")
    s_dv = b.sem("deq_dv")
    pst = Pipe(b, "deq_st")

    OC = 704
    jobs = []
    for jb in range(2):
        for oc in range(GU // OC):
            cs = slice(oc * OC, (oc + 1) * OC)
            jobs.append((128, OC, gupT[jb * 128:(jb + 1) * 128, cs],
                         sc_gu[jb * 128:(jb + 1) * 128, cs],
                         [(k, wgu_d[(jb * 8 + k) * 128:(jb * 8 + k + 1) * 128, cs])
                          for k in range(8)]))
    DC = 512
    for jb2 in range(6):
        rows = 128 if jb2 < 5 else 64
        for dc in range(D // DC):
            ds_ = slice(dc * DC, (dc + 1) * DC)
            planes = [(k, wdn_d[off:off + trows, ds_])
                      for (tjb2, k, off, trows) in DN_TILES if tjb2 == jb2]
            jobs.append((rows, DC, downT[jb2 * 128:jb2 * 128 + rows, ds_],
                         sc_dn[jb2 * 128:jb2 * 128 + rows, ds_], planes))

    nbuf = 2
    ctx = b.ctx
    mx_c = OC
    sbr = lambda name, shape, dt: ctx.enter_context(nc.sbuf_tensor(name, shape, dt, side="right"))
    slab = [sbr(f"dq_slab{i}", [128, mx_c], I32) for i in range(nbuf)]
    scl = [sbr(f"dq_scl{i}", [128, mx_c], F32) for i in range(nbuf)]
    t1 = sbr("dq_t1", [128, mx_c], I32)
    bM = sbr("dq_bM", [128, mx_c], U16)
    t4 = sbr("dq_t4", [128, mx_c], BF16)
    t5 = sbr("dq_t5", [128, mx_c], BF16)
    s32 = t1
    s16 = sbr("dq_s16", [128, mx_c], U16)
    t6 = t4.bitcast(U16) if False else sbr("dq_t6", [128, mx_c], U16)
    wout = [sbr(f"dq_w{i}", [128, mx_c], BF16) for i in range(2)]

    dv_base = s_dv.n
    dv_plan = [dv_base]
    acc = dv_base
    for (_, _, _, _, planes) in jobs:
        acc += len(planes)
        dv_plan.append(acc)
    n_planes = acc - dv_base

    ld_marks = []
    st_plan = []
    pc = 0
    for (rows, cols, _, _, planes) in jobs:
        for (k, dst) in planes:
            st_plan.append((wout[pc % 2], dst, rows, cols)); pc += 1

    st_bounds = [0]
    for (_, _, _, _, planes) in jobs:
        st_bounds.append(st_bounds[-1] + len(planes))

    def gp_io(eng):
        def store_job(j):
            for si in range(st_bounds[j], st_bounds[j + 1]):
                wo, dst, rows, cols = st_plan[si]
                eng.wait_ge(s_dv.h, dv_base + si + 1)
                pst.inc(eng.dma_start(out=dst, in_=wo[0:rows, 0:cols]), si)
        for ji, (rows, cols, slab_ap, sc_ap, planes) in enumerate(jobs):
            if ji >= 1:
                store_job(ji - 1)
            pld.inc(eng.dma_start(out=slab[ji % nbuf][0:rows, 0:cols], in_=slab_ap), ji)
            inst = eng.dma_start(out=scl[ji % nbuf][0:rows, 0:cols], in_=sc_ap)
            ld_marks.append(pld.inc(inst, ji))
        store_job(len(jobs) - 1)

    def decoder(eng):
        pcount = 0
        for ji, (rows, cols, slab_ap, sc_ap, planes) in enumerate(jobs):
            pld.wait(eng, ld_marks[ji])
            sl = slab[ji % nbuf][0:rows, 0:cols]
            sc = scl[ji % nbuf][0:rows, 0:cols]
            for (k, dst) in planes:
                wo = wout[pcount % 2][0:rows, 0:cols]
                if pcount >= 2:
                    pst.wait(eng, (pcount % 2, 16 * ((pcount - 2) // 2 + 1)))
                _t1, _bM, _t4, _t5 = (x[0:rows, 0:cols] for x in (t1, bM, t4, t5))
                _s32, _s16, _t6 = (x[0:rows, 0:cols] for x in (s32, s16, t6))
                sh = 6 - 4 * k
                if sh >= 0:
                    eng.tensor_scalar(_t1, sl, sh, 0x1C0, ALU.logical_shift_left, ALU.bitwise_and)
                else:
                    eng.tensor_scalar(_t1, sl, -sh, 0x1C0, ALU.logical_shift_right, ALU.bitwise_and)
                eng.tensor_scalar(_bM, _t1, 0x3F00, None, ALU.add)
                Bv = _bM.bitcast(BF16)
                eng.tensor_scalar(_t4, Bv, 2.0, -1.0, ALU.mult, ALU.add)
                eng.tensor_tensor(_t5, _t4, Bv, ALU.min)
                ssh = 12 - 4 * k
                if ssh >= 0:
                    eng.tensor_scalar(_s32, sl, ssh, 0x8000, ALU.logical_shift_left, ALU.bitwise_and)
                else:
                    eng.tensor_scalar(_s32, sl, -ssh, 0x8000, ALU.logical_shift_right, ALU.bitwise_and)
                eng.tensor_copy(_s16, _s32)
                eng.tensor_tensor(_t6, _t5.bitcast(U16), _s16, ALU.bitwise_or)
                inst = eng.tensor_tensor(wo, _t6.bitcast(BF16), sc, ALU.mult)
                s_dv.inc(inst, 1)
                pcount += 1

    b.sec("gpsimd", gp_io)
    b.sec("vector", decoder)
    return pst.gates()



def emit_ln(b, tag, src, T, xln_d, cst, ready):
    nc = b.nc
    NQ = T // QCW
    px = Pipe(b, "ln_x")
    po = Pipe(b, "ln_o")
    s_c = b.sem("ln_c")
    s_p = b.sem("ln_p")
    s_d = b.sem("ln_d")
    s_l = b.sem("ln_l")

    src_r = src.rearrange("(dt p) t -> p dt t", p=128)
    xln_r = xln_d.rearrange("(dt p) t -> p dt t", p=128)

    with ExitStack() as ctx:
        xc = [ctx.enter_context(nc.sbuf_tensor(f"{tag}_xc{i}", [128, NT * QCW], F32)) for i in range(2)]
        xb = [ctx.enter_context(nc.sbuf_tensor(f"{tag}_xb{i}", [128, NT * QCW], BF16)) for i in range(2)]
        sq = [ctx.enter_context(nc.sbuf_tensor(f"{tag}_sq{i}", [128, NT * QCW], BF16)) for i in range(2)]
        m_t = ctx.enter_context(nc.sbuf_tensor(f"{tag}_m", [1, 2 * QCW], F32))
        v_t = ctx.enter_context(nc.sbuf_tensor(f"{tag}_v", [1, 2 * QCW], F32))
        var_t = ctx.enter_context(nc.sbuf_tensor(f"{tag}_var", [1, 2 * QCW], F32))
        rstd_t = ctx.enter_context(nc.sbuf_tensor(f"{tag}_rs", [1, 2 * QCW], F32))
        mr_t = ctx.enter_context(nc.sbuf_tensor(f"{tag}_mr", [1, 2 * QCW], F32))
        tmp = [ctx.enter_context(nc.sbuf_tensor(f"{tag}_tm{i}", [128, QCW], F32)) for i in range(2)]
        ob = [ctx.enter_context(nc.sbuf_tensor(f"{tag}_ob{i}", [128, QCW], BF16)) for i in range(2)]
        ps_m = ctx.enter_context(nc.psum_tensor(f"{tag}_pm", [128, QCW], F32))
        ps_v = ctx.enter_context(nc.psum_tensor(f"{tag}_pv", [128, QCW], F32))
        ps_b1 = ctx.enter_context(nc.psum_tensor(f"{tag}_pb1", [128, QCW], F32))
        ps_b2 = ctx.enter_context(nc.psum_tensor(f"{tag}_pb2", [128, QCW], F32))

        c_base, p_base, d_base, l_base = s_c.n, s_p.n, s_d.n, s_l.n
        c_done = lambda qc: c_base + qc + 1
        _s1d, _s2d = {}, {}
        _pc = p_base
        _order = [("s1", 0)] + ([("s1", 1)] if NQ > 1 else [])
        for _q in range(NQ):
            _order.append(("s2", _q))
            if _q + 2 < NQ:
                _order.append(("s1", _q + 2))
        for (_k, _q) in _order:
            _pc += 1
            (_s1d if _k == "s1" else _s2d)[_q] = _pc
        s1_done = lambda qc: _s1d[qc]
        s2_done = lambda qc: _s2d[qc]
        v1_done = lambda qc: d_base + (2 + NT) * qc + 1
        v2_done = lambda qc: d_base + (2 + NT) * qc + 2
        v3_done = lambda qc, dt: d_base + (2 + NT) * qc + 2 + dt + 1
        a_done = lambda qc: l_base + qc + 1
        ob_base = [sc.n for sc in po.sems]
        o_mark = lambda gi: (gi % 2, ob_base[gi % 2] + 16 * (gi // 2 + 1))
        x_marks = []

        def sync_io(eng):
            def load(qc):
                inst = eng.dma_start(out=xc[qc % 2][:].rearrange("p (dt c) -> p dt c", dt=NT),
                                     in_=src_r[:, :, qc * QCW:(qc + 1) * QCW])
                x_marks.append(px.inc(inst, qc))
            _waits(eng, ready)
            load(0)
            if NQ > 1:
                load(1)
            for qc in range(NQ):
                for dt in range(NT):
                    gi = qc * NT + dt
                    eng.wait_ge(s_d.h, v3_done(qc, dt))
                    inst = eng.dma_start(out=xln_r[:, dt, qc * QCW:(qc + 1) * QCW],
                                         in_=ob[gi % 2][:])
                    po.inc(inst, gi)
                if qc + 2 < NQ:
                    load(qc + 2)

        def act_sec(eng):
            def casts(qc):
                px.wait(eng, x_marks[qc])
                if qc >= 2:
                    eng.wait_ge(s_p.h, s1_done(qc - 2))
                x = xc[qc % 2]
                for dt in range(NT):
                    sl = slice(dt * QCW, (dt + 1) * QCW)
                    eng.activation(sq[qc % 2][:, sl], x[:, sl], AF.Square)
                inst = None
                for dt in range(NT):
                    sl = slice(dt * QCW, (dt + 1) * QCW)
                    inst = eng.copy(xb[qc % 2][:, sl], x[:, sl])
                s_c.inc(inst, 1)
            casts(0)
            if NQ > 1:
                casts(1)
            for qc in range(NQ):
                eng.wait_ge(s_d.h, v1_done(qc))
                qs = slice((qc % 2) * QCW, (qc % 2 + 1) * QCW)
                eng.activation(rstd_t[:, qs], var_t[:, qs], AF.Ln, bias=cst["eps_ln"])
                inst = eng.activation(rstd_t[:, qs], rstd_t[:, qs], AF.Exp, scale=-0.5)
                s_l.inc(inst, 1)
                if qc + 2 < NQ:
                    casts(qc + 2)

        def pe_sec(eng):
            def s1(qc):
                eng.wait_ge(s_c.h, c_done(qc))
                if qc >= 1:
                    eng.wait_ge(s_d.h, v1_done(qc - 1))
                for dt in range(NT):
                    sl = slice(dt * QCW, (dt + 1) * QCW)
                    nc.tensor.matmul(ps_m[0:1, :], cst["ones_col_bf"], xb[qc % 2][:, sl],
                                     start=(dt == 0), stop=(dt == NT - 1))
                for dt in range(NT):
                    sl = slice(dt * QCW, (dt + 1) * QCW)
                    inst = nc.tensor.matmul(ps_v[0:1, :], cst["ones_col_bf"], sq[qc % 2][:, sl],
                                            start=(dt == 0), stop=(dt == NT - 1))
                s_p.inc(inst, 1)
            s1(0)
            if NQ > 1:
                s1(1)
            for qc in range(NQ):
                eng.wait_ge(s_d.h, v2_done(qc))
                if qc >= 1:
                    eng.wait_ge(s_d.h, v3_done(qc - 1, NT - 1))
                qs = slice((qc % 2) * QCW, (qc % 2 + 1) * QCW)
                nc.tensor.matmul(ps_b1[:], cst["ones_row_f32"], rstd_t[:, qs], start=True, stop=True)
                inst = nc.tensor.matmul(ps_b2[:], cst["ones_row_f32"], mr_t[:, qs], start=True, stop=True)
                s_p.inc(inst, 1)
                if qc + 2 < NQ:
                    s1(qc + 2)

        def dve_sec(eng):
            for qc in range(NQ):
                qs = slice((qc % 2) * QCW, (qc % 2 + 1) * QCW)
                eng.wait_ge(s_p.h, s1_done(qc))
                if qc >= 2:
                    eng.wait_ge(s_p.h, s2_done(qc - 2))
                eng.tensor_scalar(m_t[:, qs], ps_m[0:1, :], 1.0 / D, None, ALU.mult)
                eng.tensor_scalar(v_t[:, qs], ps_v[0:1, :], 1.0 / D, None, ALU.mult)
                eng.tensor_tensor(mr_t[:, qs], m_t[:, qs], m_t[:, qs], ALU.mult)
                inst = eng.tensor_tensor(var_t[:, qs], v_t[:, qs], mr_t[:, qs], ALU.subtract)
                s_d.inc(inst, 1)
                eng.wait_ge(s_l.h, a_done(qc))
                inst = eng.tensor_tensor(mr_t[:, qs], m_t[:, qs], rstd_t[:, qs], ALU.mult)
                s_d.inc(inst, 1)
                eng.wait_ge(s_p.h, s2_done(qc))
                x = xc[qc % 2]
                for dt in range(NT):
                    gi = qc * NT + dt
                    if gi >= 2:
                        po.wait(eng, o_mark(gi - 2))
                    sl = slice(dt * QCW, (dt + 1) * QCW)
                    tm = tmp[dt % 2]
                    eng.tensor_tensor(tm[:], x[:, sl], ps_b1[:], ALU.mult)
                    inst = eng.tensor_tensor(ob[gi % 2][:], tm[:], ps_b2[:], ALU.subtract)
                    s_d.inc(inst, 1)

        b.sec("sync", sync_io)
        b.sec("scalar", act_sec)
        b.sec("tensor", pe_sec)
        b.sec("vector", dve_sec)
    return po.gates()



def emit_fm_proj(b, tag, w_dram, outd, bias_2d, xln_d, T, out_dram, ready,
                 rope=None, cst=None):
    nc = b.nc
    NQ = T // QCW
    NO = outd // 128
    NTIL = NQ * NO
    s_w = b.sem("fp_w")
    px = Pipe(b, "fp_x")
    po = Pipe(b, "fp_o")
    s_mm = b.sem("fp_mm")
    s_sw = b.sem("fp_sw")
    s_e = b.sem("fp_e")
    s_r = b.sem("fp_r")

    xln_r = xln_d.rearrange("(dt p) t -> p dt t", p=128)

    with ExitStack() as ctx:
        w_sb = ctx.enter_context(nc.sbuf_tensor(f"{tag}_wsb", [128, NT * outd], BF16))
        wst = [ctx.enter_context(nc.sbuf_tensor(f"{tag}_wst{i}", [128, NT * 256], F32)) for i in range(2)]
        bias_sb = ctx.enter_context(nc.sbuf_tensor(f"{tag}_bs", [128, max(NO, 1)], F32))
        xcb = [ctx.enter_context(nc.sbuf_tensor(f"{tag}_xc{i}", [128, NT * QCW], BF16)) for i in range(2)]
        ev = [ctx.enter_context(nc.sbuf_tensor(f"{tag}_ev{i}", [128, QCW], BF16)) for i in range(2)]
        ps = [ctx.enter_context(nc.psum_tensor(f"{tag}_ps{i}", [128, QCW], F32)) for i in range(2)]
        if rope is not None:
            rt1 = ctx.enter_context(nc.sbuf_tensor(f"{tag}_rt1", [128, QCW], F32))
            rt2 = ctx.enter_context(nc.sbuf_tensor(f"{tag}_rt2", [128, QCW], F32))
            ro = [ctx.enter_context(nc.sbuf_tensor(f"{tag}_ro{i}", [128, QCW], BF16)) for i in range(2)]
            ps_sw = [ctx.enter_context(nc.psum_tensor(f"{tag}_psw{i}", [128, QCW], F32)) for i in range(2)]

        has_bias = bias_2d is not None
        NWC = outd // 256
        pwst = Pipe(b, "fp_wst")
        wst_marks = []
        w_base_cast = s_w.n
        w_mark = s_w.n + NWC + (16 if has_bias else 0)
        mm_plan = [s_mm.n + i + 1 for i in range(NTIL)]
        sw_plan = [s_sw.n + i + 1 for i in range(NTIL)]
        e_plan = [s_e.n + i + 1 for i in range(NTIL)]
        r_plan = [s_r.n + i + 1 for i in range(NTIL)]
        ob_base = [sc.n for sc in po.sems]
        o_mark = lambda i: (i % 2, ob_base[i % 2] + 16 * (i // 2 + 1))
        last_plan = r_plan if rope is not None else e_plan
        s_last = s_r if rope is not None else s_e
        x_marks = []

        def w_loader(eng):
            _waits(eng, ready)
            if has_bias:
                eng.dma_start(out=bias_sb[:, 0:NO], in_=bias_2d).then_inc(s_w.h, 16)
                s_w.n += 16
            for ch in range(NWC):
                if ch >= 2:
                    eng.wait_ge(s_w.h, w_base_cast + (16 if has_bias else 0) + ch - 1)
                inst = eng.dma_start(out=wst[ch % 2][:].rearrange("p (dt o) -> p dt o", o=256),
                                     in_=w_dram.rearrange("(dt p) o -> p dt o", p=128)[:, :, ch * 256:(ch + 1) * 256])
                wst_marks.append(pwst.inc(inst, ch))

        def w_cast(eng):
            for ch in range(NWC):
                pwst.wait(eng, wst_marks[ch])
                inst = eng.copy(w_sb[:].rearrange("p (dt o) -> p dt o", o=outd)[:, :, ch * 256:(ch + 1) * 256],
                                wst[ch % 2][:].rearrange("p (dt o) -> p dt o", o=256))
                s_w.inc(inst, 1)

        def sync_io(eng):
            def load(qc):
                inst = eng.dma_start(out=xcb[qc % 2][:].rearrange("p (dt c) -> p dt c", dt=NT),
                                     in_=xln_r[:, :, qc * QCW:(qc + 1) * QCW])
                x_marks.append(px.inc(inst, qc))
            _waits(eng, ready)
            load(0)
            if NQ > 1:
                load(1)
            idx = 0
            for qc in range(NQ):
                for ot in range(NO):
                    src = (ro[idx % 2] if rope is not None else ev[idx % 2])[:]
                    eng.wait_ge(s_last.h, last_plan[idx])
                    inst = eng.dma_start(out=out_dram[ot][:, qc * QCW:(qc + 1) * QCW], in_=src)
                    po.inc(inst, idx)
                    idx += 1
                if qc + 2 < NQ:
                    load(qc + 2)

        def mms(eng):
            eng.wait_ge(s_w.h, w_mark)
            idx = 0
            for qc in range(NQ):
                px.wait(eng, x_marks[qc])
                for ot in range(NO):
                    if idx >= 2:
                        eng.wait_ge(s_e.h, e_plan[idx - 2])
                    p = ps[idx % 2]
                    for dt in range(NT):
                        inst = nc.tensor.matmul(
                            p[:], w_sb[:, dt * outd + ot * 128: dt * outd + (ot + 1) * 128],
                            xcb[qc % 2][:, dt * QCW:(dt + 1) * QCW],
                            start=(dt == 0), stop=(dt == NT - 1))
                    s_mm.inc(inst, 1)
                    if rope is not None and idx >= 1:
                        j = idx - 1
                        eng.wait_ge(s_e.h, e_plan[j])
                        if j >= 2:
                            eng.wait_ge(s_r.h, r_plan[j - 2])
                        inst = nc.tensor.matmul(ps_sw[j % 2][:], rope["pswap"], ev[j % 2][:],
                                                start=True, stop=True)
                        s_sw.inc(inst, 1)
                    idx += 1
            if rope is not None:
                j = NTIL - 1
                eng.wait_ge(s_e.h, e_plan[j])
                if j >= 2:
                    eng.wait_ge(s_r.h, r_plan[j - 2])
                inst = nc.tensor.matmul(ps_sw[j % 2][:], rope["pswap"], ev[j % 2][:],
                                        start=True, stop=True)
                s_sw.inc(inst, 1)

        def evac(eng):
            idx = 0
            for qc in range(NQ):
                for ot in range(NO):
                    eng.wait_ge(s_mm.h, mm_plan[idx])
                    if idx >= 2:
                        eng.wait_ge(s_last.h, last_plan[idx - 2])
                        if rope is None:
                            po.wait(eng, o_mark(idx - 2))
                    bias = bias_sb[:, ot:ot + 1] if has_bias else 0.0
                    fn = AF.Identity if has_bias else AF.Copy
                    inst = eng.activation(ev[idx % 2][:], ps[idx % 2][:], fn, bias=bias)
                    s_e.inc(inst, 1)
                    idx += 1

        def rope_dve(eng):
            idx = 0
            for qc in range(NQ):
                qs = slice(qc * QCW, (qc + 1) * QCW)
                for ot in range(NO):
                    eng.wait_ge(s_sw.h, sw_plan[idx])
                    if idx >= 2:
                        po.wait(eng, o_mark(idx - 2))
                    eng.tensor_tensor(rt1[:], ps_sw[idx % 2][:], rope["sin"][:, qs], ALU.mult)
                    eng.tensor_tensor(rt2[:], ev[idx % 2][:], rope["cos"][:, qs], ALU.mult)
                    inst = eng.tensor_tensor(ro[idx % 2][:], rt1[:], rt2[:], ALU.add)
                    s_r.inc(inst, 1)
                    idx += 1

        b.sec("sync", w_loader)
        b.sec("scalar", w_cast)
        b.sec("sync", sync_io)
        b.sec("tensor", mms)
        b.sec("scalar", evac)
        if rope is not None:
            b.sec("vector", rope_dve)
    return po.gates()



def emit_v_proj(b, tag, wv_dram, outd, bv_row, xln_d, T, v_dram, ready, cst):
    nc = b.nc
    NQ = T // QCW
    NOC = outd // QCW
    NTIL = NQ * 4 * NOC
    s_w = b.sem("vp_w")
    px = Pipe(b, "vp_x")
    po = Pipe(b, "vp_o")
    s_mm = b.sem("vp_mm")
    s_e = b.sem("vp_e")

    xln_r = xln_d.rearrange("(dt p) t -> p dt t", p=128)

    with ExitStack() as ctx:
        w_sb = ctx.enter_context(nc.sbuf_tensor(f"{tag}_wsb", [128, NT * outd], BF16))
        wst = [ctx.enter_context(nc.sbuf_tensor(f"{tag}_wst{i}", [128, NT * 256], F32)) for i in range(2)]
        xcb = [ctx.enter_context(nc.sbuf_tensor(f"{tag}_xc{i}", [128, NT * QCW], BF16)) for i in range(2)]
        ev = [ctx.enter_context(nc.sbuf_tensor(f"{tag}_ev{i}", [128, QCW], BF16)) for i in range(2)]
        bvrow_sb = ctx.enter_context(nc.sbuf_tensor(f"{tag}_bvr", [1, max(outd, QCW)], F32))
        ps = [ctx.enter_context(nc.psum_tensor(f"{tag}_ps{i}", [128, QCW], F32)) for i in range(2)]

        has_bias = bv_row is not None
        NWC = outd // 256
        pwst = Pipe(b, "vp_wst")
        wst_marks = []
        w_base_cast = s_w.n
        w_mark = s_w.n + NWC + (16 if has_bias else 0)
        mm_base = s_mm.n
        mm_plan = [mm_base + i + 1 for i in range(NTIL)]
        e_plan = [s_e.n + i + 1 for i in range(NTIL)]
        ob_base = [sc.n for sc in po.sems]
        o_mark = lambda i: (i % 2, ob_base[i % 2] + 16 * 4 * (i // 2 + 1))
        x_marks = []

        def w_loader(eng):
            _waits(eng, ready)
            if has_bias:
                eng.dma_start(out=bvrow_sb[:, 0:outd], in_=bv_row).then_inc(s_w.h, 16)
                s_w.n += 16
            for ch in range(NWC):
                if ch >= 2:
                    eng.wait_ge(s_w.h, w_base_cast + (16 if has_bias else 0) + ch - 1)
                inst = eng.dma_start(out=wst[ch % 2][:].rearrange("p (dt o) -> p dt o", o=256),
                                     in_=wv_dram.rearrange("(dt p) o -> p dt o", p=128)[:, :, ch * 256:(ch + 1) * 256])
                wst_marks.append(pwst.inc(inst, ch))

        def w_cast(eng):
            for ch in range(NWC):
                pwst.wait(eng, wst_marks[ch])
                inst = eng.copy(w_sb[:].rearrange("p (dt o) -> p dt o", o=outd)[:, :, ch * 256:(ch + 1) * 256],
                                wst[ch % 2][:].rearrange("p (dt o) -> p dt o", o=256))
                s_w.inc(inst, 1)

        def sync_io(eng):
            def load(qc):
                inst = eng.dma_start(out=xcb[qc % 2][:].rearrange("p (dt c) -> p dt c", dt=NT),
                                     in_=xln_r[:, :, qc * QCW:(qc + 1) * QCW])
                x_marks.append(px.inc(inst, qc))
            _waits(eng, ready)
            load(0)
            if NQ > 1:
                load(1)
            idx = 0
            for qc in range(NQ):
                for tt in range(4):
                    for oc in range(NOC):
                        eng.wait_ge(s_e.h, e_plan[idx])
                        for j in range(4):
                            hh = oc * 4 + j
                            inst = eng.dma_start(out=v_dram[hh][qc * QCW + tt * 128: qc * QCW + (tt + 1) * 128, :],
                                                 in_=ev[idx % 2][:, j * 128:(j + 1) * 128])
                            po.inc(inst, idx)
                        idx += 1
                if qc + 2 < NQ:
                    load(qc + 2)

        def mms(eng):
            eng.wait_ge(s_w.h, w_mark)
            idx = 0
            for qc in range(NQ):
                px.wait(eng, x_marks[qc])
                for tt in range(4):
                    for oc in range(NOC):
                        if idx >= 2:
                            eng.wait_ge(s_e.h, e_plan[idx - 2])
                        p = ps[idx % 2]
                        if has_bias:
                            nc.tensor.matmul(p[:], cst["ones_row_f32"],
                                             bvrow_sb[:, oc * QCW:(oc + 1) * QCW],
                                             start=True, stop=False)
                        for dt in range(NT):
                            inst = nc.tensor.matmul(
                                p[:], xcb[qc % 2][:, dt * QCW + tt * 128: dt * QCW + (tt + 1) * 128],
                                w_sb[:, dt * outd + oc * QCW: dt * outd + (oc + 1) * QCW],
                                start=(dt == 0 and not has_bias), stop=(dt == NT - 1))
                        s_mm.inc(inst, 1)
                        idx += 1

        def evac(eng):
            idx = 0
            for qc in range(NQ):
                for tt in range(4):
                    for oc in range(NOC):
                        eng.wait_ge(s_mm.h, mm_plan[idx])
                        if idx >= 2:
                            po.wait(eng, o_mark(idx - 2))
                        inst = eng.copy(ev[idx % 2][:], ps[idx % 2][:])
                        s_e.inc(inst, 1)
                        idx += 1

        b.sec("sync", w_loader)
        b.sec("scalar", w_cast)
        b.sec("sync", sync_io)
        b.sec("tensor", mms)
        b.sec("scalar", evac)
    return po.gates()



def emit_attention(b, tag, q_dram, k_dram, v_dram, kv_of_head, TQ, TK, scale,
                   shift, pf_dram, nheads, ready, cst):
    nc = b.nc
    NQ = TQ // QCW
    NKT = TK // 128
    pld = Pipe(b, "at_ld")
    po = Pipe(b, "at_o")
    s_s = b.sem("at_s")
    s_p = b.sem("at_p")
    s_e = b.sem("at_e")
    s_d = b.sem("at_d")
    s_bc = b.sem("at_bc")
    s_v = b.sem("at_v")
    s_sm = b.sem("at_sm")

    with ExitStack() as ctx:
        NSL = 5
        k_sb = [ctx.enter_context(nc.sbuf_tensor(f"{tag}_k{i}", [128, TK], BF16)) for i in range(2)]
        v_sb = [ctx.enter_context(nc.sbuf_tensor(f"{tag}_v{i}", [128, TK], BF16)) for i in range(2)]
        q_sb = [ctx.enter_context(nc.sbuf_tensor(f"{tag}_q{i}", [128, TQ], BF16)) for i in range(2)]
        a_all = ctx.enter_context(nc.sbuf_tensor(f"{tag}_a", [128, NSL * QCW], BF16))
        a_sb = [a_all[:, i * QCW:(i + 1) * QCW] for i in range(NSL)]
        r_sb = ctx.enter_context(nc.sbuf_tensor(f"{tag}_r", [1, QCW], F32))
        rb_sb = ctx.enter_context(nc.sbuf_tensor(f"{tag}_rb", [128, QCW], F32))
        pf_sb = [ctx.enter_context(nc.sbuf_tensor(f"{tag}_pf{i}", [128, QCW], BF16)) for i in range(2)]
        acc = [ctx.enter_context(nc.sbuf_tensor(f"{tag}_acc{i}", [128, QCW], F32)) for i in range(2)]
        ps_s = ctx.enter_context(nc.psum_tensor(f"{tag}_pss", [128, NSL * QCW], F32))
        ps_av = ctx.enter_context(nc.psum_tensor(f"{tag}_pav", [128, QCW], F32))
        ps_sm = ctx.enter_context(nc.psum_tensor(f"{tag}_psm", [128, QCW], F32))
        ps_bc = ctx.enter_context(nc.psum_tensor(f"{tag}_pbc", [128, QCW], F32))

        s_plan, p_plan, v_plan, d_plan, bc_plan, o_plan = {}, {}, {}, {}, {}, {}
        cs, cp, cv, cd, cb = s_s.n, s_p.n, s_v.n, s_d.n, s_bc.n
        e_base = s_e.n
        e_done = lambda gi: e_base + gi + 1
        sm_base = s_sm.n
        sm_done = lambda u: sm_base + u + 1
        ob_base = [sc.n for sc in po.sems]
        for h in range(nheads):
            for qc in range(NQ):
                gi = h * NQ + qc
                for kt in range(NKT):
                    cs += 1; s_plan[(h, qc, kt)] = cs
                    cp += 1; p_plan[(h, qc, kt)] = cp
                    cv += 1; v_plan[(h, qc, kt)] = cv
                cd += 1; d_plan[(h, qc, "r")] = cd
                cb += 1; bc_plan[(h, qc)] = cb
                cd += 1; d_plan[(h, qc, "pf")] = cd
                o_plan[(h, qc)] = (gi % 2, ob_base[gi % 2] + 16 * (gi // 2 + 1))
        ld_marks = []

        def g2t(gi):
            return (gi // (NQ * NKT), (gi % (NQ * NKT)) // NKT, gi % NKT)

        def sync_io(eng):
            def load(h):
                kv = kv_of_head(h)
                pld.inc(eng.dma_start(out=k_sb[h % 2][:], in_=k_dram[kv]), h)
                pld.inc(eng.dma_start(out=v_sb[h % 2][:].rearrange("p (kt c) -> p kt c", kt=NKT),
                                      in_=v_dram[kv].rearrange("(kt p) c -> p kt c", p=128)), h)
                inst = eng.dma_start(out=q_sb[h % 2][:], in_=q_dram[h])
                ld_marks.append(pld.inc(inst, h))
            _waits(eng, ready)
            load(0)
            if nheads > 1:
                load(1)
            for h in range(nheads):
                for qc in range(NQ):
                    gi = h * NQ + qc
                    eng.wait_ge(s_d.h, d_plan[(h, qc, "pf")])
                    inst = eng.dma_start(out=pf_dram[h][:, qc * QCW:(qc + 1) * QCW],
                                         in_=pf_sb[gi % 2][:])
                    po.inc(inst, gi)
                if h + 2 < nheads:
                    load(h + 2)

        def pe(eng):
            nun = nheads * NQ

            def s_mm(gi):
                hh, qq, kk = g2t(gi)
                if hh * NQ + qq > 0 and kk == 0:
                    pld.wait(eng, ld_marks[hh])
                if gi >= NSL:
                    eng.wait_ge(s_e.h, e_done(gi - NSL))
                inst = nc.tensor.matmul(ps_s[:, (gi % NSL) * QCW:(gi % NSL + 1) * QCW],
                                        k_sb[hh % 2][:, kk * 128:(kk + 1) * 128],
                                        q_sb[hh % 2][:, qq * QCW:(qq + 1) * QCW],
                                        start=True, stop=True)
                s_s.inc(inst, 1)

            pld.wait(eng, ld_marks[0])
            s_mm(0)
            s_mm(1)
            for u in range(nun):
                h, qc = u // NQ, u % NQ
                if u > 0:
                    prev = ((u - 1) // NQ, (u - 1) % NQ)
                    eng.wait_ge(s_d.h, d_plan[(prev[0], prev[1], "pf")])
                for kt in range(NKT):
                    gi = u * NKT + kt
                    if gi + 2 < nun * NKT:
                        s_mm(gi + 2)
                    eng.wait_ge(s_e.h, e_done(gi))
                    inst = nc.tensor.matmul(ps_av[:], v_sb[h % 2][:, kt * 128:(kt + 1) * 128],
                                            a_sb[gi % NSL], start=(kt == 0), stop=(kt == NKT - 1))
                    s_p.inc(inst, 1)
                eng.wait_ge(s_v.h, v_plan[(h, qc, NKT - 1)])
                inst = nc.tensor.matmul(ps_sm[0:1, :], cst["ones_col_f32"], acc[u % 2][:],
                                        start=True, stop=True)
                s_sm.inc(inst, 1)
                eng.wait_ge(s_d.h, d_plan[(h, qc, "r")])
                inst = nc.tensor.matmul(ps_bc[:], cst["ones_row_f32"], r_sb[:], start=True, stop=True)
                s_bc.inc(inst, 1)

        def act(eng):
            for h in range(nheads):
                for qc in range(NQ):
                    for kt in range(NKT):
                        gi = (h * NQ + qc) * NKT + kt
                        sl = (gi % NSL) * QCW
                        eng.wait_ge(s_s.h, s_plan[(h, qc, kt)])
                        if gi >= NSL:
                            eng.wait_ge(s_p.h, p_plan[g2t(gi - NSL)])
                            eng.wait_ge(s_v.h, v_plan[g2t(gi - NSL)])
                        ebias = cst["nshift"] if shift else 0.0
                        inst = eng.activation(a_all[:, sl:sl + QCW], ps_s[:, sl:sl + QCW],
                                              AF.Exp, scale=scale, bias=ebias)
                        s_e.inc(inst, 1)

        def dve(eng):
            for h in range(nheads):
                for qc in range(NQ):
                    u = h * NQ + qc
                    for kt in range(NKT):
                        gi = u * NKT + kt
                        eng.wait_ge(s_e.h, e_done(gi))
                        if kt == 0:
                            if u >= 2:
                                eng.wait_ge(s_sm.h, sm_done(u - 2))
                            inst = eng.tensor_copy(acc[u % 2][:], a_sb[gi % NSL])
                        else:
                            inst = eng.tensor_tensor(acc[u % 2][:], acc[u % 2][:],
                                                     a_sb[gi % NSL], ALU.add)
                        s_v.inc(inst, 1)
                    eng.wait_ge(s_sm.h, sm_done(u))
                    inst = eng.reciprocal(r_sb[:], ps_sm[0:1, :])
                    s_d.inc(inst, 1)
                    eng.wait_ge(s_bc.h, bc_plan[(h, qc)])
                    eng.tensor_copy(rb_sb[:], ps_bc[:])
                    if u >= 2:
                        pv = u - 2
                        po.wait(eng, o_plan[(pv // NQ, pv % NQ)])
                    inst = eng.tensor_tensor(pf_sb[u % 2][:], ps_av[:], rb_sb[:], ALU.mult)
                    s_d.inc(inst, 1)

        b.sec("sync", sync_io)
        b.sec("tensor", pe)
        b.sec("scalar", act)
        b.sec("vector", dve)
    return po.gates()



def emit_out_proj(b, tag, wo_dram, bo_2d, pf_dram, nheads, resid, T, dst,
                  ready, cst):
    nc = b.nc
    NQ = T // QCW
    NTIL = NQ * NT
    s_w = b.sem("op_w")
    px = Pipe(b, "op_x")
    po = Pipe(b, "op_o")
    s_mm = b.sem("op_mm")
    s_e = b.sem("op_e")
    s_d = b.sem("op_d")

    resid_r = resid.rearrange("(dt p) t -> p dt t", p=128)
    dst_r = dst.rearrange("(dt p) t -> p dt t", p=128)

    with ExitStack() as ctx:
        wo_sb = ctx.enter_context(nc.sbuf_tensor(f"{tag}_wo", [128, nheads * D], BF16))
        wst = [ctx.enter_context(nc.sbuf_tensor(f"{tag}_wst{i}", [128, nheads * 64], F32)) for i in range(2)]
        bo_sb = ctx.enter_context(nc.sbuf_tensor(f"{tag}_bo", [128, NT], F32))
        pf_sb = [ctx.enter_context(nc.sbuf_tensor(f"{tag}_pf{i}", [128, nheads * QCW], BF16)) for i in range(2)]
        xr_sb = [ctx.enter_context(nc.sbuf_tensor(f"{tag}_xr{i}", [128, NT * QCW], F32)) for i in range(2)]
        tv = [ctx.enter_context(nc.sbuf_tensor(f"{tag}_tv{i}", [128, QCW], F32)) for i in range(2)]
        ov = [ctx.enter_context(nc.sbuf_tensor(f"{tag}_ov{i}", [128, QCW], F32)) for i in range(2)]
        ps = [ctx.enter_context(nc.psum_tensor(f"{tag}_ps{i}", [128, QCW], F32)) for i in range(2)]

        has_bias = bo_2d is not None
        NWC = D // 64
        pwst = Pipe(b, "op_wst")
        wst_marks = []
        w_base_cast = s_w.n
        w_mark = s_w.n + NWC + (16 if has_bias else 0)
        mm_plan = [s_mm.n + i + 1 for i in range(NTIL)]
        e_plan = [s_e.n + i + 1 for i in range(NTIL)]
        d_plan = [s_d.n + i + 1 for i in range(NTIL)]
        ob_base = [sc.n for sc in po.sems]
        o_mark = lambda i: (i % 2, ob_base[i % 2] + 16 * (i // 2 + 1))
        x_marks = []

        def w_loader(eng):
            _waits(eng, ready)
            if has_bias:
                eng.dma_start(out=bo_sb[:], in_=bo_2d).then_inc(s_w.h, 16)
                s_w.n += 16
            for ch in range(NWC):
                if ch >= 2:
                    eng.wait_ge(s_w.h, w_base_cast + (16 if has_bias else 0) + ch - 1)
                inst = eng.dma_start(out=wst[ch % 2][:].rearrange("p (hh o) -> p hh o", o=64),
                                     in_=wo_dram.rearrange("(hh p) o -> p hh o", p=128)[:, :, ch * 64:(ch + 1) * 64])
                wst_marks.append(pwst.inc(inst, ch))

        def w_cast(eng):
            for ch in range(NWC):
                pwst.wait(eng, wst_marks[ch])
                inst = eng.copy(wo_sb[:].rearrange("p (hh o) -> p hh o", o=D)[:, :, ch * 64:(ch + 1) * 64],
                                wst[ch % 2][:].rearrange("p (hh o) -> p hh o", o=64))
                s_w.inc(inst, 1)

        def sync_io(eng):
            def load(qc):
                px.inc(eng.dma_start(out=pf_sb[qc % 2][:].rearrange("p (hh c) -> p hh c", hh=nheads),
                                     in_=pf_dram.rearrange("hh p t -> p hh t")[:, :, qc * QCW:(qc + 1) * QCW]), qc)
                inst = eng.dma_start(out=xr_sb[qc % 2][:].rearrange("p (dt c) -> p dt c", dt=NT),
                                     in_=resid_r[:, :, qc * QCW:(qc + 1) * QCW])
                x_marks.append(px.inc(inst, qc))
            _waits(eng, ready)
            load(0)
            if NQ > 1:
                load(1)
            idx = 0
            for qc in range(NQ):
                for dt in range(NT):
                    eng.wait_ge(s_d.h, d_plan[idx])
                    inst = eng.dma_start(out=dst_r[:, dt, qc * QCW:(qc + 1) * QCW], in_=ov[idx % 2][:])
                    po.inc(inst, idx)
                    idx += 1
                if qc + 2 < NQ:
                    load(qc + 2)

        def mms(eng):
            eng.wait_ge(s_w.h, w_mark)
            idx = 0
            for qc in range(NQ):
                px.wait(eng, x_marks[qc])
                for dt in range(NT):
                    if idx >= 2:
                        eng.wait_ge(s_e.h, e_plan[idx - 2])
                    p = ps[idx % 2]
                    for hh in range(nheads):
                        inst = nc.tensor.matmul(
                            p[:], wo_sb[:, hh * D + dt * 128: hh * D + (dt + 1) * 128],
                            pf_sb[qc % 2][:, hh * QCW:(hh + 1) * QCW],
                            start=(hh == 0), stop=(hh == nheads - 1))
                    s_mm.inc(inst, 1)
                    idx += 1

        def evac(eng):
            idx = 0
            for qc in range(NQ):
                for dt in range(NT):
                    eng.wait_ge(s_mm.h, mm_plan[idx])
                    if idx >= 2:
                        eng.wait_ge(s_d.h, d_plan[idx - 2])
                    bias = bo_sb[:, dt:dt + 1] if has_bias else 0.0
                    fn = AF.Identity if has_bias else AF.Copy
                    inst = eng.activation(tv[idx % 2][:], ps[idx % 2][:], fn, bias=bias)
                    s_e.inc(inst, 1)
                    idx += 1

        def resid_dve(eng):
            idx = 0
            for qc in range(NQ):
                for dt in range(NT):
                    eng.wait_ge(s_e.h, e_plan[idx])
                    if idx >= 2:
                        po.wait(eng, o_mark(idx - 2))
                    inst = eng.tensor_tensor(ov[idx % 2][:], tv[idx % 2][:],
                                             xr_sb[qc % 2][:, dt * QCW:(dt + 1) * QCW], ALU.add)
                    s_d.inc(inst, 1)
                    idx += 1

        b.sec("sync", w_loader)
        b.sec("scalar", w_cast)
        b.sec("sync", sync_io)
        b.sec("tensor", mms)
        b.sec("scalar", evac)
        b.sec("vector", resid_dve)
    return po.gates()



def emit_gate_up(b, tag, wgu_d, xln_d, T, hid, ready, w_ready):
    nc = b.nc
    NQ = T // QCW
    NP = I_FFN // 128
    GC = 2
    NCH = NP // GC
    NTIL = NP * NQ
    pw = Pipe(b, "gu_w")
    s_x = b.sem("gu_x")
    po = Pipe(b, "gu_o")
    s_mm = b.sem("gu_mm")
    s_si = b.sem("gu_si")
    s_h = b.sem("gu_h")

    xln_r = xln_d.rearrange("(dt p) t -> p dt t", p=128)
    wgu_r = wgu_d.rearrange("(kt p) o -> p kt o", p=128)

    with ExitStack() as ctx:
        xln_sb = ctx.enter_context(nc.sbuf_tensor(f"{tag}_xln", [128, NT * T], BF16))
        wg = [ctx.enter_context(nc.sbuf_tensor(f"{tag}_wg{i}", [128, NT * 128 * GC], BF16)) for i in range(2)]
        wu = [ctx.enter_context(nc.sbuf_tensor(f"{tag}_wu{i}", [128, NT * 128 * GC], BF16)) for i in range(2)]
        sil = [ctx.enter_context(nc.sbuf_tensor(f"{tag}_si{i}", [128, QCW], F32)) for i in range(2)]
        hv = [ctx.enter_context(nc.sbuf_tensor(f"{tag}_hv{i}", [128, QCW], BF16)) for i in range(2)]
        ps_g = [ctx.enter_context(nc.psum_tensor(f"{tag}_pg{i}", [128, QCW], F32)) for i in range(2)]
        ps_u = [ctx.enter_context(nc.psum_tensor(f"{tag}_pu{i}", [128, QCW], F32)) for i in range(2)]

        x_mark = s_x.n + 16
        mm_plan = [s_mm.n + i + 1 for i in range(NTIL)]
        si_plan = [s_si.n + i + 1 for i in range(NTIL)]
        h_plan = [s_h.n + i + 1 for i in range(NTIL)]
        ob_base = [sc.n for sc in po.sems]
        o_mark = lambda i: (i % 2, ob_base[i % 2] + 16 * (i // 2 + 1))
        w_marks = []

        def sync_io(eng):
            _waits(eng, ready)
            eng.dma_start(out=xln_sb[:].rearrange("p (dt t) -> p dt t", t=T),
                          in_=xln_r).then_inc(s_x.h, 16)
            s_x.n += 16
            idx = 0
            for ch in range(NCH):
                for j in range(GC):
                    g = ch * GC + j
                    for qc in range(NQ):
                        eng.wait_ge(s_h.h, h_plan[idx])
                        inst = eng.dma_start(out=hid[g * 128:(g + 1) * 128, qc * QCW:(qc + 1) * QCW],
                                             in_=hv[idx % 2][:])
                        po.inc(inst, idx)
                        idx += 1

        def w_loader(eng):
            _waits(eng, w_ready)
            _waits(eng, ready)
            for ch in range(NCH):
                if ch >= 2:
                    eng.wait_ge(s_mm.h, mm_plan[(ch - 1) * GC * NQ - 1])
                cs = slice(ch * GC * 128, (ch + 1) * GC * 128)
                us = slice(I_FFN + ch * GC * 128, I_FFN + (ch + 1) * GC * 128)
                pw.inc(eng.dma_start(out=wg[ch % 2][:].rearrange("p (kt o) -> p kt o", kt=NT),
                                     in_=wgu_r[:, :, cs]), ch)
                inst = eng.dma_start(out=wu[ch % 2][:].rearrange("p (kt o) -> p kt o", kt=NT),
                                     in_=wgu_r[:, :, us])
                w_marks.append(pw.inc(inst, ch))

        def mms(eng):
            eng.wait_ge(s_x.h, x_mark)
            idx = 0
            for ch in range(NCH):
                pw.wait(eng, w_marks[ch])
                for j in range(GC):
                    for qc in range(NQ):
                        if idx >= 2:
                            eng.wait_ge(s_h.h, h_plan[idx - 2])
                            eng.wait_ge(s_si.h, si_plan[idx - 2])
                        pg, pu = ps_g[idx % 2], ps_u[idx % 2]
                        for kt in range(NT):
                            nc.tensor.matmul(pg[:], wg[ch % 2][:, kt * GC * 128 + j * 128: kt * GC * 128 + (j + 1) * 128],
                                             xln_sb[:, kt * T + qc * QCW: kt * T + (qc + 1) * QCW],
                                             start=(kt == 0), stop=(kt == NT - 1))
                        for kt in range(NT):
                            inst = nc.tensor.matmul(pu[:], wu[ch % 2][:, kt * GC * 128 + j * 128: kt * GC * 128 + (j + 1) * 128],
                                                    xln_sb[:, kt * T + qc * QCW: kt * T + (qc + 1) * QCW],
                                                    start=(kt == 0), stop=(kt == NT - 1))
                        s_mm.inc(inst, 1)
                        idx += 1

        def silu_act(eng):
            for idx in range(NTIL):
                eng.wait_ge(s_mm.h, mm_plan[idx])
                if idx >= 2:
                    eng.wait_ge(s_h.h, h_plan[idx - 2])
                inst = eng.activation(sil[idx % 2][:], ps_g[idx % 2][:], AF.Silu)
                s_si.inc(inst, 1)

        def mult_dve(eng):
            for idx in range(NTIL):
                eng.wait_ge(s_si.h, si_plan[idx])
                if idx >= 2:
                    po.wait(eng, o_mark(idx - 2))
                inst = eng.tensor_tensor(hv[idx % 2][:], sil[idx % 2][:], ps_u[idx % 2][:], ALU.mult)
                s_h.inc(inst, 1)

        b.sec("sync", sync_io)
        b.sec("gpsimd", w_loader)
        b.sec("tensor", mms)
        b.sec("scalar", silu_act)
        b.sec("vector", mult_dve)
    return po.gates()



def emit_down(b, tag, wdn_d, hid, T, resid, dst, hid_ready, w_ready, cst):
    nc = b.nc
    NQ = T // QCW
    NKT = I_FFN // 128
    NTIL = NQ * NT
    pw = Pipe(b, "dn_w")
    s_h = b.sem("dn_h")
    po = Pipe(b, "dn_o")
    s_mm = b.sem("dn_mm")
    s_e = b.sem("dn_e")
    s_q = b.sem("dn_q")
    s_ms = b.sem("dn_ms")
    s_r1 = b.sem("dn_r1")
    s_r2 = b.sem("dn_r2")
    s_bc = b.sem("dn_bc")
    s_f = b.sem("dn_f")

    resid_r = resid.rearrange("(dt p) t -> p dt t", p=128)
    dst_r = dst.rearrange("(dt p) t -> p dt t", p=128)

    def hrow(t):
        return t, 0, 128

    with ExitStack() as ctx:
        WCH = 256
        NW = D // WCH
        h_sb = ctx.enter_context(nc.sbuf_tensor(f"{tag}_hs", [128, 44 * QCW], BF16))
        w_sb = [ctx.enter_context(nc.sbuf_tensor(f"{tag}_ws{i}", [128, 44 * WCH], BF16)) for i in range(2)]
        y_sb = ctx.enter_context(nc.sbuf_tensor(f"{tag}_y", [128, NT * QCW], F32))
        sqv = [ctx.enter_context(nc.sbuf_tensor(f"{tag}_sq{i}", [128, QCW], BF16)) for i in range(2)]
        ms_t = ctx.enter_context(nc.sbuf_tensor(f"{tag}_mst", [1, 2 * QCW], F32))
        rr_t = ctx.enter_context(nc.sbuf_tensor(f"{tag}_rrt", [1, 2 * QCW], F32))
        xr_sb = [ctx.enter_context(nc.sbuf_tensor(f"{tag}_xr{i}", [128, QCW], F32)) for i in range(2)]
        tv = [ctx.enter_context(nc.sbuf_tensor(f"{tag}_tv{i}", [128, QCW], F32)) for i in range(2)]
        ov = [ctx.enter_context(nc.sbuf_tensor(f"{tag}_ov{i}", [128, QCW], F32)) for i in range(2)]
        ps_y = [ctx.enter_context(nc.psum_tensor(f"{tag}_py{i}", [128, QCW], F32)) for i in range(2)]
        ps_ms = ctx.enter_context(nc.psum_tensor(f"{tag}_pms", [128, QCW], F32))
        ps_bc = ctx.enter_context(nc.psum_tensor(f"{tag}_pbc", [128, QCW], F32))

        h_base = s_h.n
        mm_plan = [s_mm.n + i + 1 for i in range(NTIL)]
        e_plan = [s_e.n + i + 1 for i in range(NTIL)]
        q_plan = [s_q.n + i + 1 for i in range(NTIL)]
        ms_plan = [s_ms.n + i + 1 for i in range(NTIL)]
        f_plan = [s_f.n + i + 1 for i in range(NTIL)]
        ob_base = [sc.n for sc in po.sems]
        o_mark = lambda i: (i % 2, ob_base[i % 2] + 16 * (i // 2 + 1))
        r1_base, r2_base = s_r1.n, s_r2.n
        r1_done = lambda qc: r1_base + qc + 1
        r2_done = lambda qc: r2_base + qc + 1
        bc_plan = [s_bc.n + qc + 1 for qc in range(NQ)]
        w_marks = []

        pxr = Pipe(b, "dn_xrl")
        xrb_base = [sc.n for sc in pxr.sems]
        xr_mark = lambda i: (i % 2, xrb_base[i % 2] + 16 * (i // 2 + 1))

        def sync_io(eng):
            def load(qc):
                eng.dma_start(out=h_sb[:].rearrange("p (blk c) -> p blk c", blk=44),
                              in_=hid.rearrange("(blk p) t -> p blk t", p=128)[:, :, qc * QCW:(qc + 1) * QCW]
                              ).then_inc(s_h.h, 16)
                s_h.n += 16

            def load_w(idx):
                if idx >= 2:
                    eng.wait_ge(s_mm.h, mm_plan[(idx - 1) * 2 - 1])
                dtc = idx % NW
                cs = slice(dtc * WCH, (dtc + 1) * WCH)
                inst = eng.dma_start(out=w_sb[idx % 2][:].rearrange("p (blk c) -> p blk c", blk=44),
                                     in_=wdn_d.rearrange("(blk p) o -> p blk o", p=128)[:, :, cs])
                w_marks.append(pw.inc(inst, idx))

            def load_xr(gi):
                qc, dt = gi // NT, gi % NT
                pxr.inc(eng.dma_start(out=xr_sb[gi % 2][:],
                                      in_=resid_r[:, dt, qc * QCW:(qc + 1) * QCW]), gi)
            _waits(eng, hid_ready)
            _waits(eng, w_ready)
            load(0)
            if DOWN_LITE != 2:
                load_xr(0)
                load_xr(1)
            idx = 0
            for qc in range(NQ):
                for dtc in range(NW):
                    load_w(qc * NW + dtc)
                for dt in range(NT):
                    if DOWN_LITE == 2:
                        eng.wait_ge(s_e.h, e_plan[idx])
                        inst = eng.dma_start(out=dst_r[:, dt, qc * QCW:(qc + 1) * QCW],
                                             in_=y_sb[:, dt * QCW:(dt + 1) * QCW])
                        po.inc(inst, idx)
                        idx += 1
                        continue
                    eng.wait_ge(s_f.h, f_plan[idx])
                    inst = eng.dma_start(out=dst_r[:, dt, qc * QCW:(qc + 1) * QCW], in_=ov[idx % 2][:])
                    po.inc(inst, idx)
                    if idx + 2 < NQ * NT:
                        load_xr(idx + 2)
                    idx += 1
                if qc + 1 < NQ:
                    load(qc + 1)

        def pe_sec(eng):
            def ms_mm(qc, dt):
                eng.wait_ge(s_q.h, q_plan[qc * NT + dt])
                if dt == 0 and qc >= 1:
                    eng.wait_ge(s_r1.h, r1_done(qc - 1))
                inst = nc.tensor.matmul(ps_ms[0:1, :], cst["ones_col_bf"], sqv[(qc * NT + dt) % 2][:],
                                        start=(dt == 0), stop=(dt == NT - 1))
                s_ms.inc(inst, 1)
            idx = 0
            for qc in range(NQ):
                eng.wait_ge(s_h.h, h_base + 16 * (qc + 1))
                for dtc in range(NW):
                    pw.wait(eng, w_marks[qc * NW + dtc])
                    for di in range(WCH // 128):
                        dt = dtc * (WCH // 128) + di
                        if idx >= 2:
                            eng.wait_ge(s_e.h, e_plan[idx - 2])
                        p = ps_y[idx % 2]
                        for t in range(NKT):
                            blk, basep, rows = hrow(t)
                            inst = nc.tensor.matmul(
                                p[:],
                                w_sb[(qc * NW + dtc) % 2][basep:basep + rows,
                                                          blk * WCH + di * 128: blk * WCH + (di + 1) * 128],
                                h_sb[basep:basep + rows, blk * QCW:(blk + 1) * QCW],
                                start=(t == 0), stop=(t == NKT - 1))
                        s_mm.inc(inst, 1)
                        if DOWN_LITE == 0 and dt >= 2:
                            ms_mm(qc, dt - 2)
                        idx += 1
                if DOWN_LITE == 0:
                    ms_mm(qc, NT - 2)
                    ms_mm(qc, NT - 1)
                    eng.wait_ge(s_r2.h, r2_done(qc))
                    if qc >= 1:
                        eng.wait_ge(s_f.h, f_plan[qc * NT - 1])
                    inst = nc.tensor.matmul(ps_bc[:], cst["ones_row_f32"],
                                            rr_t[:, (qc % 2) * QCW:(qc % 2 + 1) * QCW], start=True, stop=True)
                    s_bc.inc(inst, 1)

        def act_sec(eng):
            if DOWN_LITE:
                return
            idx = 0
            for qc in range(NQ):
                for dt in range(NT):
                    eng.wait_ge(s_e.h, e_plan[idx])
                    if idx >= 2:
                        eng.wait_ge(s_ms.h, ms_plan[idx - 2])
                    inst = eng.activation(sqv[idx % 2][:], y_sb[:, dt * QCW:(dt + 1) * QCW], AF.Square)
                    s_q.inc(inst, 1)
                    idx += 1
                eng.wait_ge(s_r1.h, r1_done(qc))
                if qc >= 2:
                    eng.wait_ge(s_bc.h, bc_plan[qc - 2])
                qs = slice((qc % 2) * QCW, (qc % 2 + 1) * QCW)
                eng.activation(rr_t[:, qs], ms_t[:, qs], AF.Ln, bias=cst["eps_rms"])
                inst = eng.activation(rr_t[:, qs], rr_t[:, qs], AF.Exp, scale=-0.5)
                s_r2.inc(inst, 1)

        def dve_sec(eng):
            idx = 0
            for qc in range(NQ):
                for dt in range(NT):
                    eng.wait_ge(s_mm.h, mm_plan[idx])
                    if qc >= 1:
                        if DOWN_LITE == 2:
                            po.wait(eng, o_mark((qc - 1) * NT + dt))
                        else:
                            eng.wait_ge(s_f.h, f_plan[(qc - 1) * NT + dt])
                            eng.wait_ge(s_q.h, q_plan[(qc - 1) * NT + dt])
                    inst = eng.tensor_copy(y_sb[:, dt * QCW:(dt + 1) * QCW], ps_y[idx % 2][:])
                    s_e.inc(inst, 1)
                    idx += 1
                if DOWN_LITE == 0:
                    eng.wait_ge(s_ms.h, ms_plan[qc * NT + NT - 1])
                    if qc >= 2:
                        eng.wait_ge(s_r2.h, r2_done(qc - 2))
                    qs = slice((qc % 2) * QCW, (qc % 2 + 1) * QCW)
                    inst = eng.tensor_scalar(ms_t[:, qs], ps_ms[0:1, :], 1.0 / D, None, ALU.mult)
                    s_r1.inc(inst, 1)
                    eng.wait_ge(s_bc.h, bc_plan[qc])
                if DOWN_LITE == 2:
                    continue
                for dt in range(NT):
                    gi = qc * NT + dt
                    sl = slice(dt * QCW, (dt + 1) * QCW)
                    if gi >= 2:
                        po.wait(eng, o_mark(gi - 2))
                    pxr.wait(eng, xr_mark(gi))
                    if DOWN_LITE == 0:
                        eng.tensor_tensor(tv[gi % 2][:], y_sb[:, sl], ps_bc[:], ALU.mult)
                        inst = eng.tensor_tensor(ov[gi % 2][:], tv[gi % 2][:], xr_sb[gi % 2][:], ALU.add)
                    else:
                        inst = eng.tensor_tensor(ov[gi % 2][:], y_sb[:, sl], xr_sb[gi % 2][:], ALU.add)
                    s_f.inc(inst, 1)

        b.sec("sync", sync_io)
        b.sec("tensor", pe_sec)
        b.sec("scalar", act_sec)
        b.sec("vector", dve_sec)
    return po.gates()



def build_program(phases=PHASES, taps=DEBUG_TAPS):
    nc = bass.Bass("TRN2", target_bir_lowering=False)

    def dram_in(name, shape, dt):
        return nc.dram_tensor(name, shape, dt, kind="ExternalInput").ap()

    def dram_int(name, shape, dt, tap=None):
        kind = "ExternalOutput" if (tap is not None and tap in taps) else "Internal"
        return nc.dram_tensor(name, shape, dt, kind=kind).ap()

    x0 = dram_in("x0", [D, T_PRE], F32)
    wq_a = dram_in("wq_a", [D, D], F32)
    wk_a = dram_in("wk_a", [D, D], F32)
    wv_a = dram_in("wv_a", [D, D], F32)
    wo_a = dram_in("wo_a", [D, D], F32)
    bq_a = dram_in("bq_a", [128, NT], F32)
    bk_a = dram_in("bk_a", [128, NT], F32)
    bv_r = dram_in("bv_r", [1, D], F32)
    bo_a = dram_in("bo_a", [128, NT], F32)
    wq_b = dram_in("wq_b", [D, D], F32)
    wk_b = dram_in("wk_b", [D, KVH * DH], F32)
    wv_b = dram_in("wv_b", [D, KVH * DH], F32)
    wo_b = dram_in("wo_b", [D, D], F32)
    gupT = dram_in("gupT", [256, GU], I32)
    sc_gu = dram_in("sc_gu", [256, GU], F32)
    downT = dram_in("downT", [704, D], I32)
    sc_dn = dram_in("sc_dn", [704, D], F32)
    rope_cos = dram_in("rope_cos", [128, T_PRE], F32)
    rope_sin = dram_in("rope_sin", [128, T_PRE], F32)
    pswap_in = dram_in("pswap", [128, 128], F32)

    out_fm = nc.dram_tensor("out_fm", [D, T_SUF], F32, kind="ExternalOutput").ap()

    x1 = dram_int("x1", [D, T_PRE], F32, "x1")
    x2 = dram_int("x2", [D, T_PRE], F32, "x2")
    x3 = dram_int("x3", [D, T_SUF], F32, "x3")
    xln = dram_int("xln", [D, T_PRE], BF16, "xln1")
    wgu_d = dram_int("wgu_d", [D, GU], BF16, "wgu")
    wdn_d = dram_int("wdn_d", [I_FFN, D], BF16, "wdn")
    hid = dram_int("hid", [I_FFN, T_PRE], BF16, "hid")
    ka = dram_int("ka", [H, DH, T_PRE], BF16, "ka")
    va = dram_int("va", [H, T_PRE, DH], BF16, "va")
    qa = dram_int("qa", [H, DH, T_PRE], BF16, "qa")
    pfa = dram_int("pfa", [H, DH, T_PRE], BF16, "pfa")
    kb = dram_int("kb", [KVH, DH, T_PRE], BF16)
    vb = dram_int("vb", [KVH, T_PRE, DH], BF16)
    qb = dram_int("qb", [H, DH, T_SUF], BF16)
    pfb = dram_int("pfb", [H, DH, T_SUF], BF16)

    with ExitStack() as ctx:
        block = ctx.enter_context(nc.Block())
        b = B(nc, ctx, block)

        ones_col_bf = ctx.enter_context(nc.sbuf_tensor("ones_col", [128, 1], BF16))
        ones_col_f32 = ctx.enter_context(nc.sbuf_tensor("ones_colf", [128, 1], F32))
        ones_row_f32 = ctx.enter_context(nc.sbuf_tensor("ones_row", [1, 128], F32))
        ctpool = ctx.enter_context(nc.sbuf_tensor("ctpool", [128, 3], F32))
        pswap_sb = ctx.enter_context(nc.sbuf_tensor("pswap_sb", [128, 128], BF16))
        s_initc = b.sem("initc")
        s_initd = b.sem("initd")

        def init_consts(eng):
            eng.memset(ones_col_bf[:], 1.0)
            eng.memset(ones_col_f32[:], 1.0)
            eng.memset(ctpool[:, 0:1], LN_EPS)
            eng.memset(ctpool[:, 1:2], RMS_EPS)
            eng.memset(ctpool[:, 2:3], -EXP_SHIFT_B)
            inst = eng.memset(ones_row_f32[:], 1.0)
            s_initc.inc(inst, 1)

        def init_loads(eng):
            eng.dma_start(out=pswap_sb[:], in_=pswap_in).then_inc(s_initd.h, 16)
            s_initd.n += 16

        b.sec("vector", init_consts)
        b.sec("gpsimd", init_loads)

        def await_init(eng):
            eng.wait_ge(s_initc.h, 1)
            eng.wait_ge(s_initd.h, 16)
        b.sec("tensor", await_init)
        b.sec("scalar", await_init)
        b.sec("vector", await_init)
        b.sec("sync", await_init)

        cst = dict(ones_col_bf=ones_col_bf[:], ones_col_f32=ones_col_f32[:],
                   ones_row_f32=ones_row_f32[:],
                   eps_ln=ctpool[0:1, 0:1], eps_rms=ctpool[0:1, 1:2],
                   nshift=ctpool[:, 2:3])

        deq_done = None
        last = None
        if "p1" in phases or "p1ln" in phases:
            ln1 = emit_ln(b, "l1", x0, T_PRE, xln, cst, None)
            last = ln1
        if "deq" in phases:
            deq_done = emit_dequant(b, gupT, sc_gu, downT, sc_dn, wgu_d, wdn_d)
        if "p1" in phases or "p1kqv" in phases:
            ka_done = emit_fm_proj(b, "ka", wk_a, D, bk_a, xln, T_PRE,
                                   [ka[h] for h in range(H)], ln1, cst=cst)
            qa_done = emit_fm_proj(b, "qa", wq_a, D, bq_a, xln, T_PRE,
                                   [qa[h] for h in range(H)], ka_done, cst=cst)
            va_done = emit_v_proj(b, "va", wv_a, D, bv_r, xln, T_PRE,
                                  [va[h] for h in range(H)], qa_done, cst)
            last = va_done
        if "p1" in phases or "p1att" in phases:
            ata_done = emit_attention(b, "at_a", [qa[h] for h in range(H)],
                                      [ka[h] for h in range(H)], [va[h] for h in range(H)],
                                      lambda h: h, T_PRE, T_PRE, 1.0 / DH, 0.0,
                                      [pfa[h] for h in range(H)], H,
                                      ka_done + qa_done + va_done, cst)
            p1_done = emit_out_proj(b, "wo_a", wo_a, bo_a, pfa, H, x0, T_PRE, x1,
                                    ata_done, cst)
            last = p1_done

        if "p2" in phases or "p2gu" in phases or "p2dn" in phases:
            ln2 = emit_ln(b, "l2", x1, T_PRE, xln, cst, last)
            last = ln2
        if "p2" in phases or "p2gu" in phases or "p2dn" in phases:
            gu2 = emit_gate_up(b, "gu2", wgu_d, xln, T_PRE, hid, ln2, deq_done)
            last = gu2
        if "p2" in phases or "p2dn" in phases:
            p2_done = emit_down(b, "dn2", wdn_d, hid, T_PRE, x1, x2, gu2, deq_done, cst)
            last = p2_done

        if "p3" in phases:
            ln3 = emit_ln(b, "l3", x2, T_PRE, xln, cst, last)
            with ExitStack() as p3ctx:
                cos_sb = p3ctx.enter_context(nc.sbuf_tensor("cos_sb", [128, T_PRE], F32))
                sin_sb = p3ctx.enter_context(nc.sbuf_tensor("sin_sb", [128, T_PRE], F32))
                s_rt = b.sem("ropetab")
                rt_gate_val = ln3

                def rope_loads(eng):
                    _waits(eng, rt_gate_val)
                    eng.dma_start(out=cos_sb[:], in_=rope_cos).then_inc(s_rt.h, 16)
                    s_rt.n += 16
                    eng.dma_start(out=sin_sb[:], in_=rope_sin).then_inc(s_rt.h, 16)
                    s_rt.n += 16
                b.sec("gpsimd", rope_loads)
                rope = dict(pswap=pswap_sb[:], cos=cos_sb[:], sin=sin_sb[:])
                kb_done = emit_fm_proj(b, "kb", wk_b, KVH * DH, None, xln, T_PRE,
                                       [kb[h] for h in range(KVH)], ln3 + [(s_rt, 32)],
                                       rope=rope, cst=cst)
                vb_done = emit_v_proj(b, "vb", wv_b, KVH * DH, None, xln, T_PRE,
                                      [vb[h] for h in range(KVH)], kb_done, cst)
                qb_done = emit_fm_proj(b, "qb", wq_b, D, None, xln, T_SUF,
                                       [qb[h] for h in range(H)], vb_done, rope=rope, cst=cst)
                atb_done = emit_attention(b, "at_b", [qb[h] for h in range(H)],
                                          [kb[h] for h in range(KVH)], [vb[h] for h in range(KVH)],
                                          lambda h: h % 4, T_SUF, T_PRE, 1.0 / float(np.sqrt(DH)),
                                          EXP_SHIFT_B, [pfb[h] for h in range(H)], H,
                                          kb_done + vb_done + qb_done, cst)
            p3_done = emit_out_proj(b, "wo_b", wo_b, None, pfb, H, x2[:, 0:T_SUF], T_SUF,
                                    x3, atb_done, cst)
            last = p3_done

        if "p4" in phases:
            xln_s = xln[:, 0:T_SUF]
            hid_s = hid[:, 0:T_SUF]
            ln4 = emit_ln(b, "l4", x3, T_SUF, xln_s, cst, last)
            gu4 = emit_gate_up(b, "gu4", wgu_d, xln_s, T_SUF, hid_s, ln4, deq_done)
            p4_done = emit_down(b, "dn4", wdn_d, hid_s, T_SUF, x3, out_fm, gu4,
                                deq_done, cst)
            last = p4_done

        fin = last if last is not None else deq_done
        fin_deq = deq_done

        def final_wait(eng):
            _waits(eng, fin)
            _waits(eng, fin_deq)

        b.sec("sync", final_wait)
    return nc



def _bias2d(v):
    return np.ascontiguousarray(np.asarray(v, np.float32).reshape(NT, 128).T)


def _prep_inputs(inp):
    f32 = np.float32
    x = np.asarray(inp["L0_attn_a_x"])
    hm = HMAP
    im_full = np.concatenate([IMAP, I_FFN + IMAP])

    shared = dict(
        wq_a=np.ascontiguousarray(np.asarray(inp["L0_attn_a_Wq"], f32)[hm, :]),
        wk_a=np.ascontiguousarray(np.asarray(inp["L0_attn_a_Wk"], f32)[hm, :]),
        wv_a=np.ascontiguousarray(np.asarray(inp["L0_attn_a_Wv"], f32)[hm, :]),
        wo_a=np.ascontiguousarray(np.asarray(inp["L0_attn_a_Wo"], f32)[:, hm]),
        bq_a=_bias2d(inp["L0_attn_a_bq"]),
        bk_a=_bias2d(inp["L0_attn_a_bk"]),
        bv_r=np.asarray(inp["L0_attn_a_bv"], f32).reshape(1, D),
        bo_a=_bias2d(np.asarray(inp["L0_attn_a_bo"], f32)[hm]),
        wq_b=np.ascontiguousarray(np.asarray(inp["L0_attn_b_W_q"], f32)[hm, :]),
        wk_b=np.ascontiguousarray(np.asarray(inp["L0_attn_b_W_k"], f32)[hm, :]),
        wv_b=np.ascontiguousarray(np.asarray(inp["L0_attn_b_W_v"], f32)[hm, :]),
        wo_b=np.ascontiguousarray(np.asarray(inp["L0_attn_b_W_o"], f32)[:, hm]),
    )
    gup = np.asarray(inp["L0_ffn_gate_up_packed"])
    gup_p = np.concatenate([gup[:I_FFN][IMAP], gup[I_FFN:][IMAP]])
    shared["gupT"] = np.ascontiguousarray(gup_p.T)
    shared["sc_gu"] = np.ascontiguousarray(
        np.repeat(np.asarray(inp["L0_ffn_gate_up_scales"]).astype(f32), 16, axis=0)[:, im_full])
    dnp = np.asarray(inp["L0_ffn_down_packed"])[hm, :]
    shared["downT"] = np.ascontiguousarray(dnp.T)
    shared["sc_dn"] = np.ascontiguousarray(
        np.repeat(np.asarray(inp["L0_ffn_down_scales"]).astype(f32), 16, axis=0)[:, hm])
    pswap = np.zeros((128, 128), f32)
    pswap[np.arange(128), np.arange(128) ^ 1] = 1.0
    shared["pswap"] = pswap

    in_maps = []
    for c in range(8):
        bidx, half = c // 2, c % 2
        xb = np.asarray(x[bidx])[:, hm]
        if half == 1:
            xb = np.roll(xb, -T_SUF, axis=0)
        m = dict(shared)
        m["x0"] = np.ascontiguousarray(xb.T.astype(f32))
        pos = (np.arange(T_PRE) + half * T_SUF) % T_PRE
        cidx = np.arange(64, dtype=np.float64)
        freqs = 1.0 / (10000.0 ** (cidx / 64.0))
        ang = pos[None, :] * freqs[:, None]
        cos_t = np.repeat(np.cos(ang), 2, axis=0).astype(f32)
        sin_t = np.repeat(np.sin(ang), 2, axis=0).astype(f32)
        sin_t[0::2, :] *= -1.0
        m["rope_cos"] = np.ascontiguousarray(cos_t)
        m["rope_sin"] = np.ascontiguousarray(sin_t)
        in_maps.append(m)
    return in_maps


def kernel(**inputs):
    in_maps = _prep_inputs(inputs)
    nc = build_program()
    res = run_bass_kernel_spmd(nc, in_maps, core_ids=list(range(8)))
    out = np.empty((4, T_PRE, D), np.float32)
    for c in range(8):
        bidx, half = c // 2, c % 2
        of = np.asarray(res.results[c]["out_fm"])
        sl = slice(half * T_SUF, (half + 1) * T_SUF)
        out[bidx, sl, :][:, HMAP] = of.T
    return out


# version 10 (final)
# speedup vs baseline: 1.2972x; 1.0636x over previous
import numpy as np
import ml_dtypes
from contextlib import ExitStack

import concourse.bass as bass
import concourse.mybir as mybir
from concourse.bass_utils import run_bass_kernel_spmd

F32 = mybir.dt.float32
BF16 = mybir.dt.bfloat16
I32 = mybir.dt.int32
U16 = mybir.dt.uint16
AF = mybir.ActivationFunctionType
ALU = mybir.AluOpType

D = 2048
NT = D // 128
H = 16
DH = 128
KVH = 4
I_FFN = 5632
GU = 2 * I_FFN
T_PRE = 4096
T_SUF = 2048
QCW = 512
LN_EPS = 1e-5
RMS_EPS = 1e-6
EXP_SHIFT_B = 4.0

PHASES = ("deq", "p1", "p2", "p3", "p4")
DEBUG_TAPS = ()
DOWN_LITE = 0


def _pi_h():
    hmap = np.empty(D, np.int64)
    for jb in range(2):
        for k in range(8):
            jj = np.arange(128)
            hmap[(jb * 8 + k) * 128 + jj] = 8 * (jb * 128 + jj) + k
    return hmap


def _pi_i():
    imap = np.empty(I_FFN, np.int64)
    off = 0
    tiles = []
    for jb2 in range(6):
        rows = 128 if jb2 < 5 else 64
        for k in range(8):
            jj = np.arange(rows)
            imap[off:off + rows] = 8 * (jb2 * 128 + jj) + k
            tiles.append((jb2, k, off, rows))
            off += rows
    assert off == I_FFN
    return imap, tiles


HMAP = _pi_h()
IMAP, DN_TILES = _pi_i()


class SemC:
    def __init__(self, ctx, nc, name):
        self.h = ctx.enter_context(nc.semaphore(name))
        self.n = 0

    def inc(self, inst, k):
        inst.then_inc(self.h, k)
        self.n += k
        return self.n


class Pipe:

    def __init__(self, b, name, nslots=2):
        self.sems = [b.sem(f"{name}_{i}") for i in range(nslots)]
        self.nslots = nslots

    def inc(self, inst, slot, k=16):
        sc = self.sems[slot % self.nslots]
        inst.then_inc(sc.h, k)
        sc.n += k
        return (slot % self.nslots, sc.n)

    def wait(self, eng, mark):
        slot, val = mark
        eng.wait_ge(self.sems[slot].h, val)

    def gates(self):
        return [(sc, sc.n) for sc in self.sems]


class B:
    def __init__(self, nc, ctx, block):
        self.nc = nc
        self.ctx = ctx
        self.block = block
        self.sems = {}

    def sem(self, name):
        if name not in self.sems:
            self.sems[name] = SemC(self.ctx, self.nc, name)
        return self.sems[name]

    def sec(self, engine_name, fn):
        getattr(self.block, engine_name)(fn)


def _waits(eng, ready):
    if ready is None:
        return
    if isinstance(ready, tuple):
        ready = [ready]
    for (s, v) in ready:
        if v > 0:
            eng.wait_ge(s.h, v)



def emit_dequant(b, gupT, sc_gu, downT, sc_dn, wgu_d, wdn_d):
    nc = b.nc
    pld = Pipe(b, "deq_ld")
    s_dv = b.sem("deq_dv")
    pst = Pipe(b, "deq_st")

    OC = 704
    jobs = []
    for jb in range(2):
        for oc in range(GU // OC):
            cs = slice(oc * OC, (oc + 1) * OC)
            jobs.append((128, OC, gupT[jb * 128:(jb + 1) * 128, cs],
                         sc_gu[jb * 128:(jb + 1) * 128, cs],
                         [(k, wgu_d[(jb * 8 + k) * 128:(jb * 8 + k + 1) * 128, cs])
                          for k in range(8)]))
    DC = 512
    for jb2 in range(6):
        rows = 128 if jb2 < 5 else 64
        for dc in range(D // DC):
            ds_ = slice(dc * DC, (dc + 1) * DC)
            planes = [(k, wdn_d[off:off + trows, ds_])
                      for (tjb2, k, off, trows) in DN_TILES if tjb2 == jb2]
            jobs.append((rows, DC, downT[jb2 * 128:jb2 * 128 + rows, ds_],
                         sc_dn[jb2 * 128:jb2 * 128 + rows, ds_], planes))

    nbuf = 2
    ctx = b.ctx
    mx_c = OC
    sbr = lambda name, shape, dt: ctx.enter_context(nc.sbuf_tensor(name, shape, dt, side="right"))
    slab = [sbr(f"dq_slab{i}", [128, mx_c], I32) for i in range(nbuf)]
    scl = [sbr(f"dq_scl{i}", [128, mx_c], F32) for i in range(nbuf)]
    t1 = sbr("dq_t1", [128, mx_c], I32)
    bM = sbr("dq_bM", [128, mx_c], U16)
    t4 = sbr("dq_t4", [128, mx_c], BF16)
    t5 = sbr("dq_t5", [128, mx_c], BF16)
    s32 = t1
    s16 = sbr("dq_s16", [128, mx_c], U16)
    t6 = t4.bitcast(U16) if False else sbr("dq_t6", [128, mx_c], U16)
    wout = [sbr(f"dq_w{i}", [128, mx_c], BF16) for i in range(2)]

    dv_base = s_dv.n
    dv_plan = [dv_base]
    acc = dv_base
    for (_, _, _, _, planes) in jobs:
        acc += len(planes)
        dv_plan.append(acc)
    n_planes = acc - dv_base

    ld_marks = []
    st_plan = []
    pc = 0
    for (rows, cols, _, _, planes) in jobs:
        for (k, dst) in planes:
            st_plan.append((wout[pc % 2], dst, rows, cols)); pc += 1

    st_bounds = [0]
    for (_, _, _, _, planes) in jobs:
        st_bounds.append(st_bounds[-1] + len(planes))

    def gp_io(eng):
        def store_job(j):
            for si in range(st_bounds[j], st_bounds[j + 1]):
                wo, dst, rows, cols = st_plan[si]
                eng.wait_ge(s_dv.h, dv_base + si + 1)
                pst.inc(eng.dma_start(out=dst, in_=wo[0:rows, 0:cols]), si)
        for ji, (rows, cols, slab_ap, sc_ap, planes) in enumerate(jobs):
            if ji >= 1:
                store_job(ji - 1)
            pld.inc(eng.dma_start(out=slab[ji % nbuf][0:rows, 0:cols], in_=slab_ap), ji)
            inst = eng.dma_start(out=scl[ji % nbuf][0:rows, 0:cols], in_=sc_ap)
            ld_marks.append(pld.inc(inst, ji))
        store_job(len(jobs) - 1)

    def decoder(eng):
        pcount = 0
        for ji, (rows, cols, slab_ap, sc_ap, planes) in enumerate(jobs):
            pld.wait(eng, ld_marks[ji])
            sl = slab[ji % nbuf][0:rows, 0:cols]
            sc = scl[ji % nbuf][0:rows, 0:cols]
            for (k, dst) in planes:
                wo = wout[pcount % 2][0:rows, 0:cols]
                if pcount >= 2:
                    pst.wait(eng, (pcount % 2, 16 * ((pcount - 2) // 2 + 1)))
                _t1, _bM, _t4, _t5 = (x[0:rows, 0:cols] for x in (t1, bM, t4, t5))
                _s32, _s16, _t6 = (x[0:rows, 0:cols] for x in (s32, s16, t6))
                sh = 6 - 4 * k
                if sh >= 0:
                    eng.tensor_scalar(_t1, sl, sh, 0x1C0, ALU.logical_shift_left, ALU.bitwise_and)
                else:
                    eng.tensor_scalar(_t1, sl, -sh, 0x1C0, ALU.logical_shift_right, ALU.bitwise_and)
                eng.tensor_scalar(_bM, _t1, 0x3F00, None, ALU.add)
                Bv = _bM.bitcast(BF16)
                eng.tensor_scalar(_t4, Bv, 2.0, -1.0, ALU.mult, ALU.add)
                eng.tensor_tensor(_t5, _t4, Bv, ALU.min)
                ssh = 12 - 4 * k
                if ssh >= 0:
                    eng.tensor_scalar(_s32, sl, ssh, 0x8000, ALU.logical_shift_left, ALU.bitwise_and)
                else:
                    eng.tensor_scalar(_s32, sl, -ssh, 0x8000, ALU.logical_shift_right, ALU.bitwise_and)
                eng.tensor_copy(_s16, _s32)
                eng.tensor_tensor(_t6, _t5.bitcast(U16), _s16, ALU.bitwise_or)
                inst = eng.tensor_tensor(wo, _t6.bitcast(BF16), sc, ALU.mult)
                s_dv.inc(inst, 1)
                pcount += 1

    b.sec("gpsimd", gp_io)
    b.sec("vector", decoder)
    return pst.gates()



def emit_ln(b, tag, src, T, xln_d, cst, ready):
    nc = b.nc
    NQ = T // QCW
    px = Pipe(b, "ln_x")
    po = Pipe(b, "ln_o")
    s_c = b.sem("ln_c")
    s_p = b.sem("ln_p")
    s_d = b.sem("ln_d")
    s_l = b.sem("ln_l")

    src_r = src.rearrange("(dt p) t -> p dt t", p=128)
    xln_r = xln_d.rearrange("(dt p) t -> p dt t", p=128)

    with ExitStack() as ctx:
        xc = [ctx.enter_context(nc.sbuf_tensor(f"{tag}_xc{i}", [128, NT * QCW], F32)) for i in range(2)]
        xb = [ctx.enter_context(nc.sbuf_tensor(f"{tag}_xb{i}", [128, NT * QCW], BF16)) for i in range(2)]
        sq = [ctx.enter_context(nc.sbuf_tensor(f"{tag}_sq{i}", [128, NT * QCW], BF16)) for i in range(2)]
        m_t = ctx.enter_context(nc.sbuf_tensor(f"{tag}_m", [1, 2 * QCW], F32))
        v_t = ctx.enter_context(nc.sbuf_tensor(f"{tag}_v", [1, 2 * QCW], F32))
        var_t = ctx.enter_context(nc.sbuf_tensor(f"{tag}_var", [1, 2 * QCW], F32))
        rstd_t = ctx.enter_context(nc.sbuf_tensor(f"{tag}_rs", [1, 2 * QCW], F32))
        mr_t = ctx.enter_context(nc.sbuf_tensor(f"{tag}_mr", [1, 2 * QCW], F32))
        tmp = [ctx.enter_context(nc.sbuf_tensor(f"{tag}_tm{i}", [128, QCW], F32)) for i in range(2)]
        ob = [ctx.enter_context(nc.sbuf_tensor(f"{tag}_ob{i}", [128, QCW], BF16)) for i in range(2)]
        ps_m = ctx.enter_context(nc.psum_tensor(f"{tag}_pm", [128, QCW], F32))
        ps_v = ctx.enter_context(nc.psum_tensor(f"{tag}_pv", [128, QCW], F32))
        ps_b1 = ctx.enter_context(nc.psum_tensor(f"{tag}_pb1", [128, QCW], F32))
        ps_b2 = ctx.enter_context(nc.psum_tensor(f"{tag}_pb2", [128, QCW], F32))

        c_base, p_base, d_base, l_base = s_c.n, s_p.n, s_d.n, s_l.n
        c_done = lambda qc: c_base + qc + 1
        _s1d, _s2d = {}, {}
        _pc = p_base
        _order = [("s1", 0)] + ([("s1", 1)] if NQ > 1 else [])
        for _q in range(NQ):
            _order.append(("s2", _q))
            if _q + 2 < NQ:
                _order.append(("s1", _q + 2))
        for (_k, _q) in _order:
            _pc += 1
            (_s1d if _k == "s1" else _s2d)[_q] = _pc
        s1_done = lambda qc: _s1d[qc]
        s2_done = lambda qc: _s2d[qc]
        v1_done = lambda qc: d_base + (2 + NT) * qc + 1
        v2_done = lambda qc: d_base + (2 + NT) * qc + 2
        v3_done = lambda qc, dt: d_base + (2 + NT) * qc + 2 + dt + 1
        a_done = lambda qc: l_base + qc + 1
        ob_base = [sc.n for sc in po.sems]
        o_mark = lambda gi: (gi % 2, ob_base[gi % 2] + 16 * (gi // 2 + 1))
        x_marks = []

        def sync_io(eng):
            def load(qc):
                inst = eng.dma_start(out=xc[qc % 2][:].rearrange("p (dt c) -> p dt c", dt=NT),
                                     in_=src_r[:, :, qc * QCW:(qc + 1) * QCW])
                x_marks.append(px.inc(inst, qc))
            _waits(eng, ready)
            load(0)
            if NQ > 1:
                load(1)
            for qc in range(NQ):
                for dt in range(NT):
                    gi = qc * NT + dt
                    eng.wait_ge(s_d.h, v3_done(qc, dt))
                    inst = eng.dma_start(out=xln_r[:, dt, qc * QCW:(qc + 1) * QCW],
                                         in_=ob[gi % 2][:])
                    po.inc(inst, gi)
                if qc + 2 < NQ:
                    load(qc + 2)

        def act_sec(eng):
            def casts(qc):
                px.wait(eng, x_marks[qc])
                if qc >= 2:
                    eng.wait_ge(s_p.h, s1_done(qc - 2))
                x = xc[qc % 2]
                for dt in range(NT):
                    sl = slice(dt * QCW, (dt + 1) * QCW)
                    eng.activation(sq[qc % 2][:, sl], x[:, sl], AF.Square)
                inst = None
                for dt in range(NT):
                    sl = slice(dt * QCW, (dt + 1) * QCW)
                    inst = eng.copy(xb[qc % 2][:, sl], x[:, sl])
                s_c.inc(inst, 1)
            casts(0)
            if NQ > 1:
                casts(1)
            for qc in range(NQ):
                eng.wait_ge(s_d.h, v1_done(qc))
                qs = slice((qc % 2) * QCW, (qc % 2 + 1) * QCW)
                eng.activation(rstd_t[:, qs], var_t[:, qs], AF.Ln, bias=cst["eps_ln"])
                inst = eng.activation(rstd_t[:, qs], rstd_t[:, qs], AF.Exp, scale=-0.5)
                s_l.inc(inst, 1)
                if qc + 2 < NQ:
                    casts(qc + 2)

        def pe_sec(eng):
            def s1(qc):
                eng.wait_ge(s_c.h, c_done(qc))
                if qc >= 1:
                    eng.wait_ge(s_d.h, v1_done(qc - 1))
                for dt in range(NT):
                    sl = slice(dt * QCW, (dt + 1) * QCW)
                    nc.tensor.matmul(ps_m[0:1, :], cst["ones_col_bf"], xb[qc % 2][:, sl],
                                     start=(dt == 0), stop=(dt == NT - 1))
                for dt in range(NT):
                    sl = slice(dt * QCW, (dt + 1) * QCW)
                    inst = nc.tensor.matmul(ps_v[0:1, :], cst["ones_col_bf"], sq[qc % 2][:, sl],
                                            start=(dt == 0), stop=(dt == NT - 1))
                s_p.inc(inst, 1)
            s1(0)
            if NQ > 1:
                s1(1)
            for qc in range(NQ):
                eng.wait_ge(s_d.h, v2_done(qc))
                if qc >= 1:
                    eng.wait_ge(s_d.h, v3_done(qc - 1, NT - 1))
                qs = slice((qc % 2) * QCW, (qc % 2 + 1) * QCW)
                nc.tensor.matmul(ps_b1[:], cst["ones_row_f32"], rstd_t[:, qs], start=True, stop=True)
                inst = nc.tensor.matmul(ps_b2[:], cst["ones_row_f32"], mr_t[:, qs], start=True, stop=True)
                s_p.inc(inst, 1)
                if qc + 2 < NQ:
                    s1(qc + 2)

        def dve_sec(eng):
            for qc in range(NQ):
                qs = slice((qc % 2) * QCW, (qc % 2 + 1) * QCW)
                eng.wait_ge(s_p.h, s1_done(qc))
                if qc >= 2:
                    eng.wait_ge(s_p.h, s2_done(qc - 2))
                eng.tensor_scalar(m_t[:, qs], ps_m[0:1, :], 1.0 / D, None, ALU.mult)
                eng.tensor_scalar(v_t[:, qs], ps_v[0:1, :], 1.0 / D, None, ALU.mult)
                eng.tensor_tensor(mr_t[:, qs], m_t[:, qs], m_t[:, qs], ALU.mult)
                inst = eng.tensor_tensor(var_t[:, qs], v_t[:, qs], mr_t[:, qs], ALU.subtract)
                s_d.inc(inst, 1)
                eng.wait_ge(s_l.h, a_done(qc))
                inst = eng.tensor_tensor(mr_t[:, qs], m_t[:, qs], rstd_t[:, qs], ALU.mult)
                s_d.inc(inst, 1)
                eng.wait_ge(s_p.h, s2_done(qc))
                x = xc[qc % 2]
                for dt in range(NT):
                    gi = qc * NT + dt
                    if gi >= 2:
                        po.wait(eng, o_mark(gi - 2))
                    sl = slice(dt * QCW, (dt + 1) * QCW)
                    tm = tmp[dt % 2]
                    eng.tensor_tensor(tm[:], x[:, sl], ps_b1[:], ALU.mult)
                    inst = eng.tensor_tensor(ob[gi % 2][:], tm[:], ps_b2[:], ALU.subtract)
                    s_d.inc(inst, 1)

        b.sec("sync", sync_io)
        b.sec("scalar", act_sec)
        b.sec("tensor", pe_sec)
        b.sec("vector", dve_sec)
    return po.gates()



def emit_fm_proj(b, tag, w_dram, outd, bias_2d, xln_d, T, out_dram, ready,
                 rope=None, cst=None):
    nc = b.nc
    NQ = T // QCW
    NO = outd // 128
    NTIL = NQ * NO
    s_w = b.sem("fp_w")
    px = Pipe(b, "fp_x")
    po = Pipe(b, "fp_o")
    s_mm = b.sem("fp_mm")
    s_sw = b.sem("fp_sw")
    s_e = b.sem("fp_e")
    s_r = b.sem("fp_r")

    xln_r = xln_d.rearrange("(dt p) t -> p dt t", p=128)

    with ExitStack() as ctx:
        w_sb = ctx.enter_context(nc.sbuf_tensor(f"{tag}_wsb", [128, NT * outd], BF16))
        wst = [ctx.enter_context(nc.sbuf_tensor(f"{tag}_wst{i}", [128, NT * 256], F32)) for i in range(2)]
        bias_sb = ctx.enter_context(nc.sbuf_tensor(f"{tag}_bs", [128, max(NO, 1)], F32))
        xcb = [ctx.enter_context(nc.sbuf_tensor(f"{tag}_xc{i}", [128, NT * QCW], BF16)) for i in range(2)]
        ev = [ctx.enter_context(nc.sbuf_tensor(f"{tag}_ev{i}", [128, QCW], BF16)) for i in range(2)]
        ps = [ctx.enter_context(nc.psum_tensor(f"{tag}_ps{i}", [128, QCW], F32)) for i in range(2)]
        if rope is not None:
            rt1 = ctx.enter_context(nc.sbuf_tensor(f"{tag}_rt1", [128, QCW], F32))
            rt2 = ctx.enter_context(nc.sbuf_tensor(f"{tag}_rt2", [128, QCW], F32))
            ro = [ctx.enter_context(nc.sbuf_tensor(f"{tag}_ro{i}", [128, QCW], BF16)) for i in range(2)]
            ps_sw = [ctx.enter_context(nc.psum_tensor(f"{tag}_psw{i}", [128, QCW], F32)) for i in range(2)]

        has_bias = bias_2d is not None
        NWC = outd // 256
        pwst = Pipe(b, "fp_wst")
        wst_marks = []
        w_base_cast = s_w.n
        w_mark = s_w.n + NWC + (16 if has_bias else 0)
        mm_plan = [s_mm.n + i + 1 for i in range(NTIL)]
        sw_plan = [s_sw.n + i + 1 for i in range(NTIL)]
        e_plan = [s_e.n + i + 1 for i in range(NTIL)]
        r_plan = [s_r.n + i + 1 for i in range(NTIL)]
        ob_base = [sc.n for sc in po.sems]
        o_mark = lambda i: (i % 2, ob_base[i % 2] + 16 * (i // 2 + 1))
        last_plan = r_plan if rope is not None else e_plan
        s_last = s_r if rope is not None else s_e
        x_marks = []

        def w_loader(eng):
            _waits(eng, ready)
            if has_bias:
                eng.dma_start(out=bias_sb[:, 0:NO], in_=bias_2d).then_inc(s_w.h, 16)
                s_w.n += 16
            for ch in range(NWC):
                if ch >= 2:
                    eng.wait_ge(s_w.h, w_base_cast + (16 if has_bias else 0) + ch - 1)
                inst = eng.dma_start(out=wst[ch % 2][:].rearrange("p (dt o) -> p dt o", o=256),
                                     in_=w_dram.rearrange("(dt p) o -> p dt o", p=128)[:, :, ch * 256:(ch + 1) * 256])
                wst_marks.append(pwst.inc(inst, ch))

        def w_cast(eng):
            for ch in range(NWC):
                pwst.wait(eng, wst_marks[ch])
                inst = eng.copy(w_sb[:].rearrange("p (dt o) -> p dt o", o=outd)[:, :, ch * 256:(ch + 1) * 256],
                                wst[ch % 2][:].rearrange("p (dt o) -> p dt o", o=256))
                s_w.inc(inst, 1)

        def sync_io(eng):
            def load(qc):
                inst = eng.dma_start(out=xcb[qc % 2][:].rearrange("p (dt c) -> p dt c", dt=NT),
                                     in_=xln_r[:, :, qc * QCW:(qc + 1) * QCW])
                x_marks.append(px.inc(inst, qc))
            _waits(eng, ready)
            load(0)
            if NQ > 1:
                load(1)
            idx = 0
            for qc in range(NQ):
                for ot in range(NO):
                    src = (ro[idx % 2] if rope is not None else ev[idx % 2])[:]
                    eng.wait_ge(s_last.h, last_plan[idx])
                    inst = eng.dma_start(out=out_dram[ot][:, qc * QCW:(qc + 1) * QCW], in_=src)
                    po.inc(inst, idx)
                    idx += 1
                if qc + 2 < NQ:
                    load(qc + 2)

        def mms(eng):
            eng.wait_ge(s_w.h, w_mark)
            idx = 0
            for qc in range(NQ):
                px.wait(eng, x_marks[qc])
                for ot in range(NO):
                    if idx >= 2:
                        eng.wait_ge(s_e.h, e_plan[idx - 2])
                    p = ps[idx % 2]
                    for dt in range(NT):
                        inst = nc.tensor.matmul(
                            p[:], w_sb[:, dt * outd + ot * 128: dt * outd + (ot + 1) * 128],
                            xcb[qc % 2][:, dt * QCW:(dt + 1) * QCW],
                            start=(dt == 0), stop=(dt == NT - 1))
                    s_mm.inc(inst, 1)
                    if rope is not None and idx >= 1:
                        j = idx - 1
                        eng.wait_ge(s_e.h, e_plan[j])
                        if j >= 2:
                            eng.wait_ge(s_r.h, r_plan[j - 2])
                        inst = nc.tensor.matmul(ps_sw[j % 2][:], rope["pswap"], ev[j % 2][:],
                                                start=True, stop=True)
                        s_sw.inc(inst, 1)
                    idx += 1
            if rope is not None:
                j = NTIL - 1
                eng.wait_ge(s_e.h, e_plan[j])
                if j >= 2:
                    eng.wait_ge(s_r.h, r_plan[j - 2])
                inst = nc.tensor.matmul(ps_sw[j % 2][:], rope["pswap"], ev[j % 2][:],
                                        start=True, stop=True)
                s_sw.inc(inst, 1)

        def evac(eng):
            idx = 0
            for qc in range(NQ):
                for ot in range(NO):
                    eng.wait_ge(s_mm.h, mm_plan[idx])
                    if idx >= 2:
                        eng.wait_ge(s_last.h, last_plan[idx - 2])
                        if rope is None:
                            po.wait(eng, o_mark(idx - 2))
                    bias = bias_sb[:, ot:ot + 1] if has_bias else 0.0
                    fn = AF.Identity if has_bias else AF.Copy
                    inst = eng.activation(ev[idx % 2][:], ps[idx % 2][:], fn, bias=bias)
                    s_e.inc(inst, 1)
                    idx += 1

        def rope_dve(eng):
            idx = 0
            for qc in range(NQ):
                qs = slice(qc * QCW, (qc + 1) * QCW)
                for ot in range(NO):
                    eng.wait_ge(s_sw.h, sw_plan[idx])
                    if idx >= 2:
                        po.wait(eng, o_mark(idx - 2))
                    eng.tensor_tensor(rt1[:], ps_sw[idx % 2][:], rope["sin"][:, qs], ALU.mult)
                    eng.tensor_tensor(rt2[:], ev[idx % 2][:], rope["cos"][:, qs], ALU.mult)
                    inst = eng.tensor_tensor(ro[idx % 2][:], rt1[:], rt2[:], ALU.add)
                    s_r.inc(inst, 1)
                    idx += 1

        b.sec("sync", w_loader)
        b.sec("scalar", w_cast)
        b.sec("sync", sync_io)
        b.sec("tensor", mms)
        b.sec("scalar", evac)
        if rope is not None:
            b.sec("vector", rope_dve)
    return po.gates()



def emit_v_proj(b, tag, wv_dram, outd, bv_row, xln_d, T, v_dram, ready, cst):
    nc = b.nc
    NQ = T // QCW
    NOC = outd // QCW
    NTIL = NQ * 4 * NOC
    s_w = b.sem("vp_w")
    px = Pipe(b, "vp_x")
    po = Pipe(b, "vp_o")
    s_mm = b.sem("vp_mm")
    s_e = b.sem("vp_e")

    xln_r = xln_d.rearrange("(dt p) t -> p dt t", p=128)

    with ExitStack() as ctx:
        w_sb = ctx.enter_context(nc.sbuf_tensor(f"{tag}_wsb", [128, NT * outd], BF16))
        wst = [ctx.enter_context(nc.sbuf_tensor(f"{tag}_wst{i}", [128, NT * 256], F32)) for i in range(2)]
        xcb = [ctx.enter_context(nc.sbuf_tensor(f"{tag}_xc{i}", [128, NT * QCW], BF16)) for i in range(2)]
        ev = [ctx.enter_context(nc.sbuf_tensor(f"{tag}_ev{i}", [128, QCW], BF16)) for i in range(2)]
        bvrow_sb = ctx.enter_context(nc.sbuf_tensor(f"{tag}_bvr", [1, max(outd, QCW)], F32))
        ps = [ctx.enter_context(nc.psum_tensor(f"{tag}_ps{i}", [128, QCW], F32)) for i in range(2)]

        has_bias = bv_row is not None
        NWC = outd // 256
        pwst = Pipe(b, "vp_wst")
        wst_marks = []
        w_base_cast = s_w.n
        w_mark = s_w.n + NWC + (16 if has_bias else 0)
        mm_base = s_mm.n
        mm_plan = [mm_base + i + 1 for i in range(NTIL)]
        e_plan = [s_e.n + i + 1 for i in range(NTIL)]
        ob_base = [sc.n for sc in po.sems]
        o_mark = lambda i: (i % 2, ob_base[i % 2] + 16 * 4 * (i // 2 + 1))
        x_marks = []

        def w_loader(eng):
            _waits(eng, ready)
            if has_bias:
                eng.dma_start(out=bvrow_sb[:, 0:outd], in_=bv_row).then_inc(s_w.h, 16)
                s_w.n += 16
            for ch in range(NWC):
                if ch >= 2:
                    eng.wait_ge(s_w.h, w_base_cast + (16 if has_bias else 0) + ch - 1)
                inst = eng.dma_start(out=wst[ch % 2][:].rearrange("p (dt o) -> p dt o", o=256),
                                     in_=wv_dram.rearrange("(dt p) o -> p dt o", p=128)[:, :, ch * 256:(ch + 1) * 256])
                wst_marks.append(pwst.inc(inst, ch))

        def w_cast(eng):
            for ch in range(NWC):
                pwst.wait(eng, wst_marks[ch])
                inst = eng.copy(w_sb[:].rearrange("p (dt o) -> p dt o", o=outd)[:, :, ch * 256:(ch + 1) * 256],
                                wst[ch % 2][:].rearrange("p (dt o) -> p dt o", o=256))
                s_w.inc(inst, 1)

        def sync_io(eng):
            def load(qc):
                inst = eng.dma_start(out=xcb[qc % 2][:].rearrange("p (dt c) -> p dt c", dt=NT),
                                     in_=xln_r[:, :, qc * QCW:(qc + 1) * QCW])
                x_marks.append(px.inc(inst, qc))
            _waits(eng, ready)
            load(0)
            if NQ > 1:
                load(1)
            idx = 0
            for qc in range(NQ):
                for tt in range(4):
                    for oc in range(NOC):
                        eng.wait_ge(s_e.h, e_plan[idx])
                        for j in range(4):
                            hh = oc * 4 + j
                            inst = eng.dma_start(out=v_dram[hh][qc * QCW + tt * 128: qc * QCW + (tt + 1) * 128, :],
                                                 in_=ev[idx % 2][:, j * 128:(j + 1) * 128])
                            po.inc(inst, idx)
                        idx += 1
                if qc + 2 < NQ:
                    load(qc + 2)

        def mms(eng):
            eng.wait_ge(s_w.h, w_mark)
            idx = 0
            for qc in range(NQ):
                px.wait(eng, x_marks[qc])
                for tt in range(4):
                    for oc in range(NOC):
                        if idx >= 2:
                            eng.wait_ge(s_e.h, e_plan[idx - 2])
                        p = ps[idx % 2]
                        if has_bias:
                            nc.tensor.matmul(p[:], cst["ones_row_f32"],
                                             bvrow_sb[:, oc * QCW:(oc + 1) * QCW],
                                             start=True, stop=False)
                        for dt in range(NT):
                            inst = nc.tensor.matmul(
                                p[:], xcb[qc % 2][:, dt * QCW + tt * 128: dt * QCW + (tt + 1) * 128],
                                w_sb[:, dt * outd + oc * QCW: dt * outd + (oc + 1) * QCW],
                                start=(dt == 0 and not has_bias), stop=(dt == NT - 1))
                        s_mm.inc(inst, 1)
                        idx += 1

        def evac(eng):
            idx = 0
            for qc in range(NQ):
                for tt in range(4):
                    for oc in range(NOC):
                        eng.wait_ge(s_mm.h, mm_plan[idx])
                        if idx >= 2:
                            po.wait(eng, o_mark(idx - 2))
                        inst = eng.copy(ev[idx % 2][:], ps[idx % 2][:])
                        s_e.inc(inst, 1)
                        idx += 1

        b.sec("sync", w_loader)
        b.sec("scalar", w_cast)
        b.sec("sync", sync_io)
        b.sec("tensor", mms)
        b.sec("scalar", evac)
    return po.gates()



def emit_attention(b, tag, q_dram, k_dram, v_dram, kv_of_head, TQ, TK, scale,
                   shift, pf_dram, nheads, ready, cst):
    nc = b.nc
    NQ = TQ // QCW
    NKT = TK // 128
    pld = Pipe(b, "at_ld")
    po = Pipe(b, "at_o")
    s_s = b.sem("at_s")
    s_p = b.sem("at_p")
    s_e = b.sem("at_e")
    s_d = b.sem("at_d")
    s_bc = b.sem("at_bc")
    s_v = b.sem("at_v")
    s_sm = b.sem("at_sm")

    with ExitStack() as ctx:
        NSL = 4
        k_sb = [ctx.enter_context(nc.sbuf_tensor(f"{tag}_k{i}", [128, TK], BF16)) for i in range(2)]
        v_sb = [ctx.enter_context(nc.sbuf_tensor(f"{tag}_v{i}", [128, TK], BF16)) for i in range(2)]
        q_sb = [ctx.enter_context(nc.sbuf_tensor(f"{tag}_q{i}", [128, TQ], BF16)) for i in range(2)]
        a_all = ctx.enter_context(nc.sbuf_tensor(f"{tag}_a", [128, NSL * QCW], BF16))
        a_sb = [a_all[:, i * QCW:(i + 1) * QCW] for i in range(NSL)]
        r_sb = ctx.enter_context(nc.sbuf_tensor(f"{tag}_r", [1, QCW], F32))
        rb_sb = ctx.enter_context(nc.sbuf_tensor(f"{tag}_rb", [128, QCW], F32))
        pf_sb = [ctx.enter_context(nc.sbuf_tensor(f"{tag}_pf{i}", [128, QCW], BF16)) for i in range(2)]
        acc = [ctx.enter_context(nc.sbuf_tensor(f"{tag}_acc{i}", [128, QCW], F32)) for i in range(2)]
        ps_s = ctx.enter_context(nc.psum_tensor(f"{tag}_pss", [128, NSL * QCW], F32))
        ps_av = [ctx.enter_context(nc.psum_tensor(f"{tag}_pav{i}", [128, QCW], F32)) for i in range(2)]
        ps_sm = ctx.enter_context(nc.psum_tensor(f"{tag}_psm", [128, QCW], F32))
        ps_bc = ctx.enter_context(nc.psum_tensor(f"{tag}_pbc", [128, QCW], F32))

        s_plan, p_plan, v_plan, d_plan, bc_plan, o_plan = {}, {}, {}, {}, {}, {}
        cs, cp, cv, cd, cb = s_s.n, s_p.n, s_v.n, s_d.n, s_bc.n
        e_base = s_e.n
        e_done = lambda gi: e_base + gi + 1
        sm_base = s_sm.n
        sm_done = lambda u: sm_base + u + 1
        ob_base = [sc.n for sc in po.sems]
        for h in range(nheads):
            for qc in range(NQ):
                gi = h * NQ + qc
                for kt in range(NKT):
                    cs += 1; s_plan[(h, qc, kt)] = cs
                    cp += 1; p_plan[(h, qc, kt)] = cp
                    cv += 1; v_plan[(h, qc, kt)] = cv
                cd += 1; d_plan[(h, qc, "r")] = cd
                cb += 1; bc_plan[(h, qc)] = cb
                cd += 1; d_plan[(h, qc, "pf")] = cd
                o_plan[(h, qc)] = (gi % 2, ob_base[gi % 2] + 16 * (gi // 2 + 1))
        ld_marks = []

        def g2t(gi):
            return (gi // (NQ * NKT), (gi % (NQ * NKT)) // NKT, gi % NKT)

        def sync_io(eng):
            def load(h):
                kv = kv_of_head(h)
                pld.inc(eng.dma_start(out=k_sb[h % 2][:], in_=k_dram[kv]), h)
                pld.inc(eng.dma_start(out=v_sb[h % 2][:].rearrange("p (kt c) -> p kt c", kt=NKT),
                                      in_=v_dram[kv].rearrange("(kt p) c -> p kt c", p=128)), h)
                inst = eng.dma_start(out=q_sb[h % 2][:], in_=q_dram[h])
                ld_marks.append(pld.inc(inst, h))
            _waits(eng, ready)
            load(0)
            if nheads > 1:
                load(1)
            for h in range(nheads):
                for qc in range(NQ):
                    gi = h * NQ + qc
                    eng.wait_ge(s_d.h, d_plan[(h, qc, "pf")])
                    inst = eng.dma_start(out=pf_dram[h][:, qc * QCW:(qc + 1) * QCW],
                                         in_=pf_sb[gi % 2][:])
                    po.inc(inst, gi)
                if h + 2 < nheads:
                    load(h + 2)

        def pe(eng):
            nun = nheads * NQ

            def s_mm(gi):
                hh, qq, kk = g2t(gi)
                if hh * NQ + qq > 0 and kk == 0:
                    pld.wait(eng, ld_marks[hh])
                if gi >= NSL:
                    eng.wait_ge(s_e.h, e_done(gi - NSL))
                inst = nc.tensor.matmul(ps_s[:, (gi % NSL) * QCW:(gi % NSL + 1) * QCW],
                                        k_sb[hh % 2][:, kk * 128:(kk + 1) * 128],
                                        q_sb[hh % 2][:, qq * QCW:(qq + 1) * QCW],
                                        start=True, stop=True)
                s_s.inc(inst, 1)

            def bc_mm(u):
                hh, qq = u // NQ, u % NQ
                eng.wait_ge(s_d.h, d_plan[(hh, qq, "r")])
                inst = nc.tensor.matmul(ps_bc[:], cst["ones_row_f32"], r_sb[:], start=True, stop=True)
                s_bc.inc(inst, 1)

            pld.wait(eng, ld_marks[0])
            s_mm(0)
            s_mm(1)
            for u in range(nun):
                h, qc = u // NQ, u % NQ
                if u > 1:
                    prev = ((u - 2) // NQ, (u - 2) % NQ)
                    eng.wait_ge(s_d.h, d_plan[(prev[0], prev[1], "pf")])
                for kt in range(NKT):
                    gi = u * NKT + kt
                    if gi + 2 < nun * NKT:
                        s_mm(gi + 2)
                    eng.wait_ge(s_e.h, e_done(gi))
                    inst = nc.tensor.matmul(ps_av[u % 2][:], v_sb[h % 2][:, kt * 128:(kt + 1) * 128],
                                            a_sb[gi % NSL], start=(kt == 0), stop=(kt == NKT - 1))
                    s_p.inc(inst, 1)
                    if kt == 2 and u > 0:
                        bc_mm(u - 1)
                eng.wait_ge(s_v.h, v_plan[(h, qc, NKT - 1)])
                if u > 0:
                    prev = ((u - 1) // NQ, (u - 1) % NQ)
                    eng.wait_ge(s_d.h, d_plan[(prev[0], prev[1], "r")])
                inst = nc.tensor.matmul(ps_sm[0:1, :], cst["ones_col_f32"], acc[u % 2][:],
                                        start=True, stop=True)
                s_sm.inc(inst, 1)
            bc_mm(nun - 1)

        def act(eng):
            for h in range(nheads):
                for qc in range(NQ):
                    for kt in range(NKT):
                        gi = (h * NQ + qc) * NKT + kt
                        sl = (gi % NSL) * QCW
                        eng.wait_ge(s_s.h, s_plan[(h, qc, kt)])
                        if gi >= NSL:
                            eng.wait_ge(s_p.h, p_plan[g2t(gi - NSL)])
                            eng.wait_ge(s_v.h, v_plan[g2t(gi - NSL)])
                        ebias = cst["nshift"] if shift else 0.0
                        inst = eng.activation(a_all[:, sl:sl + QCW], ps_s[:, sl:sl + QCW],
                                              AF.Exp, scale=scale, bias=ebias)
                        s_e.inc(inst, 1)

        def dve(eng):
            for h in range(nheads):
                for qc in range(NQ):
                    u = h * NQ + qc
                    for kt in range(NKT):
                        gi = u * NKT + kt
                        eng.wait_ge(s_e.h, e_done(gi))
                        if kt == 0:
                            if u >= 2:
                                eng.wait_ge(s_sm.h, sm_done(u - 2))
                            inst = eng.tensor_copy(acc[u % 2][:], a_sb[gi % NSL])
                        else:
                            inst = eng.tensor_tensor(acc[u % 2][:], acc[u % 2][:],
                                                     a_sb[gi % NSL], ALU.add)
                        s_v.inc(inst, 1)
                    eng.wait_ge(s_sm.h, sm_done(u))
                    inst = eng.reciprocal(r_sb[:], ps_sm[0:1, :])
                    s_d.inc(inst, 1)
                    eng.wait_ge(s_bc.h, bc_plan[(h, qc)])
                    eng.tensor_copy(rb_sb[:], ps_bc[:])
                    if u >= 2:
                        pv = u - 2
                        po.wait(eng, o_plan[(pv // NQ, pv % NQ)])
                    inst = eng.tensor_tensor(pf_sb[u % 2][:], ps_av[u % 2][:], rb_sb[:], ALU.mult)
                    s_d.inc(inst, 1)

        b.sec("sync", sync_io)
        b.sec("tensor", pe)
        b.sec("scalar", act)
        b.sec("vector", dve)
    return po.gates()



def emit_out_proj(b, tag, wo_dram, bo_2d, pf_dram, nheads, resid, T, dst,
                  ready, cst):
    nc = b.nc
    NQ = T // QCW
    NTIL = NQ * NT
    s_w = b.sem("op_w")
    px = Pipe(b, "op_x")
    po = Pipe(b, "op_o")
    s_mm = b.sem("op_mm")
    s_e = b.sem("op_e")
    s_d = b.sem("op_d")

    resid_r = resid.rearrange("(dt p) t -> p dt t", p=128)
    dst_r = dst.rearrange("(dt p) t -> p dt t", p=128)

    with ExitStack() as ctx:
        wo_sb = ctx.enter_context(nc.sbuf_tensor(f"{tag}_wo", [128, nheads * D], BF16))
        wst = [ctx.enter_context(nc.sbuf_tensor(f"{tag}_wst{i}", [128, nheads * 64], F32)) for i in range(2)]
        bo_sb = ctx.enter_context(nc.sbuf_tensor(f"{tag}_bo", [128, NT], F32))
        pf_sb = [ctx.enter_context(nc.sbuf_tensor(f"{tag}_pf{i}", [128, nheads * QCW], BF16)) for i in range(2)]
        xr_sb = [ctx.enter_context(nc.sbuf_tensor(f"{tag}_xr{i}", [128, NT * QCW], F32)) for i in range(2)]
        tv = [ctx.enter_context(nc.sbuf_tensor(f"{tag}_tv{i}", [128, QCW], F32)) for i in range(2)]
        ov = [ctx.enter_context(nc.sbuf_tensor(f"{tag}_ov{i}", [128, QCW], F32)) for i in range(2)]
        ps = [ctx.enter_context(nc.psum_tensor(f"{tag}_ps{i}", [128, QCW], F32)) for i in range(2)]

        has_bias = bo_2d is not None
        NWC = D // 64
        pwst = Pipe(b, "op_wst")
        wst_marks = []
        w_base_cast = s_w.n
        w_mark = s_w.n + NWC + (16 if has_bias else 0)
        mm_plan = [s_mm.n + i + 1 for i in range(NTIL)]
        e_plan = [s_e.n + i + 1 for i in range(NTIL)]
        d_plan = [s_d.n + i + 1 for i in range(NTIL)]
        ob_base = [sc.n for sc in po.sems]
        o_mark = lambda i: (i % 2, ob_base[i % 2] + 16 * (i // 2 + 1))
        x_marks = []

        def w_loader(eng):
            _waits(eng, ready)
            if has_bias:
                eng.dma_start(out=bo_sb[:], in_=bo_2d).then_inc(s_w.h, 16)
                s_w.n += 16
            for ch in range(NWC):
                if ch >= 2:
                    eng.wait_ge(s_w.h, w_base_cast + (16 if has_bias else 0) + ch - 1)
                inst = eng.dma_start(out=wst[ch % 2][:].rearrange("p (hh o) -> p hh o", o=64),
                                     in_=wo_dram.rearrange("(hh p) o -> p hh o", p=128)[:, :, ch * 64:(ch + 1) * 64])
                wst_marks.append(pwst.inc(inst, ch))

        def w_cast(eng):
            for ch in range(NWC):
                pwst.wait(eng, wst_marks[ch])
                inst = eng.copy(wo_sb[:].rearrange("p (hh o) -> p hh o", o=D)[:, :, ch * 64:(ch + 1) * 64],
                                wst[ch % 2][:].rearrange("p (hh o) -> p hh o", o=64))
                s_w.inc(inst, 1)

        def sync_io(eng):
            def load(qc):
                px.inc(eng.dma_start(out=pf_sb[qc % 2][:].rearrange("p (hh c) -> p hh c", hh=nheads),
                                     in_=pf_dram.rearrange("hh p t -> p hh t")[:, :, qc * QCW:(qc + 1) * QCW]), qc)
                inst = eng.dma_start(out=xr_sb[qc % 2][:].rearrange("p (dt c) -> p dt c", dt=NT),
                                     in_=resid_r[:, :, qc * QCW:(qc + 1) * QCW])
                x_marks.append(px.inc(inst, qc))
            _waits(eng, ready)
            load(0)
            if NQ > 1:
                load(1)
            idx = 0
            for qc in range(NQ):
                for dt in range(NT):
                    eng.wait_ge(s_d.h, d_plan[idx])
                    inst = eng.dma_start(out=dst_r[:, dt, qc * QCW:(qc + 1) * QCW], in_=ov[idx % 2][:])
                    po.inc(inst, idx)
                    idx += 1
                if qc + 2 < NQ:
                    load(qc + 2)

        def mms(eng):
            eng.wait_ge(s_w.h, w_mark)
            idx = 0
            for qc in range(NQ):
                px.wait(eng, x_marks[qc])
                for dt in range(NT):
                    if idx >= 2:
                        eng.wait_ge(s_e.h, e_plan[idx - 2])
                    p = ps[idx % 2]
                    for hh in range(nheads):
                        inst = nc.tensor.matmul(
                            p[:], wo_sb[:, hh * D + dt * 128: hh * D + (dt + 1) * 128],
                            pf_sb[qc % 2][:, hh * QCW:(hh + 1) * QCW],
                            start=(hh == 0), stop=(hh == nheads - 1))
                    s_mm.inc(inst, 1)
                    idx += 1

        def evac(eng):
            idx = 0
            for qc in range(NQ):
                for dt in range(NT):
                    eng.wait_ge(s_mm.h, mm_plan[idx])
                    if idx >= 2:
                        eng.wait_ge(s_d.h, d_plan[idx - 2])
                    bias = bo_sb[:, dt:dt + 1] if has_bias else 0.0
                    fn = AF.Identity if has_bias else AF.Copy
                    inst = eng.activation(tv[idx % 2][:], ps[idx % 2][:], fn, bias=bias)
                    s_e.inc(inst, 1)
                    idx += 1

        def resid_dve(eng):
            idx = 0
            for qc in range(NQ):
                for dt in range(NT):
                    eng.wait_ge(s_e.h, e_plan[idx])
                    if idx >= 2:
                        po.wait(eng, o_mark(idx - 2))
                    inst = eng.tensor_tensor(ov[idx % 2][:], tv[idx % 2][:],
                                             xr_sb[qc % 2][:, dt * QCW:(dt + 1) * QCW], ALU.add)
                    s_d.inc(inst, 1)
                    idx += 1

        b.sec("sync", w_loader)
        b.sec("scalar", w_cast)
        b.sec("sync", sync_io)
        b.sec("tensor", mms)
        b.sec("scalar", evac)
        b.sec("vector", resid_dve)
    return po.gates()



def emit_gate_up(b, tag, wgu_d, xln_d, T, hid, ready, w_ready):
    nc = b.nc
    NQ = T // QCW
    NP = I_FFN // 128
    GC = 2
    NCH = NP // GC
    NTIL = NP * NQ
    pw = Pipe(b, "gu_w")
    s_x = b.sem("gu_x")
    po = Pipe(b, "gu_o")
    s_mm = b.sem("gu_mm")
    s_si = b.sem("gu_si")
    s_h = b.sem("gu_h")

    xln_r = xln_d.rearrange("(dt p) t -> p dt t", p=128)
    wgu_r = wgu_d.rearrange("(kt p) o -> p kt o", p=128)

    with ExitStack() as ctx:
        xln_sb = ctx.enter_context(nc.sbuf_tensor(f"{tag}_xln", [128, NT * T], BF16))
        wg = [ctx.enter_context(nc.sbuf_tensor(f"{tag}_wg{i}", [128, NT * 128 * GC], BF16)) for i in range(2)]
        wu = [ctx.enter_context(nc.sbuf_tensor(f"{tag}_wu{i}", [128, NT * 128 * GC], BF16)) for i in range(2)]
        sil = [ctx.enter_context(nc.sbuf_tensor(f"{tag}_si{i}", [128, QCW], F32)) for i in range(2)]
        hv = [ctx.enter_context(nc.sbuf_tensor(f"{tag}_hv{i}", [128, QCW], BF16)) for i in range(2)]
        ps_g = [ctx.enter_context(nc.psum_tensor(f"{tag}_pg{i}", [128, QCW], F32)) for i in range(2)]
        ps_u = [ctx.enter_context(nc.psum_tensor(f"{tag}_pu{i}", [128, QCW], F32)) for i in range(2)]

        x_mark = s_x.n + 16
        mm_plan = [s_mm.n + i + 1 for i in range(NTIL)]
        si_plan = [s_si.n + i + 1 for i in range(NTIL)]
        h_plan = [s_h.n + i + 1 for i in range(NTIL)]
        ob_base = [sc.n for sc in po.sems]
        o_mark = lambda i: (i % 2, ob_base[i % 2] + 16 * (i // 2 + 1))
        w_marks = []

        def sync_io(eng):
            _waits(eng, ready)
            eng.dma_start(out=xln_sb[:].rearrange("p (dt t) -> p dt t", t=T),
                          in_=xln_r).then_inc(s_x.h, 16)
            s_x.n += 16
            idx = 0
            for ch in range(NCH):
                for j in range(GC):
                    g = ch * GC + j
                    for qc in range(NQ):
                        eng.wait_ge(s_h.h, h_plan[idx])
                        inst = eng.dma_start(out=hid[g * 128:(g + 1) * 128, qc * QCW:(qc + 1) * QCW],
                                             in_=hv[idx % 2][:])
                        po.inc(inst, idx)
                        idx += 1

        def w_loader(eng):
            _waits(eng, w_ready)
            _waits(eng, ready)
            for ch in range(NCH):
                if ch >= 2:
                    eng.wait_ge(s_mm.h, mm_plan[(ch - 1) * GC * NQ - 1])
                cs = slice(ch * GC * 128, (ch + 1) * GC * 128)
                us = slice(I_FFN + ch * GC * 128, I_FFN + (ch + 1) * GC * 128)
                pw.inc(eng.dma_start(out=wg[ch % 2][:].rearrange("p (kt o) -> p kt o", kt=NT),
                                     in_=wgu_r[:, :, cs]), ch)
                inst = eng.dma_start(out=wu[ch % 2][:].rearrange("p (kt o) -> p kt o", kt=NT),
                                     in_=wgu_r[:, :, us])
                w_marks.append(pw.inc(inst, ch))

        def mms(eng):
            eng.wait_ge(s_x.h, x_mark)
            idx = 0
            for ch in range(NCH):
                pw.wait(eng, w_marks[ch])
                for j in range(GC):
                    for qc in range(NQ):
                        if idx >= 2:
                            eng.wait_ge(s_h.h, h_plan[idx - 2])
                            eng.wait_ge(s_si.h, si_plan[idx - 2])
                        pg, pu = ps_g[idx % 2], ps_u[idx % 2]
                        for kt in range(NT):
                            nc.tensor.matmul(pg[:], wg[ch % 2][:, kt * GC * 128 + j * 128: kt * GC * 128 + (j + 1) * 128],
                                             xln_sb[:, kt * T + qc * QCW: kt * T + (qc + 1) * QCW],
                                             start=(kt == 0), stop=(kt == NT - 1))
                        for kt in range(NT):
                            inst = nc.tensor.matmul(pu[:], wu[ch % 2][:, kt * GC * 128 + j * 128: kt * GC * 128 + (j + 1) * 128],
                                                    xln_sb[:, kt * T + qc * QCW: kt * T + (qc + 1) * QCW],
                                                    start=(kt == 0), stop=(kt == NT - 1))
                        s_mm.inc(inst, 1)
                        idx += 1

        def silu_act(eng):
            for idx in range(NTIL):
                eng.wait_ge(s_mm.h, mm_plan[idx])
                if idx >= 2:
                    eng.wait_ge(s_h.h, h_plan[idx - 2])
                inst = eng.activation(sil[idx % 2][:], ps_g[idx % 2][:], AF.Silu)
                s_si.inc(inst, 1)

        def mult_dve(eng):
            for idx in range(NTIL):
                eng.wait_ge(s_si.h, si_plan[idx])
                if idx >= 2:
                    po.wait(eng, o_mark(idx - 2))
                inst = eng.tensor_tensor(hv[idx % 2][:], sil[idx % 2][:], ps_u[idx % 2][:], ALU.mult)
                s_h.inc(inst, 1)

        b.sec("sync", sync_io)
        b.sec("gpsimd", w_loader)
        b.sec("tensor", mms)
        b.sec("scalar", silu_act)
        b.sec("vector", mult_dve)
    return po.gates()



def emit_down(b, tag, wdn_d, hid, T, resid, dst, hid_ready, w_ready, cst):
    nc = b.nc
    NQ = T // QCW
    NKT = I_FFN // 128
    NTIL = NQ * NT
    pw = Pipe(b, "dn_w")
    s_h = b.sem("dn_h")
    po = Pipe(b, "dn_o")
    s_mm = b.sem("dn_mm")
    s_e = b.sem("dn_e")
    s_q = b.sem("dn_q")
    s_ms = b.sem("dn_ms")
    s_r1 = b.sem("dn_r1")
    s_r2 = b.sem("dn_r2")
    s_bc = b.sem("dn_bc")
    s_f = b.sem("dn_f")

    resid_r = resid.rearrange("(dt p) t -> p dt t", p=128)
    dst_r = dst.rearrange("(dt p) t -> p dt t", p=128)

    def hrow(t):
        return t, 0, 128

    with ExitStack() as ctx:
        WCH = 256
        NW = D // WCH
        h_sb = ctx.enter_context(nc.sbuf_tensor(f"{tag}_hs", [128, 44 * QCW], BF16))
        w_sb = [ctx.enter_context(nc.sbuf_tensor(f"{tag}_ws{i}", [128, 44 * WCH], BF16)) for i in range(2)]
        y_sb = ctx.enter_context(nc.sbuf_tensor(f"{tag}_y", [128, NT * QCW], F32))
        sqv = [ctx.enter_context(nc.sbuf_tensor(f"{tag}_sq{i}", [128, QCW], BF16)) for i in range(2)]
        ms_t = ctx.enter_context(nc.sbuf_tensor(f"{tag}_mst", [1, 2 * QCW], F32))
        rr_t = ctx.enter_context(nc.sbuf_tensor(f"{tag}_rrt", [1, 2 * QCW], F32))
        xr_sb = [ctx.enter_context(nc.sbuf_tensor(f"{tag}_xr{i}", [128, QCW], F32)) for i in range(2)]
        tv = [ctx.enter_context(nc.sbuf_tensor(f"{tag}_tv{i}", [128, QCW], F32)) for i in range(2)]
        ov = [ctx.enter_context(nc.sbuf_tensor(f"{tag}_ov{i}", [128, QCW], F32)) for i in range(2)]
        ps_y = [ctx.enter_context(nc.psum_tensor(f"{tag}_py{i}", [128, QCW], F32)) for i in range(2)]
        ps_ms = ctx.enter_context(nc.psum_tensor(f"{tag}_pms", [128, QCW], F32))
        ps_bc = ctx.enter_context(nc.psum_tensor(f"{tag}_pbc", [128, QCW], F32))

        h_base = s_h.n
        mm_plan = [s_mm.n + i + 1 for i in range(NTIL)]
        e_plan = [s_e.n + i + 1 for i in range(NTIL)]
        q_plan = [s_q.n + i + 1 for i in range(NTIL)]
        ms_plan = [s_ms.n + i + 1 for i in range(NTIL)]
        f_plan = [s_f.n + i + 1 for i in range(NTIL)]
        ob_base = [sc.n for sc in po.sems]
        o_mark = lambda i: (i % 2, ob_base[i % 2] + 16 * (i // 2 + 1))
        r1_base, r2_base = s_r1.n, s_r2.n
        r1_done = lambda qc: r1_base + qc + 1
        r2_done = lambda qc: r2_base + qc + 1
        bc_plan = [s_bc.n + qc + 1 for qc in range(NQ)]
        w_marks = []

        pxr = Pipe(b, "dn_xrl")
        xrb_base = [sc.n for sc in pxr.sems]
        xr_mark = lambda i: (i % 2, xrb_base[i % 2] + 16 * (i // 2 + 1))

        def mk_load_w(eng):
            def load_w_gp(idx):
                if idx >= 2:
                    eng.wait_ge(s_mm.h, mm_plan[(idx - 1) * 2 - 1])
                dtc = idx % NW
                cs = slice(dtc * WCH, (dtc + 1) * WCH)
                inst = eng.dma_start(out=w_sb[idx % 2][:].rearrange("p (blk c) -> p blk c", blk=44),
                                     in_=wdn_d.rearrange("(blk p) o -> p blk o", p=128)[:, :, cs])
                w_marks.append(pw.inc(inst, idx))
            return load_w_gp

        def sync_io(eng):
            def load(qc):
                eng.dma_start(out=h_sb[:].rearrange("p (blk c) -> p blk c", blk=44),
                              in_=hid.rearrange("(blk p) t -> p blk t", p=128)[:, :, qc * QCW:(qc + 1) * QCW]
                              ).then_inc(s_h.h, 16)
                s_h.n += 16

            def load_xr(gi):
                qc, dt = gi // NT, gi % NT
                pxr.inc(eng.dma_start(out=xr_sb[gi % 2][:],
                                      in_=resid_r[:, dt, qc * QCW:(qc + 1) * QCW]), gi)
            _waits(eng, hid_ready)
            load(0)
            if DOWN_LITE != 2:
                load_xr(0)
                load_xr(1)
            idx = 0
            for qc in range(NQ):
                for dt in range(NT):
                    if DOWN_LITE == 2:
                        eng.wait_ge(s_e.h, e_plan[idx])
                        inst = eng.dma_start(out=dst_r[:, dt, qc * QCW:(qc + 1) * QCW],
                                             in_=y_sb[:, dt * QCW:(dt + 1) * QCW])
                        po.inc(inst, idx)
                        idx += 1
                        continue
                    eng.wait_ge(s_f.h, f_plan[idx])
                    inst = eng.dma_start(out=dst_r[:, dt, qc * QCW:(qc + 1) * QCW], in_=ov[idx % 2][:])
                    po.inc(inst, idx)
                    if idx + 2 < NQ * NT:
                        load_xr(idx + 2)
                    idx += 1
                if qc + 1 < NQ:
                    load(qc + 1)

        def w_loader(eng):
            _waits(eng, w_ready)
            _waits(eng, hid_ready)
            load_w_gp = mk_load_w(eng)
            for qc in range(NQ):
                for dtc in range(NW):
                    load_w_gp(qc * NW + dtc)

        def pe_sec(eng):
            def ms_mm(qc, dt):
                eng.wait_ge(s_q.h, q_plan[qc * NT + dt])
                if dt == 0 and qc >= 1:
                    eng.wait_ge(s_r1.h, r1_done(qc - 1))
                inst = nc.tensor.matmul(ps_ms[0:1, :], cst["ones_col_bf"], sqv[(qc * NT + dt) % 2][:],
                                        start=(dt == 0), stop=(dt == NT - 1))
                s_ms.inc(inst, 1)
            idx = 0
            for qc in range(NQ):
                eng.wait_ge(s_h.h, h_base + 16 * (qc + 1))
                for dtc in range(NW):
                    pw.wait(eng, w_marks[qc * NW + dtc])
                    for di in range(WCH // 128):
                        dt = dtc * (WCH // 128) + di
                        if idx >= 2:
                            eng.wait_ge(s_e.h, e_plan[idx - 2])
                        p = ps_y[idx % 2]
                        for t in range(NKT):
                            blk, basep, rows = hrow(t)
                            inst = nc.tensor.matmul(
                                p[:],
                                w_sb[(qc * NW + dtc) % 2][basep:basep + rows,
                                                          blk * WCH + di * 128: blk * WCH + (di + 1) * 128],
                                h_sb[basep:basep + rows, blk * QCW:(blk + 1) * QCW],
                                start=(t == 0), stop=(t == NKT - 1))
                        s_mm.inc(inst, 1)
                        if DOWN_LITE == 0 and dt >= 2:
                            ms_mm(qc, dt - 2)
                        idx += 1
                if DOWN_LITE == 0:
                    ms_mm(qc, NT - 2)
                    ms_mm(qc, NT - 1)
                    eng.wait_ge(s_r2.h, r2_done(qc))
                    if qc >= 1:
                        eng.wait_ge(s_f.h, f_plan[qc * NT - 1])
                    inst = nc.tensor.matmul(ps_bc[:], cst["ones_row_f32"],
                                            rr_t[:, (qc % 2) * QCW:(qc % 2 + 1) * QCW], start=True, stop=True)
                    s_bc.inc(inst, 1)

        def act_sec(eng):
            if DOWN_LITE:
                return
            idx = 0
            for qc in range(NQ):
                for dt in range(NT):
                    eng.wait_ge(s_e.h, e_plan[idx])
                    if idx >= 2:
                        eng.wait_ge(s_ms.h, ms_plan[idx - 2])
                    inst = eng.activation(sqv[idx % 2][:], y_sb[:, dt * QCW:(dt + 1) * QCW], AF.Square)
                    s_q.inc(inst, 1)
                    idx += 1
                eng.wait_ge(s_r1.h, r1_done(qc))
                if qc >= 2:
                    eng.wait_ge(s_bc.h, bc_plan[qc - 2])
                qs = slice((qc % 2) * QCW, (qc % 2 + 1) * QCW)
                eng.activation(rr_t[:, qs], ms_t[:, qs], AF.Ln, bias=cst["eps_rms"])
                inst = eng.activation(rr_t[:, qs], rr_t[:, qs], AF.Exp, scale=-0.5)
                s_r2.inc(inst, 1)

        def dve_sec(eng):
            idx = 0
            for qc in range(NQ):
                for dt in range(NT):
                    eng.wait_ge(s_mm.h, mm_plan[idx])
                    if qc >= 1:
                        if DOWN_LITE == 2:
                            po.wait(eng, o_mark((qc - 1) * NT + dt))
                        else:
                            eng.wait_ge(s_f.h, f_plan[(qc - 1) * NT + dt])
                            eng.wait_ge(s_q.h, q_plan[(qc - 1) * NT + dt])
                    inst = eng.tensor_copy(y_sb[:, dt * QCW:(dt + 1) * QCW], ps_y[idx % 2][:])
                    s_e.inc(inst, 1)
                    idx += 1
                if DOWN_LITE == 0:
                    eng.wait_ge(s_ms.h, ms_plan[qc * NT + NT - 1])
                    if qc >= 2:
                        eng.wait_ge(s_r2.h, r2_done(qc - 2))
                    qs = slice((qc % 2) * QCW, (qc % 2 + 1) * QCW)
                    inst = eng.tensor_scalar(ms_t[:, qs], ps_ms[0:1, :], 1.0 / D, None, ALU.mult)
                    s_r1.inc(inst, 1)
                    eng.wait_ge(s_bc.h, bc_plan[qc])
                if DOWN_LITE == 2:
                    continue
                for dt in range(NT):
                    gi = qc * NT + dt
                    sl = slice(dt * QCW, (dt + 1) * QCW)
                    if gi >= 2:
                        po.wait(eng, o_mark(gi - 2))
                    pxr.wait(eng, xr_mark(gi))
                    if DOWN_LITE == 0:
                        eng.tensor_tensor(tv[gi % 2][:], y_sb[:, sl], ps_bc[:], ALU.mult)
                        inst = eng.tensor_tensor(ov[gi % 2][:], tv[gi % 2][:], xr_sb[gi % 2][:], ALU.add)
                    else:
                        inst = eng.tensor_tensor(ov[gi % 2][:], y_sb[:, sl], xr_sb[gi % 2][:], ALU.add)
                    s_f.inc(inst, 1)

        b.sec("gpsimd", w_loader)
        b.sec("sync", sync_io)
        b.sec("tensor", pe_sec)
        b.sec("scalar", act_sec)
        b.sec("vector", dve_sec)
    return po.gates()



def build_program(phases=PHASES, taps=DEBUG_TAPS):
    nc = bass.Bass("TRN2", target_bir_lowering=False)

    def dram_in(name, shape, dt):
        return nc.dram_tensor(name, shape, dt, kind="ExternalInput").ap()

    def dram_int(name, shape, dt, tap=None):
        kind = "ExternalOutput" if (tap is not None and tap in taps) else "Internal"
        return nc.dram_tensor(name, shape, dt, kind=kind).ap()

    x0 = dram_in("x0", [D, T_PRE], F32)
    wq_a = dram_in("wq_a", [D, D], F32)
    wk_a = dram_in("wk_a", [D, D], F32)
    wv_a = dram_in("wv_a", [D, D], F32)
    wo_a = dram_in("wo_a", [D, D], F32)
    bq_a = dram_in("bq_a", [128, NT], F32)
    bk_a = dram_in("bk_a", [128, NT], F32)
    bv_r = dram_in("bv_r", [1, D], F32)
    bo_a = dram_in("bo_a", [128, NT], F32)
    wq_b = dram_in("wq_b", [D, D], F32)
    wk_b = dram_in("wk_b", [D, KVH * DH], F32)
    wv_b = dram_in("wv_b", [D, KVH * DH], F32)
    wo_b = dram_in("wo_b", [D, D], F32)
    gupT = dram_in("gupT", [256, GU], I32)
    sc_gu = dram_in("sc_gu", [256, GU], F32)
    downT = dram_in("downT", [704, D], I32)
    sc_dn = dram_in("sc_dn", [704, D], F32)
    rope_cos = dram_in("rope_cos", [128, T_PRE], F32)
    rope_sin = dram_in("rope_sin", [128, T_PRE], F32)
    pswap_in = dram_in("pswap", [128, 128], F32)

    out_fm = nc.dram_tensor("out_fm", [D, T_SUF], F32, kind="ExternalOutput").ap()

    x1 = dram_int("x1", [D, T_PRE], F32, "x1")
    x2 = dram_int("x2", [D, T_PRE], F32, "x2")
    x3 = dram_int("x3", [D, T_SUF], F32, "x3")
    xln = dram_int("xln", [D, T_PRE], BF16, "xln1")
    wgu_d = dram_int("wgu_d", [D, GU], BF16, "wgu")
    wdn_d = dram_int("wdn_d", [I_FFN, D], BF16, "wdn")
    hid = dram_int("hid", [I_FFN, T_PRE], BF16, "hid")
    ka = dram_int("ka", [H, DH, T_PRE], BF16, "ka")
    va = dram_int("va", [H, T_PRE, DH], BF16, "va")
    qa = dram_int("qa", [H, DH, T_PRE], BF16, "qa")
    pfa = dram_int("pfa", [H, DH, T_PRE], BF16, "pfa")
    kb = dram_int("kb", [KVH, DH, T_PRE], BF16)
    vb = dram_int("vb", [KVH, T_PRE, DH], BF16)
    qb = dram_int("qb", [H, DH, T_SUF], BF16)
    pfb = dram_int("pfb", [H, DH, T_SUF], BF16)

    with ExitStack() as ctx:
        block = ctx.enter_context(nc.Block())
        b = B(nc, ctx, block)

        ones_col_bf = ctx.enter_context(nc.sbuf_tensor("ones_col", [128, 1], BF16))
        ones_col_f32 = ctx.enter_context(nc.sbuf_tensor("ones_colf", [128, 1], F32))
        ones_row_f32 = ctx.enter_context(nc.sbuf_tensor("ones_row", [1, 128], F32))
        ctpool = ctx.enter_context(nc.sbuf_tensor("ctpool", [128, 3], F32))
        pswap_sb = ctx.enter_context(nc.sbuf_tensor("pswap_sb", [128, 128], BF16))
        s_initc = b.sem("initc")
        s_initd = b.sem("initd")

        def init_consts(eng):
            eng.memset(ones_col_bf[:], 1.0)
            eng.memset(ones_col_f32[:], 1.0)
            eng.memset(ctpool[:, 0:1], LN_EPS)
            eng.memset(ctpool[:, 1:2], RMS_EPS)
            eng.memset(ctpool[:, 2:3], -EXP_SHIFT_B)
            inst = eng.memset(ones_row_f32[:], 1.0)
            s_initc.inc(inst, 1)

        def init_loads(eng):
            eng.dma_start(out=pswap_sb[:], in_=pswap_in).then_inc(s_initd.h, 16)
            s_initd.n += 16

        b.sec("vector", init_consts)
        b.sec("gpsimd", init_loads)

        def await_init(eng):
            eng.wait_ge(s_initc.h, 1)
            eng.wait_ge(s_initd.h, 16)
        b.sec("tensor", await_init)
        b.sec("scalar", await_init)
        b.sec("vector", await_init)
        b.sec("sync", await_init)

        cst = dict(ones_col_bf=ones_col_bf[:], ones_col_f32=ones_col_f32[:],
                   ones_row_f32=ones_row_f32[:],
                   eps_ln=ctpool[0:1, 0:1], eps_rms=ctpool[0:1, 1:2],
                   nshift=ctpool[:, 2:3])

        deq_done = None
        last = None
        if "p1" in phases or "p1ln" in phases:
            ln1 = emit_ln(b, "l1", x0, T_PRE, xln, cst, None)
            last = ln1
        if "deq" in phases:
            deq_done = emit_dequant(b, gupT, sc_gu, downT, sc_dn, wgu_d, wdn_d)
        if "p1" in phases or "p1kqv" in phases:
            ka_done = emit_fm_proj(b, "ka", wk_a, D, bk_a, xln, T_PRE,
                                   [ka[h] for h in range(H)], ln1, cst=cst)
            qa_done = emit_fm_proj(b, "qa", wq_a, D, bq_a, xln, T_PRE,
                                   [qa[h] for h in range(H)], ka_done, cst=cst)
            va_done = emit_v_proj(b, "va", wv_a, D, bv_r, xln, T_PRE,
                                  [va[h] for h in range(H)], qa_done, cst)
            last = va_done
        if "p1" in phases or "p1att" in phases:
            ata_done = emit_attention(b, "at_a", [qa[h] for h in range(H)],
                                      [ka[h] for h in range(H)], [va[h] for h in range(H)],
                                      lambda h: h, T_PRE, T_PRE, 1.0 / DH, 0.0,
                                      [pfa[h] for h in range(H)], H,
                                      ka_done + qa_done + va_done, cst)
            p1_done = emit_out_proj(b, "wo_a", wo_a, bo_a, pfa, H, x0, T_PRE, x1,
                                    ata_done, cst)
            last = p1_done

        if "p2" in phases or "p2gu" in phases or "p2dn" in phases:
            ln2 = emit_ln(b, "l2", x1, T_PRE, xln, cst, last)
            last = ln2
        if "p2" in phases or "p2gu" in phases or "p2dn" in phases:
            gu2 = emit_gate_up(b, "gu2", wgu_d, xln, T_PRE, hid, ln2, deq_done)
            last = gu2
        if "p2" in phases or "p2dn" in phases:
            p2_done = emit_down(b, "dn2", wdn_d, hid, T_PRE, x1, x2, gu2, deq_done, cst)
            last = p2_done

        if "p3" in phases:
            ln3 = emit_ln(b, "l3", x2, T_PRE, xln, cst, last)
            with ExitStack() as p3ctx:
                cos_sb = p3ctx.enter_context(nc.sbuf_tensor("cos_sb", [128, T_PRE], F32))
                sin_sb = p3ctx.enter_context(nc.sbuf_tensor("sin_sb", [128, T_PRE], F32))
                s_rt = b.sem("ropetab")
                rt_gate_val = ln3

                def rope_loads(eng):
                    _waits(eng, rt_gate_val)
                    eng.dma_start(out=cos_sb[:], in_=rope_cos).then_inc(s_rt.h, 16)
                    s_rt.n += 16
                    eng.dma_start(out=sin_sb[:], in_=rope_sin).then_inc(s_rt.h, 16)
                    s_rt.n += 16
                b.sec("gpsimd", rope_loads)
                rope = dict(pswap=pswap_sb[:], cos=cos_sb[:], sin=sin_sb[:])
                kb_done = emit_fm_proj(b, "kb", wk_b, KVH * DH, None, xln, T_PRE,
                                       [kb[h] for h in range(KVH)], ln3 + [(s_rt, 32)],
                                       rope=rope, cst=cst)
                vb_done = emit_v_proj(b, "vb", wv_b, KVH * DH, None, xln, T_PRE,
                                      [vb[h] for h in range(KVH)], kb_done, cst)
                qb_done = emit_fm_proj(b, "qb", wq_b, D, None, xln, T_SUF,
                                       [qb[h] for h in range(H)], vb_done, rope=rope, cst=cst)
                atb_done = emit_attention(b, "at_b", [qb[h] for h in range(H)],
                                          [kb[h] for h in range(KVH)], [vb[h] for h in range(KVH)],
                                          lambda h: h % 4, T_SUF, T_PRE, 1.0 / float(np.sqrt(DH)),
                                          EXP_SHIFT_B, [pfb[h] for h in range(H)], H,
                                          kb_done + vb_done + qb_done, cst)
            p3_done = emit_out_proj(b, "wo_b", wo_b, None, pfb, H, x2[:, 0:T_SUF], T_SUF,
                                    x3, atb_done, cst)
            last = p3_done

        if "p4" in phases:
            xln_s = xln[:, 0:T_SUF]
            hid_s = hid[:, 0:T_SUF]
            ln4 = emit_ln(b, "l4", x3, T_SUF, xln_s, cst, last)
            gu4 = emit_gate_up(b, "gu4", wgu_d, xln_s, T_SUF, hid_s, ln4, deq_done)
            p4_done = emit_down(b, "dn4", wdn_d, hid_s, T_SUF, x3, out_fm, gu4,
                                deq_done, cst)
            last = p4_done

        fin = last if last is not None else deq_done
        fin_deq = deq_done

        def final_wait(eng):
            _waits(eng, fin)
            _waits(eng, fin_deq)

        b.sec("sync", final_wait)
    return nc



def _bias2d(v):
    return np.ascontiguousarray(np.asarray(v, np.float32).reshape(NT, 128).T)


def _prep_inputs(inp):
    f32 = np.float32
    x = np.asarray(inp["L0_attn_a_x"])
    hm = HMAP
    im_full = np.concatenate([IMAP, I_FFN + IMAP])

    shared = dict(
        wq_a=np.ascontiguousarray(np.asarray(inp["L0_attn_a_Wq"], f32)[hm, :]),
        wk_a=np.ascontiguousarray(np.asarray(inp["L0_attn_a_Wk"], f32)[hm, :]),
        wv_a=np.ascontiguousarray(np.asarray(inp["L0_attn_a_Wv"], f32)[hm, :]),
        wo_a=np.ascontiguousarray(np.asarray(inp["L0_attn_a_Wo"], f32)[:, hm]),
        bq_a=_bias2d(inp["L0_attn_a_bq"]),
        bk_a=_bias2d(inp["L0_attn_a_bk"]),
        bv_r=np.asarray(inp["L0_attn_a_bv"], f32).reshape(1, D),
        bo_a=_bias2d(np.asarray(inp["L0_attn_a_bo"], f32)[hm]),
        wq_b=np.ascontiguousarray(np.asarray(inp["L0_attn_b_W_q"], f32)[hm, :]),
        wk_b=np.ascontiguousarray(np.asarray(inp["L0_attn_b_W_k"], f32)[hm, :]),
        wv_b=np.ascontiguousarray(np.asarray(inp["L0_attn_b_W_v"], f32)[hm, :]),
        wo_b=np.ascontiguousarray(np.asarray(inp["L0_attn_b_W_o"], f32)[:, hm]),
    )
    gup = np.asarray(inp["L0_ffn_gate_up_packed"])
    gup_p = np.concatenate([gup[:I_FFN][IMAP], gup[I_FFN:][IMAP]])
    shared["gupT"] = np.ascontiguousarray(gup_p.T)
    shared["sc_gu"] = np.ascontiguousarray(
        np.repeat(np.asarray(inp["L0_ffn_gate_up_scales"]).astype(f32), 16, axis=0)[:, im_full])
    dnp = np.asarray(inp["L0_ffn_down_packed"])[hm, :]
    shared["downT"] = np.ascontiguousarray(dnp.T)
    shared["sc_dn"] = np.ascontiguousarray(
        np.repeat(np.asarray(inp["L0_ffn_down_scales"]).astype(f32), 16, axis=0)[:, hm])
    pswap = np.zeros((128, 128), f32)
    pswap[np.arange(128), np.arange(128) ^ 1] = 1.0
    shared["pswap"] = pswap

    in_maps = []
    for c in range(8):
        bidx, half = c // 2, c % 2
        xb = np.asarray(x[bidx])[:, hm]
        if half == 1:
            xb = np.roll(xb, -T_SUF, axis=0)
        m = dict(shared)
        m["x0"] = np.ascontiguousarray(xb.T.astype(f32))
        pos = (np.arange(T_PRE) + half * T_SUF) % T_PRE
        cidx = np.arange(64, dtype=np.float64)
        freqs = 1.0 / (10000.0 ** (cidx / 64.0))
        ang = pos[None, :] * freqs[:, None]
        cos_t = np.repeat(np.cos(ang), 2, axis=0).astype(f32)
        sin_t = np.repeat(np.sin(ang), 2, axis=0).astype(f32)
        sin_t[0::2, :] *= -1.0
        m["rope_cos"] = np.ascontiguousarray(cos_t)
        m["rope_sin"] = np.ascontiguousarray(sin_t)
        in_maps.append(m)
    return in_maps


def kernel(**inputs):
    in_maps = _prep_inputs(inputs)
    nc = build_program()
    res = run_bass_kernel_spmd(nc, in_maps, core_ids=list(range(8)))
    out = np.empty((4, T_PRE, D), np.float32)
    for c in range(8):
        bidx, half = c // 2, c % 2
        of = np.asarray(res.results[c]["out_fm"])
        sl = slice(half * T_SUF, (half + 1) * T_SUF)
        out[bidx, sl, :][:, HMAP] = of.T
    return out
